# version 01
# baseline (speedup 1.0000x reference)
import os
import numpy as np
from contextlib import ExitStack
import concourse.bass as bass
import concourse.mybir as mybir
from concourse.bass_utils import run_bass_kernel_spmd

F32 = mybir.dt.float32
BF16 = mybir.dt.bfloat16
AF = mybir.ActivationFunctionType
ALU = mybir.AluOpType
AX = mybir.AxisListType

NB = 2
NT = 2048
DM = 1024
NCH = 8
NE = 8
HID = 4096

ST_RM = 0
ST_CS = 32
ST_NAM = 64
ST_RN = 96
ST_MS1 = 112
ST_NMX = 128
ST_RS = 144
ST_RR = 160
ST_B1 = 176
ST_GN = 208
ST_GS = 216
ST_MS2 = 224


class P:
    ENGS = ("g", "t", "v", "s", "y")

    def __init__(self, nc, ctx):
        self.nc = nc
        self.ops = {e: [] for e in self.ENGS}
        self.cnt = {}
        self.sems = {}
        for name in ("gld", "yld", "gc", "tc", "vc", "sc"):
            self.sems[name] = ctx.enter_context(nc.semaphore(name))
            self.cnt[name] = 0

    def do(self, eng, fn, wait=(), inc=None, inc_amt=1):
        wait = tuple((s, v) for (s, v) in wait if v > 0)
        for s, v in wait:
            assert v <= self.cnt[s], f"wait {s}>={v} but planned only {self.cnt[s]}"
        if inc is not None:
            self.cnt[inc] += inc_amt
        sems = self.sems

        def emit(e, fn=fn, wait=wait, inc=inc, inc_amt=inc_amt):
            for s, v in wait:
                e.wait_ge(sems[s], v)
            inst = fn(e)
            if inc is not None:
                assert inst is not None
                inst.then_inc(sems[inc], inc_amt)
        self.ops[eng].append(emit)
        return self.cnt[inc] if inc is not None else None

    def now(self, name):
        return self.cnt[name]

    def snap(self):
        return dict(self.cnt)

    def build(self, block):
        ops = self.ops

        @block.gpsimd
        def _(g):
            for f in ops["g"]:
                f(g)

        @block.tensor
        def _(t):
            for f in ops["t"]:
                f(t)

        @block.vector
        def _(v):
            for f in ops["v"]:
                f(v)

        @block.scalar
        def _(s):
            for f in ops["s"]:
                f(s)

        @block.sync
        def _(y):
            for f in ops["y"]:
                f(y)


class Kern:
    def __init__(self, nc, ctx, parts):
        self.nc = nc
        self.ctx = ctx
        self.parts = list(parts)
        self.p = P(nc, ctx)
        self.xfm_ready = ("vc", 0)

    def declare_io(self):
        nc = self.nc
        def ap(name, shape, dt, kind):
            return nc.dram_tensor(name, shape, dt, kind=kind).ap()
        I, O, N = "ExternalInput", "ExternalOutput", "Internal"
        self.x_in = ap("x", [NB, NT, DM], F32, I)
        self.y_out = ap("y", [NB, NT, DM], F32, O)
        pset = {"l1ffn" if q.startswith("l1ffn") else q for q in self.parts}
        if "l0ffn" in pset:
            self.w1 = ap("w1", [NE, DM, HID], F32, I)
            self.w2 = ap("w2", [NE, HID, DM], F32, I)
            self.se = ap("se", [NE * 512, DM], F32, I)
            self.gnc = ap("gnc", [128, NCH], F32, I)
            self.gsc = ap("gsc", [128, NCH], F32, I)
            self.b1c = ap("b1c", [NE, 128, 32], F32, I)
            self.b2r = ap("b2r", [NE, 1, DM], F32, I)
            self.Lsp = ap("Lsp", [NB, 32, 128, NT], F32, N)
            self.eod = ap("eod", [NB, NE * 512, DM], BF16, N)
        if "l0attn" in pset:
            self.u_n = ap("u_n", [DM, DM], F32, I)
            self.uT = ap("uT", [DM, DM], F32, I)
            self.v_n = ap("v_n", [DM, DM], F32, I)
            self.vT = ap("vT", [DM, DM], F32, I)
        if "l1attn" in pset:
            self.trilT = ap("trilT", [16, 64, 64], F32, I)
            self.WvT = ap("WvT", [DM, DM], F32, I)
            self.bvr = ap("bvr", [1, DM], F32, I)
            self.WoT = ap("WoT", [DM, DM], F32, I)
            self.bor = ap("bor", [1, DM], F32, I)
        if "l1ffn" in pset:
            self.G1r = ap("G1r", [4, 2560], F32, I)
            self.brr = ap("brr", [1, 512], F32, I)
            self.bir = ap("bir", [1, 512], F32, I)
            self.pdr = ap("pdr", [1024, 256], F32, N)
            self.wts = ap("wts", [128, 6144], BF16, N)
        self.bncb = ap("bncb", [1, 2048], BF16, N)

    def alloc(self):
        nc, E = self.nc, self.ctx.enter_context
        self.XFM = E(nc.sbuf_tensor("XFM", [128, NB * NCH * NT], BF16))
        self.A = E(nc.sbuf_tensor("Abuf", [128, 32768], BF16))
        self.Bar = E(nc.sbuf_tensor("Bar", [128, 16384], BF16))
        self.DT = E(nc.sbuf_tensor("DT", [128, 6144], BF16))
        self.WB = E(nc.sbuf_tensor("WB", [128, 8192], BF16))
        self.SL = E(nc.sbuf_tensor("SL", [128, 2048], BF16))
        self.FSA = E(nc.sbuf_tensor("FSA", [128, 2048], F32))
        self.PMRN = E(nc.sbuf_tensor("PMRN", [128, 2048], BF16))
        self.ST = E(nc.sbuf_tensor("STT", [128, 256], F32))
        self.ident = E(nc.sbuf_tensor("ident", [128, 128], BF16))
        self.ones = E(nc.sbuf_tensor("onesb", [128, 512], BF16))
        self.PL = E(nc.psum_tensor("PL", [128, 2048], F32))
        self.PA = E(nc.psum_tensor("PA", [128, 512], F32))
        self.PB = E(nc.psum_tensor("PB", [128, 512], F32))
        self.PC = E(nc.psum_tensor("PC", [128, 512], F32))
        self.PD = E(nc.psum_tensor("PD", [128, 512], F32))

    def xfm(self, b, c):
        return self.XFM[:, (b * NCH + c) * NT:(b * NCH + c) * NT + NT]

    def emit_init(self):
        p = self.p
        p.do("g", lambda g: g.memset(self.ones[:, :], 1.0), inc="gc")
        p.do("g", lambda g: g.memset(self.ident[:, :], 0.0), inc="gc")
        v = p.now("gc")
        p.do("g", lambda g: g.affine_select(
            out=self.ident[:, :], in_=self.ident[:, :],
            compare_op=ALU.not_equal, fill=1.0, base=0,
            pattern=[[-1, 128]], channel_multiplier=1), wait=(("gc", v),), inc="gc")
        p.do("g", lambda g: g.memset(self.ST[:, ST_MS2 + 4:ST_MS2 + 5], 1e-5),
             inc="gc")
        if "l0ffn" in self.parts:
            p.do("y", lambda y: y.dma_start(self.ST[:, ST_GN:ST_GN + 8], self.gnc[:, :]),
                 inc="yld", inc_amt=16)
            p.do("y", lambda y: y.dma_start(self.ST[:, ST_GS:ST_GS + 8], self.gsc[:, :]),
                 inc="yld", inc_amt=16)
        self.init_gc = p.now("gc")
        self.init_yld = p.now("yld")

    def emit_load_x(self):
        p = self.p
        xin = self.SL[:, 0:1024]
        pbf = self.PL[:, 0:512].bitcast(BF16)
        xfm3 = self.XFM[:, :].rearrange("p (k n) -> p k n", n=NT)
        for b in range(NB):
            for k in range(16):
                ld = p.do("g", lambda g, b=b, k=k: g.dma_start(
                    xin, self.x_in[b, k * 128:(k + 1) * 128, :]),
                    wait=(("tc", p.now("tc")),), inc="gld", inc_amt=16)
                tw = [("gld", ld), ("gc", self.init_gc), ("vc", p.now("vc"))]
                tcv = None
                for c in range(NCH):
                    tcv = p.do("t", lambda t, c=c: t.transpose(
                        pbf[:, c * 128:(c + 1) * 128], xin[:, c * 128:(c + 1) * 128],
                        self.ident[:, :]),
                        wait=tw if c == 0 else (), inc="tc" if c == NCH - 1 else None)
                dst = xfm3[:, b * NCH:(b + 1) * NCH, k * 128:k * 128 + 128]
                src = pbf[:, :].rearrange("p (c j) -> p c j", j=128)
                p.do("v", lambda v, dst=dst, src=src: v.tensor_copy(dst, src),
                     wait=(("tc", tcv),), inc="vc")
        self.xfm_ready = ("vc", p.now("vc"))

    def emit_store_y(self):
        p = self.p
        xout = self.FSA[:, 0:1024]
        pbf = self.PL[:, 0:512].bitcast(BF16)
        xfm3 = self.XFM[:, :].rearrange("p (k n) -> p k n", n=NT)
        rs, rv = self.xfm_ready
        for b in range(NB):
            for k in range(16):
                src3 = xfm3[:, b * NCH:(b + 1) * NCH, k * 128:k * 128 + 128]
                tcv = None
                for c in range(NCH):
                    tw = [(rs, rv), ("vc", p.now("vc")), ("yld", p.now("yld"))] if c == 0 else []
                    tcv = p.do("t", lambda t, c=c, src3=src3: t.transpose(
                        pbf[:, c * 128:(c + 1) * 128], src3[:, c, :], self.ident[:, :]),
                        wait=tw, inc="tc" if c == NCH - 1 else None)
                p.do("v", lambda v: v.tensor_copy(xout, pbf[:, :]),
                     wait=(("tc", tcv),), inc="vc")
                p.do("y", lambda y, b=b, k=k: y.dma_start(
                    self.y_out[b, k * 128:(k + 1) * 128, :], xout),
                    wait=(("vc", p.now("vc")),), inc="yld", inc_amt=16)
        p.do("y", lambda y: None, wait=(("yld", p.now("yld")),))

    def emit_ln(self, b, dst, rms=False):
        p = self.p
        sq = [self.SL[:, 0:512], self.SL[:, 512:1024]]
        rs, rv = self.xfm_ready
        r0 = lambda q: self.FSA[0:1, 0:512]
        r1 = lambda q: self.FSA[0:1, 512:1024]
        r2 = lambda q: self.FSA[0:1, 1024:1536]
        r3 = lambda q: self.FSA[0:1, 1536:2048]
        TMPB = self.A[:, 0:2048]
        SRNB = self.A[:, 2048:4096]
        MSRB = self.A[:, 4096:6144]

        for q in range(4):
            n0 = q * 512
            wA = [(rs, rv), ("vc", p.now("vc")), ("sc", p.now("sc")),
                  ("yld", p.now("yld"))]
            sqv = []
            for c in range(2):
                sqv.append(p.do("s", lambda s, c=c, n0=n0: s.activation(
                    sq[c % 2], self.xfm(b, c)[:, n0:n0 + 512], AF.Square),
                    wait=[(rs, rv), ("tc", p.now("tc")), ("yld", p.now("yld")),
                          ("gld", p.now("gld"))] if c == 0 else (), inc="sc"))
            tsum = None
            for c in range(NCH):
                tsum = p.do("t", lambda t, c=c, n0=n0: t.matmul(
                    self.PA[0:1, :], self.ones[:, 0:1], self.xfm(b, c)[:, n0:n0 + 512],
                    start=(c == 0), stop=(c == NCH - 1)),
                    wait=wA if c == 0 else (), inc="tc" if c == NCH - 1 else None)
            tssq = None
            for c in range(NCH):
                tssq = p.do("t", lambda t, c=c: t.matmul(
                    self.PB[0:1, :], self.ones[:, 0:1], sq[c % 2],
                    start=(c == 0), stop=(c == NCH - 1)),
                    wait=[("sc", sqv[c]), ("vc", p.now("vc"))] if c == 0 else
                         [("sc", sqv[c])], inc="tc")
                if c + 2 < NCH:
                    sqv.append(p.do("s", lambda s, c=c, n0=n0: s.activation(
                        sq[c % 2], self.xfm(b, c + 2)[:, n0:n0 + 512], AF.Square),
                        wait=(("tc", tssq),), inc="sc"))
            p.do("v", lambda v, q=q: v.tensor_copy(r0(q), self.PA[0:1, :]),
                 wait=(("tc", p.now("tc")), ("sc", p.now("sc"))), inc="vc")
            p.do("v", lambda v, q=q: v.tensor_copy(r1(q), self.PB[0:1, :]), inc="vc")
            p.do("v", lambda v, q=q: v.tensor_scalar_mul(r2(q), r0(q), 1.0 / DM),
                 inc="vc")
            p.do("v", lambda v, q=q: v.tensor_scalar_mul(r1(q), r1(q), 1.0 / DM),
                 inc="vc")
            p.do("v", lambda v, q=q: v.tensor_mul(r3(q), r2(q), r2(q)), inc="vc")
            p.do("v", lambda v, q=q: v.tensor_sub(r1(q), r1(q), r3(q)), inc="vc")
            vv = p.now("vc")
            sv = p.do("s", lambda s, q=q: s.activation(
                r3(q), r1(q), AF.Sqrt, bias=self.ST[0:1, ST_MS2 + 4:ST_MS2 + 5]),
                wait=(("vc", vv), ("gc", self.init_gc)), inc="sc")
            p.do("v", lambda v, q=q: v.reciprocal(r0(q), r3(q)),
                 wait=(("sc", sv),), inc="vc")
            if rms:
                sv = p.do("s", lambda s, q=q: s.activation(r3(q), r1(q), AF.Sqrt),
                          wait=(("vc", p.now("vc")),), inc="sc")
                p.do("v", lambda v, q=q: v.tensor_mul(r3(q), r3(q), r0(q)),
                     wait=(("sc", sv),), inc="vc")
                p.do("v", lambda v, q=q: v.tensor_scalar(r3(q), r3(q), 32.0, 1e-12,
                                                         ALU.mult, ALU.max), inc="vc")
                p.do("v", lambda v, q=q: v.reciprocal(r1(q), r3(q)), inc="vc")
                p.do("v", lambda v, q=q: v.tensor_mul(r0(q), r0(q), r1(q)), inc="vc")
                p.do("v", lambda v, q=q: v.tensor_scalar_mul(r0(q), r0(q), 32.0),
                     inc="vc")
            p.do("v", lambda v, q=q: v.tensor_mul(r2(q), r2(q), r0(q)), inc="vc")
            p.do("v", lambda v, q=q: v.tensor_copy(self.SL[0:1, 1024:1536], r0(q)),
                 inc="vc")
            p.do("v", lambda v, q=q: v.tensor_copy(self.SL[0:1, 1536:2048], r2(q)),
                 inc="vc")
            vrow = p.now("vc")
            p.do("t", lambda t, q=q: t.matmul(
                self.PA[:, :], self.ones[0:1, 0:128], self.SL[0:1, 1024:1536],
                start=True, stop=True),
                wait=[("vc", vrow)], inc="tc")
            tb = p.do("t", lambda t, q=q: t.matmul(
                self.PB[:, :], self.ones[0:1, 0:128], self.SL[0:1, 1536:2048],
                start=True, stop=True),
                inc="tc")
            p.do("v", lambda v, q=q: v.tensor_copy(SRNB[:, q * 512:(q + 1) * 512],
                                                   self.PA[:, :]),
                 wait=[("tc", tb)] + ([("tc", p.now("tc")), ("sc", p.now("sc"))]
                                      if q == 0 else []), inc="vc")
            p.do("v", lambda v, q=q: v.tensor_copy(MSRB[:, q * 512:(q + 1) * 512],
                                                   self.PB[:, :]), inc="vc")
        vbc = p.now("vc")
        for c in range(NCH):
            p.do("v", lambda v, c=c: v.tensor_mul(TMPB, self.xfm(b, c), SRNB),
                 wait=[("vc", vbc), ("yld", self.init_yld),
                       ("tc", p.now("tc")), ("sc", p.now("sc"))] if c == 0 else (),
                 inc="vc")
            p.do("v", lambda v, c=c: v.tensor_sub(dst(c), TMPB, MSRB), inc="vc")
            if rms:
                p.do("v", lambda v, c=c: v.tensor_scalar_mul(
                    dst(c), dst(c), self.ST[:, ST_GN + c:ST_GN + c + 1]), inc="vc")
        return p.now("vc")

    def emit_l1ffn(self, first):
        p = self.p
        WRT = self.DT[:, 0:2048]
        WIT = self.DT[:, 2048:4096]
        NIT = self.DT[:, 4096:6144]
        if first:
            g1 = self.WB[0:4, 0:2560]
            l1 = p.do("g", lambda g: g.dma_start(g1, self.G1r[:, :]),
                      wait=(("vc", p.now("vc")), ("sc", p.now("sc"))),
                      inc="gld", inc_amt=16)
            fs = self.FSA[:, 0:512]
            wn = self.WB[:, 2560:4608]
            pbf = self.PC[:, :].bitcast(BF16)
            gq_prev = p.now("gld")
            for ri in range(2):
                for ms in range(8):
                    tv = p.do("t", lambda t, ri=ri, ms=ms: t.matmul(
                        self.PA[:, 0:256],
                        g1[:, ri * 1024 + ms * 128:ri * 1024 + (ms + 1) * 128],
                        g1[:, 2048 + ri * 256:2048 + ri * 256 + 256],
                        start=True, stop=True),
                        wait=[("gld", max(l1, gq_prev)), ("vc", p.now("vc")),
                              ("yld", p.now("yld"))], inc="tc")
                    vv = p.do("v", lambda v: v.tensor_copy(fs[:, 0:256], self.PA[:, 0:256]),
                              wait=(("tc", tv), ("yld", p.now("yld"))), inc="vc")
                    p.do("y", lambda y, ms=ms: y.dma_start(
                        self.pdr[ms * 128:(ms + 1) * 128, :], fs[:, 0:256]),
                        wait=(("vc", vv),), inc="yld", inc_amt=16)
                yv = p.now("yld")
                lw = None
                for b32 in range(32):
                    rc = b32 // 8
                    dst = wn[(b32 % 8) * 16:(b32 % 8) * 16 + 16,
                             rc * 512:(rc + 1) * 512].rearrange("p (d g) -> p d g", g=16)
                    src2 = self.pdr[b32 * 32:(b32 + 1) * 32, :].rearrange(
                        "d (e g) -> e d g", g=16)
                    lw = p.do("g", lambda g, dst=dst, src2=src2: g.dma_start(dst, src2),
                              wait=[("yld", yv), ("tc", p.now("tc"))] if b32 == 0 else (),
                              inc="gld", inc_amt=16)
                gq_prev = lw
                dstT = WRT if ri == 0 else WIT
                for grp in range(2):
                    tv = None
                    for j in range(8):
                        kk = grp * 8 + j
                        rc, jc = kk // 4, kk % 4
                        tv = p.do("t", lambda t, rc=rc, jc=jc, j=j: t.transpose(
                            pbf[:, j * 128:(j + 1) * 128],
                            wn[:, rc * 512 + jc * 128:rc * 512 + (jc + 1) * 128],
                            self.ident[:, :]),
                            wait=[("gld", lw), ("vc", p.now("vc"))] if j == 0 else (),
                            inc="tc" if j == 7 else None)
                    for j in range(8):
                        kk = grp * 8 + j
                        rc, jc = kk // 4, kk % 4
                        p.do("v", lambda v, rc=rc, jc=jc, j=j, dstT=dstT: v.tensor_copy(
                            dstT[:, jc * 512 + rc * 128:jc * 512 + (rc + 1) * 128],
                            pbf[:, j * 128:(j + 1) * 128]),
                            wait=[("tc", tv)] if j == 0 else (), inc="vc")
            p.do("v", lambda v: v.tensor_scalar_mul(NIT, WIT, -1.0), inc="vc")
            wv = p.now("vc")
            p.do("y", lambda y: y.dma_start(self.wts[:, :], self.DT[:, :]),
                 wait=(("vc", wv),), inc="yld", inc_amt=16)
            self.l1ffn_w_done = p.now("yld")
            wgate = ("vc", wv)
        else:
            lw = p.do("g", lambda g: g.dma_start(self.DT[:, :], self.wts[:, :]),
                      wait=(("vc", p.now("vc")), ("sc", p.now("sc")),
                            ("yld", self.l1ffn_w_done)),
                      inc="gld", inc_amt=16)
            wgate = ("gld", lw)

        lbr = p.do("g", lambda g: g.dma_start(self.PMRN[0:1, 0:512], self.brr[:, :]),
                   wait=(("tc", p.now("tc")), ("sc", p.now("sc")),
                         ("vc", p.now("vc"))), inc="gld", inc_amt=16)
        lbr = p.do("g", lambda g: g.dma_start(self.PMRN[0:1, 512:1024], self.bir[:, :]),
                   inc="gld", inc_amt=16)
        for b in range(NB):
            lnv = self.emit_ln(b, lambda c: self.Bar[:, c * 2048:(c + 1) * 2048])
            pp = [self.PA, self.PB]
            gate = {0: ("vc", 0), 1: ("vc", 0)}
            for oc in range(NCH):
                real = oc < 4
                om = oc % 4
                for nb4 in range(4):
                    pi = (oc * 4 + nb4) % 2
                    ps = pp[pi]
                    bias_row = (self.PMRN[0:1, 0:512] if real
                                else self.PMRN[0:1, 512:1024])
                    wa, wb_ = (WRT, NIT) if real else (WIT, WRT)
                    for k in range(4):
                        p.do("t", lambda t, k=k, om=om, nb4=nb4, wa=wa, ps=ps: t.matmul(
                            ps[:, :], wa[:, k * 512 + om * 128:k * 512 + (om + 1) * 128],
                            self.Bar[:, k * 2048 + nb4 * 512:k * 2048 + nb4 * 512 + 512],
                            start=(k == 0), stop=False),
                            wait=[("vc", lnv), wgate, gate[pi]] if k == 0 else ())
                    for k in range(4):
                        p.do("t", lambda t, k=k, om=om, nb4=nb4, wb_=wb_, ps=ps: t.matmul(
                            ps[:, :], wb_[:, k * 512 + om * 128:k * 512 + (om + 1) * 128],
                            self.Bar[:, (4 + k) * 2048 + nb4 * 512:(4 + k) * 2048 + nb4 * 512 + 512],
                            start=False, stop=False))
                    tv = p.do("t", lambda t, om=om, ps=ps, bias_row=bias_row: t.matmul(
                        ps[:, :], bias_row[:, om * 128:(om + 1) * 128],
                        self.ones[0:1, 0:512], start=False, stop=True),
                        wait=(("gld", lbr),), inc="tc")
                    xs = self.xfm(b, oc)[:, nb4 * 512:(nb4 + 1) * 512]
                    ev = p.do("v", lambda v, xs=xs, ps=ps: v.scalar_tensor_tensor(
                        xs, ps[:, :], 0.5, xs, ALU.mult, ALU.add),
                        wait=(("tc", tv),), inc="vc")
                    gate[pi] = ("vc", ev)
            self.xfm_ready = ("vc", p.now("vc"))

    def emit_l0attn(self):
        p = self.p
        for b in range(NB):
            lnv = self.emit_ln(b, lambda c: self.Bar[:, c * 2048:(c + 1) * 2048])
            pp = [self.PA, self.PB]
            gate = {0: ("vc", 0), 1: ("sc", 0)}
            slot_gate = [("tc", 0)] * 4
            pcount = 0
            for m in range(8):
                lds = []
                for ti, wsrc in enumerate((self.uT, self.vT)):
                    si = ti * 2 + (m % 2)
                    slot = self.DT[:, si * 1024:(si + 1) * 1024]
                    src = wsrc[:, m * 128:(m + 1) * 128].rearrange(
                        "(k q) c -> q k c", q=128)
                    dst3 = slot.rearrange("q (k c) -> q k c", c=128)
                    ld = p.do("g", lambda g, dst3=dst3, src=src: g.dma_start(dst3, src),
                              wait=[slot_gate[si], ("vc", lnv)], inc="gld", inc_amt=16)
                    lds.append(ld)
                for ti in range(2):
                    si = ti * 2 + (m % 2)
                    slot = self.DT[:, si * 1024:(si + 1) * 1024]
                    for nb4 in range(4):
                        pi = pcount % 2
                        pcount += 1
                        ps = pp[pi]
                        tv = None
                        for k in range(NCH):
                            tv = p.do("t", lambda t, k=k, m=m, nb4=nb4, slot=slot, ps=ps: t.matmul(
                                ps[:, :], slot[:, k * 128:(k + 1) * 128],
                                self.Bar[:, k * 2048 + nb4 * 512:k * 2048 + nb4 * 512 + 512],
                                start=(k == 0), stop=(k == NCH - 1)),
                                wait=[("gld", lds[ti]), ("vc", lnv), gate[pi]] if k == 0 else (),
                                inc="tc" if k == NCH - 1 else None)
                        dst = self.A[:, ti * 16384 + m * 2048 + nb4 * 512:
                                     ti * 16384 + m * 2048 + nb4 * 512 + 512]
                        ev = p.do("s", lambda s, dst=dst, ps=ps: s.mul(dst, ps[:, :], 0.03125),
                                  wait=(("tc", tv),), inc="sc")
                        gate[pi] = ("sc", ev)
                    slot_gate[si] = ("tc", p.now("tc"))
            SKIP = os.environ.get("L0SKIP", "")
            pbf = [self.PC[:, :].bitcast(BF16), self.PD[:, :].bitcast(BF16)]
            vbank = [self.PA, self.PB, self.PC, self.PD]
            qh_row = self.Bar[:, 4096:6144]
            qhT_row = self.Bar[:, 6144:8192]
            bounce = self.Bar[:, 8192:10240]
            zrow = self.SL[0:1, 0:128]
            zv = p.do("g", lambda g: g.memset(zrow, 0.0),
                      wait=(("sc", p.now("sc")), ("tc", p.now("tc"))), inc="gc")
            prev_pair_done = ("vc", p.now("vc"))
            prev_bounce = ("gld", p.now("gld"))
            for pr in range(0 if "heads" not in SKIP else 8, 8):
                tmv = []
                for ti in range(2 if "notm" not in SKIP else 0):
                    for grp in range(2):
                        tv = None
                        for j in range(8):
                            srcap = self.A[:, ti * 16384 + pr * 2048 + (grp * 8 + j) * 128:
                                           ti * 16384 + pr * 2048 + (grp * 8 + j) * 128 + 128]
                            tv = p.do("t", lambda t, j=j, ti=ti, srcap=srcap: t.transpose(
                                pbf[ti][:, j * 128:(j + 1) * 128], srcap, self.ident[:, :]),
                                wait=([("sc", p.now("sc")), ("vc", prev_pair_done[1]),
                                       ("gld", p.now("gld"))] if grp == 0 and ti == 0 else
                                      [("vc", p.now("vc"))]) if j == 0 else (),
                                inc="tc" if j == 7 else None)
                        dst = self.Bar[:, ti * 2048 + grp * 1024:ti * 2048 + (grp + 1) * 1024]
                        ev = p.do("v", lambda v, dst=dst, ti=ti: v.tensor_copy(
                            dst, pbf[ti][:, 0:1024]), wait=(("tc", tv),), inc="vc")
                        tmv.append(ev)
                tm_done = p.now("vc")
                ev_gate = ("vc", tm_done)
                for h in range(2):
                    base = 64 * h
                    for tb in range(0 if "noz" not in SKIP else 4, 4):
                        p.do("t", lambda t, tb=tb: t.matmul(
                            vbank[tb][:, :], zrow, self.ones[0:1, 0:512],
                            start=True, stop=False),
                            wait=[ev_gate, ("gc", zv)] if tb == 0 else ())
                    sg2 = 0
                    for i in range(0 if "norows" not in SKIP else 16, 16):
                        tv = None
                        for tb in range(4):
                            tv = p.do("t", lambda t, i=i, tb=tb, pr=pr, base=base: t.matmul(
                                self.PL[:, tb * 512:(tb + 1) * 512],
                                self.A[base:base + 64, pr * 2048 + i * 128:pr * 2048 + (i + 1) * 128],
                                self.A[base:base + 64,
                                       16384 + pr * 2048 + tb * 512:16384 + pr * 2048 + tb * 512 + 512],
                                start=True, stop=True),
                                wait=([("sc", p.now("sc")), ("vc", tm_done), ev_gate]
                                      if i == 0 else [("sc", sg2)]) if tb == 0 else (),
                                inc="tc" if tb == 3 else None)
                        sg = p.do("s", lambda s: s.activation(qh_row, self.PL[:, :], AF.Sigmoid),
                                  wait=(("tc", tv), ("gld", p.now("gld"))), inc="sc")
                        for tb in range(0 if "nov2" not in SKIP else 4, 4):
                            p.do("t", lambda t, i=i, tb=tb, base=base: t.matmul(
                                vbank[tb][64:128, :],
                                self.Bar[:, i * 128 + base:i * 128 + base + 64],
                                qh_row[:, tb * 512:(tb + 1) * 512],
                                start=False, stop=False, tile_position=(0, 64)),
                                wait=[("sc", sg)] if tb == 0 else ())
                        tv = None
                        for tb in range(4):
                            tv = p.do("t", lambda t, i=i, tb=tb, pr=pr, base=base: t.matmul(
                                self.PL[:, tb * 512:(tb + 1) * 512],
                                self.A[base:base + 64,
                                       16384 + pr * 2048 + i * 128:16384 + pr * 2048 + (i + 1) * 128],
                                self.A[base:base + 64, pr * 2048 + tb * 512:pr * 2048 + tb * 512 + 512],
                                start=True, stop=True),
                                wait=[("sc", sg)] if tb == 0 else (),
                                inc="tc" if tb == 3 else None)
                        sg2 = p.do("s", lambda s: s.activation(qhT_row, self.PL[:, :], AF.Sigmoid),
                                   wait=(("tc", tv),), inc="sc")
                        for sb in range(0 if "nov1" not in SKIP else 4, 4):
                            p.do("t", lambda t, i=i, sb=sb, base=base: t.matmul(
                                vbank[sb][0:64, :],
                                self.Bar[:, 2048 + i * 128 + base:2048 + i * 128 + base + 64],
                                qhT_row[:, sb * 512:(sb + 1) * 512],
                                start=False, stop=False, tile_position=(0, 0)),
                                wait=[("sc", sg2)] if sb == 0 else ())
                    vdone = None
                    for tb in range(0 if "noz" not in SKIP else 4, 4):
                        vdone = p.do("t", lambda t, tb=tb: t.matmul(
                            vbank[tb][:, :], zrow, self.ones[0:1, 0:512],
                            start=False, stop=True),
                            inc="tc" if tb == 3 else None)
                    evs = []
                    for ti, rows0 in (((0, 0), (1, 64)) if "noevac" not in SKIP else ()):
                        direct = (rows0 == base)
                        for tb in range(4):
                            if direct:
                                dst = self.A[base:base + 64,
                                             ti * 16384 + pr * 2048 + tb * 512:
                                             ti * 16384 + pr * 2048 + tb * 512 + 512]
                            else:
                                dst = bounce[rows0:rows0 + 64, tb * 512:(tb + 1) * 512]
                            ev = p.do("v", lambda v, dst=dst, tb=tb, rows0=rows0: v.tensor_copy(
                                dst, vbank[tb][rows0:rows0 + 64, :]),
                                wait=[("tc", vdone), prev_bounce] if (tb == 0 and ti == 0 and vdone) else [prev_bounce] if tb == 0 and ti == 0 else (),
                                inc="vc")
                        if (not direct) and ("nobounce" not in SKIP):
                            bv = p.do("g", lambda g, ti=ti, pr=pr, rows0=rows0, base=base: g.dma_start(
                                self.A[base:base + 64,
                                       ti * 16384 + pr * 2048:ti * 16384 + (pr + 1) * 2048],
                                bounce[rows0:rows0 + 64, :]),
                                wait=(("vc", p.now("vc")),), inc="gld", inc_amt=16)
                            prev_bounce = ("gld", bv)
                    ev_gate = ("vc", p.now("vc"))
                prev_pair_done = ("vc", p.now("vc"))
            gate = {0: ("vc", 0), 1: ("vc", 0)}
            slot_gate = [("tc", 0)] * 4
            pcount = 0
            for dc in range(8):
                lds = []
                for ti, wsrc in enumerate((self.u_n, self.v_n)):
                    si = ti * 2 + (dc % 2)
                    slot = self.DT[:, si * 1024:(si + 1) * 1024]
                    src = wsrc[:, dc * 128:(dc + 1) * 128].rearrange(
                        "(k q) c -> q k c", q=128)
                    dst3 = slot.rearrange("q (k c) -> q k c", c=128)
                    ld = p.do("g", lambda g, dst3=dst3, src=src: g.dma_start(dst3, src),
                              wait=[slot_gate[si], prev_bounce, ("vc", prev_pair_done[1])],
                              inc="gld", inc_amt=16)
                    lds.append(ld)
                for nb4 in range(4):
                    pi = pcount % 2
                    pcount += 1
                    ps = pp[pi]
                    tv = None
                    for ti in range(2):
                        for k in range(NCH):
                            tv = p.do("t", lambda t, k=k, ti=ti, dc=dc, nb4=nb4, ps=ps: t.matmul(
                                ps[:, :],
                                self.DT[:, (ti * 2 + dc % 2) * 1024 + k * 128:
                                        (ti * 2 + dc % 2) * 1024 + (k + 1) * 128],
                                self.A[:, ti * 16384 + k * 2048 + nb4 * 512:
                                       ti * 16384 + k * 2048 + nb4 * 512 + 512],
                                start=(ti == 0 and k == 0), stop=(ti == 1 and k == NCH - 1)),
                                wait=[("gld", lds[1]), gate[pi],
                                      ("vc", prev_pair_done[1])] if ti == 0 and k == 0 else (),
                                inc="tc" if (ti == 1 and k == NCH - 1) else None)
                    xs = self.xfm(b, dc)[:, nb4 * 512:(nb4 + 1) * 512]
                    ev = p.do("v", lambda v, xs=xs, ps=ps: v.scalar_tensor_tensor(
                        xs, ps[:, :], 0.03125, xs, ALU.mult, ALU.add),
                        wait=(("tc", tv),), inc="vc")
                    gate[pi] = ("vc", ev)
                slot_gate[2 * 0 + dc % 2] = ("tc", p.now("tc"))
                slot_gate[2 * 1 + dc % 2] = ("tc", p.now("tc"))
            self.xfm_ready = ("vc", p.now("vc"))

    def emit_l1attn(self):
        p = self.p
        WQK = self.DT[:, 0:1024]
        att = [self.DT[:, 1024:3072], self.DT[:, 3072:5120]]
        nrow = self.SL[0:1, 0:2048]
        stg1 = self.DT[0:16, 5120:5248]
        stg2 = self.DT[0:16, 5248:5376]
        nmx_bf = self.DT[:, 5376:5392]
        rr_bf = self.DT[:, 5392:5408]
        rrb = self.PMRN
        zrow = self.DT[0:1, 5408:5536]
        SKIP1 = os.environ.get("L1SKIP", "")
        for b in range(NB if "bb0" not in SKIP1 else 1):
            zv = p.do("g", lambda g: g.memset(zrow, 0.0),
                      wait=(("sc", p.now("sc")), ("tc", p.now("tc")),
                            ("vc", p.now("vc"))), inc="gc")
            lbv = p.do("g", lambda g: g.dma_start(self.PMRN[0:1, 0:1024],
                                                  self.bvr[:, :]),
                       inc="gld", inc_amt=16)
            lnv = self.emit_ln(b, lambda c: self.Bar[:, c * 2048:(c + 1) * 2048])
            lv = p.do("g", lambda g: g.dma_start(
                self.WB[:, :].rearrange("q (k c) -> q k c", c=1024),
                self.WvT[:, :].rearrange("(k q) c -> q k c", q=128)),
                wait=(("tc", p.now("tc")), ("sc", p.now("sc"))), inc="gld", inc_amt=16)
            pp = [self.PA, self.PB]
            gate = {0: ("vc", 0), 1: ("vc", 0)}
            pcount = 0
            for nt in range(16):
                for oh in range(2):
                    pi = pcount % 2
                    pcount += 1
                    ps = pp[pi]
                    for k in range(NCH):
                        p.do("t", lambda t, k=k, nt=nt, oh=oh, ps=ps: t.matmul(
                            ps[:, :],
                            self.Bar[:, k * 2048 + nt * 128:k * 2048 + (nt + 1) * 128],
                            self.WB[:, k * 1024 + oh * 512:k * 1024 + oh * 512 + 512],
                            start=(k == 0), stop=False),
                            wait=[("vc", lnv), ("gld", lv), gate[pi]] if k == 0 else ())
                    tv = p.do("t", lambda t, oh=oh, ps=ps: t.matmul(
                        ps[:, :], self.ones[0:1, 0:128],
                        self.PMRN[0:1, oh * 512:oh * 512 + 512],
                        start=False, stop=True), wait=(("gld", lbv),), inc="tc")
                    dst = self.A[:, 16384 + nt * 1024 + oh * 512:16384 + nt * 1024 + oh * 512 + 512]
                    ev = p.do("v", lambda v, dst=dst, ps=ps: v.tensor_copy(dst, ps[:, :]),
                              wait=(("tc", tv),), inc="vc")
                    gate[pi] = ("vc", ev)
            vh_done = p.now("vc")
            wq_gate = ("vc", 0)
            for h in range(16):
                base = 64 * (h % 2)
                ld = p.do("g", lambda g, h=h, base=base: g.dma_start(
                    self.SL[base:base + 64, 0:64], self.trilT[h, :, :]),
                    wait=[wq_gate, ("tc", p.now("tc"))], inc="gld", inc_amt=16)
                tv = p.do("t", lambda t, base=base: t.matmul(
                    self.PC[base:base + 64, 0:64], self.SL[base:base + 64, 0:64],
                    self.SL[base:base + 64, 0:64], start=True, stop=True,
                    tile_position=(base, base)),
                    wait=(("gld", ld), ("vc", p.now("vc"))), inc="tc")
                ev = p.do("v", lambda v, h=h, base=base: v.tensor_copy(
                    WQK[base:base + 64, h * 64:(h + 1) * 64],
                    self.PC[base:base + 64, 0:64]),
                    wait=(("tc", tv),), inc="vc")
                wq_gate = ("vc", ev)
            gate = {0: ("sc", 0), 1: ("sc", 0)}
            pcount = 0
            for pr in range(8):
                for nb4 in range(4):
                    pi = pcount % 2
                    pcount += 1
                    ps0, ps1 = (self.PA, self.PB) if pi == 0 else (self.PC, self.PD)
                    tv = None
                    for par in range(2):
                        h = 2 * pr + par
                        base = 64 * par
                        psx = ps0 if par == 0 else ps1
                        tv = p.do("t", lambda t, h=h, base=base, pr=pr, nb4=nb4, psx=psx: t.matmul(
                            psx[base:base + 64, :], WQK[base:base + 64, h * 64:(h + 1) * 64],
                            self.Bar[base:base + 64, pr * 2048 + nb4 * 512:pr * 2048 + nb4 * 512 + 512],
                            start=True, stop=True, tile_position=(base, base)),
                            wait=[("vc", wq_gate[1]), gate[pi]] if par == 0 else (),
                            inc="tc" if par == 1 else None)
                    for par in range(2):
                        base = 64 * par
                        psx = ps0 if par == 0 else ps1
                        dst = self.A[base:base + 64,
                                     pr * 2048 + nb4 * 512:pr * 2048 + nb4 * 512 + 512]
                        ev = p.do("s", lambda s, dst=dst, psx=psx, base=base: s.mul(
                            dst, psx[base:base + 64, :], 0.125),
                            wait=(("tc", tv),) if par == 0 else (), inc="sc")
                    gate[pi] = ("sc", ev)
            lo = p.do("g", lambda g: g.dma_start(
                self.WB[:, :].rearrange("q (k c) -> q k c", c=1024),
                self.WoT[:, :].rearrange("(k q) c -> q k c", q=128)),
                wait=(("vc", vh_done), ("tc", p.now("tc"))), inc="gld", inc_amt=16)
            pvbank = [self.PA, self.PB, self.PC, self.PD]
            prev_evac = ("vc", 0)
            plbf = self.PL[:, :].bitcast(BF16)
            for h in range(0 if "heads" not in SKIP1 else 16, 1 if "h1" in SKIP1 else 16):
                pr, par = h // 2, h % 2
                base = 64 * par
                sgp = 0
                for i in range(16):
                    tv = None
                    for tb in range(4):
                        tv = p.do("t", lambda t, i=i, tb=tb, pr=pr, base=base: t.matmul(
                            self.PL[:, tb * 512:(tb + 1) * 512],
                            self.A[base:base + 64, pr * 2048 + i * 128:pr * 2048 + (i + 1) * 128],
                            self.Bar[base:base + 64, pr * 2048 + tb * 512:pr * 2048 + tb * 512 + 512],
                            start=True, stop=True),
                            wait=[("sc", p.now("sc")), ("vc", p.now("vc"))] if (i == 0 and tb == 0)
                                 else ([("sc", sgp)] if tb == 0 else ()),
                            inc="tc" if tb == 3 else None)
                    rv = p.do("v", lambda v: v.reduce_max(
                        self.ST[:, ST_MS1:ST_MS1 + 1], self.PL[:, :], axis=AX.X,
                        negate=True), wait=(("tc", tv),), inc="vc")
                    rv = p.do("v", lambda v, i=i: v.tensor_copy(
                        nmx_bf[:, i:i + 1], self.ST[:, ST_MS1:ST_MS1 + 1]),
                        wait=(("vc", rv),), inc="vc")
                    rv = p.do("v", lambda v, i=i: v.tensor_copy(
                        self.ST[:, ST_NMX + i:ST_NMX + i + 1], nmx_bf[:, i:i + 1]),
                        wait=(("vc", rv),), inc="vc")
                    sgp = p.do("s", lambda s, i=i: s.activation(
                        att[0], self.PL[:, :], AF.Exp,
                        bias=self.ST[:, ST_NMX + i:ST_NMX + i + 1],
                        accum_out=self.ST[:, ST_RS + i:ST_RS + i + 1]),
                        wait=(("vc", rv),), inc="sc")
                tv = p.do("t", lambda t: t.transpose(
                    plbf[0:16, 0:128], nmx_bf[:, 0:16], self.ident[:, :]),
                    wait=(("sc", sgp),), inc="tc")
                fv = p.do("v", lambda v: v.tensor_copy(stg1, plbf[0:16, 0:128]),
                          wait=(("tc", tv),), inc="vc")
                yv = p.do("y", lambda y: y.dma_start(
                    self.bncb[0:1, :].rearrange("x (a c) -> (x a) c", c=128), stg1),
                    wait=(("vc", fv), ("yld", p.now("yld"))), inc="yld", inc_amt=16)
                yv = p.do("y", lambda y: y.dma_start(
                    nrow, self.bncb[0:1, :]), wait=(("yld", yv),), inc="yld", inc_amt=16)
                rv = p.do("v", lambda v: v.reciprocal(
                    self.ST[:, ST_RR:ST_RR + 16], self.ST[:, ST_RS:ST_RS + 16]),
                    wait=(("sc", sgp),), inc="vc")
                rv = p.do("v", lambda v: v.tensor_copy(
                    rr_bf[:, 0:16], self.ST[:, ST_RR:ST_RR + 16]),
                    wait=(("vc", rv),), inc="vc")
                tv = p.do("t", lambda t: t.transpose(
                    plbf[0:16, 128:256], rr_bf[:, 0:16], self.ident[:, :]),
                    wait=(("vc", rv),), inc="tc")
                fv = p.do("v", lambda v: v.tensor_copy(stg2, plbf[0:16, 128:256]),
                          wait=(("tc", tv),), inc="vc")
                yv2 = p.do("y", lambda y: y.dma_start(
                    self.bncb[0:1, :].rearrange("x (a c) -> (x a) c", c=128), stg2),
                    wait=(("vc", fv), ("yld", yv)), inc="yld", inc_amt=16)
                yv2 = p.do("g", lambda g, base=base: g.dma_start(
                    rrb[base:base + 64, :],
                    bass.AP(self.bncb.tensor, 0, [[0, 64], [1, 2048]])),
                    wait=(("yld", yv2),
                          ("vc", prev_evac[1] if h else p.now("vc"))),
                    inc="gld", inc_amt=16)
                rrb_done = yv2
                if par == 0:
                    for sb in range(4):
                        p.do("t", lambda t, sb=sb: t.matmul(
                            pvbank[sb][:, :], zrow, self.ones[0:1, 0:512],
                            start=True, stop=False),
                            wait=[prev_evac, ("gc", zv)] if sb == 0 else ())
                sg2 = 0
                pvlast = None
                for j in range(16):
                    tv = None
                    for sb in range(4):
                        p.do("t", lambda t, j=j, sb=sb, pr=pr, base=base: t.matmul(
                            self.PL[:, sb * 512:(sb + 1) * 512],
                            self.Bar[base:base + 64, pr * 2048 + j * 128:pr * 2048 + (j + 1) * 128],
                            self.A[base:base + 64, pr * 2048 + sb * 512:pr * 2048 + sb * 512 + 512],
                            start=True, stop=False),
                            wait=[("yld", p.now("yld")), ("sc", p.now("sc"))]
                                 if (sb == 0 and j == 0) else
                                 ([("sc", sg2)] if sb == 0 else ()))
                        tv = p.do("t", lambda t, sb=sb: t.matmul(
                            self.PL[:, sb * 512:(sb + 1) * 512], self.ones[0:1, 0:128],
                            nrow[0:1, sb * 512:sb * 512 + 512],
                            start=False, stop=True), inc="tc" if sb == 3 else None)
                    sg2 = p.do("s", lambda s, j=j: s.activation(
                        att[j % 2], self.PL[:, :], AF.Exp),
                        wait=(("tc", tv),), inc="sc")
                    for sb in range(4):
                        pvlast = p.do("t", lambda t, j=j, sb=sb, par=par, h=h: t.matmul(
                            pvbank[sb][64 * par:64 * par + 64, :],
                            self.A[:, 16384 + j * 1024 + h * 64:16384 + j * 1024 + (h + 1) * 64],
                            att[j % 2][:, sb * 512:(sb + 1) * 512],
                            start=False, stop=False, tile_position=(0, 64 * par)),
                            wait=[("sc", sg2)] if sb == 0 else ())
                if par == 1:
                    pvlast = None
                    for sb in range(4):
                        pvlast = p.do("t", lambda t, sb=sb: t.matmul(
                            pvbank[sb][:, :], zrow, self.ones[0:1, 0:512],
                            start=False, stop=True),
                            inc="tc" if sb == 3 else None)
                    evs = None
                    for sb in range(4):
                        dst = self.A[:, pr * 2048 + sb * 512:pr * 2048 + sb * 512 + 512]
                        evs = p.do("v", lambda v, dst=dst, sb=sb: v.tensor_tensor(
                            dst, pvbank[sb][:, :], rrb[:, sb * 512:(sb + 1) * 512],
                            ALU.mult),
                            wait=[("tc", pvlast), ("gld", rrb_done)] if sb == 0 else (),
                            inc="vc")
                    prev_evac = ("vc", evs)
            lbo = p.do("g", lambda g: g.dma_start(self.PMRN[0:1, 0:1024],
                                                  self.bor[:, :]),
                       wait=(("vc", p.now("vc")),), inc="gld", inc_amt=16)
            gate = {0: ("vc", 0), 1: ("vc", 0)}
            pcount = 0
            for dc in range(8):
                for nb4 in range(4):
                    pi = pcount % 2
                    pcount += 1
                    ps = pp[pi]
                    for k in range(NCH):
                        p.do("t", lambda t, k=k, dc=dc, nb4=nb4, ps=ps: t.matmul(
                            ps[:, :], self.WB[:, k * 1024 + dc * 128:k * 1024 + (dc + 1) * 128],
                            self.A[:, k * 2048 + nb4 * 512:k * 2048 + nb4 * 512 + 512],
                            start=(k == 0), stop=False),
                            wait=[("gld", lo), ("vc", p.now("vc")), gate[pi]] if k == 0 else ())
                    tv = p.do("t", lambda t, dc=dc, ps=ps: t.matmul(
                        ps[:, :], self.PMRN[0:1, dc * 128:(dc + 1) * 128],
                        self.ones[0:1, 0:512], start=False, stop=True),
                        wait=(("gld", lbo),), inc="tc")
                    xs = self.xfm(b, dc)[:, nb4 * 512:(nb4 + 1) * 512]
                    ev = p.do("v", lambda v, xs=xs, ps=ps: v.tensor_add(xs, ps[:, :], xs),
                              wait=(("tc", tv),), inc="vc")
                    gate[pi] = ("vc", ev)
            self.xfm_ready = ("vc", p.now("vc"))

    def emit_l0ffn(self):
        p = self.p
        for b in range(NB):
            lnv = self.emit_ln(b, lambda c: self.Bar[:, c * 2048:(c + 1) * 2048], rms=True)
            p.do("g", lambda g: g.memset(self.PMRN[:, :], -1e30),
                 wait=(("vc", p.now("vc")),), inc="gc")
            pmg = p.now("gc")
            se_b = self.SL[:, 0:1024]
            sn_b = self.SL[:, 1024:2048]
            snf = self.DT[:, 2048:3072]
            pbf = self.PC[:, :].bitcast(BF16)
            ev_gate = ("vc", 0)
            sp_gate = ("yld", 0)
            st_gate = ("sc", 0)
            for mt in range(32):
                ld = p.do("g", lambda g, mt=mt: g.dma_start(
                    se_b, self.se[mt * 128:(mt + 1) * 128, :]),
                    wait=(("tc", p.now("tc")),), inc="gld", inc_amt=16)
                sv = p.do("v", lambda v: v.scalar_tensor_tensor(
                    sn_b, se_b, 1.0, se_b, ALU.mult, ALU.mult,
                    accum_out=self.ST[:, ST_MS1:ST_MS1 + 1]),
                    wait=(("gld", ld), ("yld", self.init_yld)), inc="vc")
                sq = p.do("s", lambda s: s.activation(
                    self.ST[:, ST_MS1 + 1:ST_MS1 + 2], self.ST[:, ST_MS1:ST_MS1 + 1],
                    AF.Sqrt), wait=(("vc", sv),), inc="sc")
                sv = p.do("v", lambda v: v.tensor_scalar_max(
                    self.ST[:, ST_MS1 + 1:ST_MS1 + 2], self.ST[:, ST_MS1 + 1:ST_MS1 + 2],
                    1e-12), wait=(("sc", sq),), inc="vc")
                sv = p.do("v", lambda v: v.reciprocal(
                    self.ST[:, ST_MS1 + 2:ST_MS1 + 3], self.ST[:, ST_MS1 + 1:ST_MS1 + 2]),
                    wait=(("vc", sv),), inc="vc")
                sv = p.do("v", lambda v: v.tensor_scalar(
                    sn_b, se_b, self.ST[:, ST_MS1 + 2:ST_MS1 + 3], 32.0,
                    ALU.mult, ALU.mult), wait=(("vc", sv),), inc="vc")
                tv = None
                for c in range(NCH):
                    tv = p.do("t", lambda t, c=c: t.transpose(
                        pbf[:, c * 128:(c + 1) * 128], sn_b[:, c * 128:(c + 1) * 128],
                        self.ident[:, :]),
                        wait=[("vc", p.now("vc")), ("gc", self.init_gc),
                              st_gate] if c == 0 else (),
                        inc="tc" if c == NCH - 1 else None)
                snv = None
                for c in range(NCH):
                    snv = p.do("v", lambda v, c=c: v.tensor_scalar_mul(
                        snf[:, c * 128:(c + 1) * 128], pbf[:, c * 128:(c + 1) * 128],
                        self.ST[:, ST_GS + c:ST_GS + c + 1]),
                        wait=[("tc", tv)] if c == 0 else (), inc="vc")
                tv = None
                for nb4 in range(4):
                    for k in range(NCH):
                        tv = p.do("t", lambda t, k=k, nb4=nb4: t.matmul(
                            self.PL[:, nb4 * 512:(nb4 + 1) * 512],
                            snf[:, k * 128:(k + 1) * 128],
                            self.Bar[:, k * 2048 + nb4 * 512:k * 2048 + nb4 * 512 + 512],
                            start=(k == 0), stop=(k == NCH - 1)),
                            wait=[("vc", max(snv, lnv)), ev_gate,
                                  st_gate] if k == 0 and nb4 == 0 else (),
                            inc="tc" if (k == NCH - 1 and nb4 == 3) else None)
                rv = p.do("v", lambda v, mt=mt: v.reduce_max(
                    self.ST[:, ST_MS1 + 3:ST_MS1 + 4], self.PL[:, :], axis=AX.X,
                    negate=True), wait=(("tc", tv), ("gc", pmg)), inc="vc")
                p.do("v", lambda v, mt=mt: v.tensor_scalar_mul(
                    self.ST[:, ST_RM + mt:ST_RM + mt + 1],
                    self.ST[:, ST_MS1 + 3:ST_MS1 + 4], -1.0),
                    wait=(("vc", rv),), inc="vc")
                pmv = p.do("v", lambda v: v.tensor_max(
                    self.PMRN[:, :], self.PMRN[:, :], self.PL[:, :]), inc="vc")
                sx = p.do("s", lambda s, mt=mt: s.activation(
                    self.DT[:, 0:2048], self.PL[:, :], AF.Exp,
                    bias=self.ST[:, ST_MS1 + 3:ST_MS1 + 4],
                    accum_out=self.ST[:, ST_CS + mt:ST_CS + mt + 1]),
                    wait=(("vc", rv),), inc="sc")
                sx = p.do("s", lambda s: s.activation(self.FSA[:, :], self.PL[:, :],
                                                      AF.Copy),
                          wait=(sp_gate,), inc="sc")
                st_gate = ("sc", sx)
                yv = p.do("y", lambda y, b=b, mt=mt: y.dma_start(
                    self.Lsp[b, mt, :, :], self.FSA[:, :]),
                    wait=(("sc", sx),), inc="yld", inc_amt=16)
                sp_gate = ("yld", yv)
                ev_gate = ("vc", pmv)
            sv = p.do("s", lambda s: s.activation(
                self.ST[:, ST_NAM:ST_NAM + 32], self.ST[:, ST_CS:ST_CS + 32], AF.Ln),
                wait=(("sc", p.now("sc")),), inc="sc")
            vv = p.do("v", lambda v: v.tensor_add(
                self.ST[:, ST_NAM:ST_NAM + 32], self.ST[:, ST_NAM:ST_NAM + 32],
                self.ST[:, ST_RM:ST_RM + 32]), wait=(("sc", sv),), inc="vc")
            vv = p.do("v", lambda v: v.tensor_scalar_mul(
                self.ST[:, ST_NAM:ST_NAM + 32], self.ST[:, ST_NAM:ST_NAM + 32], -1.0),
                wait=(("vc", vv),), inc="vc")
            nam_done = vv
            rnv = None
            for grp in range(4):
                tv = None
                for j in range(4):
                    k = grp * 4 + j
                    tv = p.do("t", lambda t, k=k, j=j: t.transpose(
                        pbf[:, j * 128:(j + 1) * 128],
                        self.PMRN[:, k * 128:(k + 1) * 128], self.ident[:, :]),
                        wait=[("vc", p.now("vc"))] if j == 0 else (),
                        inc="tc" if j == 3 else None)
                rnv = p.do("v", lambda v, grp=grp: v.reduce_max(
                    self.ST[:, ST_RN + grp * 4:ST_RN + (grp + 1) * 4],
                    pbf[:, 0:512].rearrange("p (a c) -> p a c", a=4), axis=AX.X),
                    wait=(("tc", tv),), inc="vc")
            cv = p.do("v", lambda v: v.tensor_copy(
                self.SL[:, 0:16], self.ST[:, ST_RN:ST_RN + 16]),
                wait=(("vc", rnv),), inc="vc")
            tv = p.do("t", lambda t: t.transpose(
                self.PL[:, :].bitcast(BF16)[0:16, 0:128], self.SL[:, 0:16],
                self.ident[:, :]), wait=(("vc", cv), st_gate), inc="tc")
            cv = p.do("v", lambda v: v.tensor_copy(
                self.SL[0:16, 128:256], self.PL[:, :].bitcast(BF16)[0:16, 0:128]),
                wait=(("tc", tv),), inc="vc")
            yv = p.do("y", lambda y: y.dma_start(
                self.bncb[0:1, :].rearrange("x (a c) -> (x a) c", c=128),
                self.SL[0:16, 128:256]), wait=(("vc", cv), sp_gate), inc="yld", inc_amt=16)
            bv = p.do("g", lambda g: g.dma_start(
                self.PMRN[:, :], bass.AP(self.bncb.tensor, 0, [[0, 128], [1, 2048]])),
                wait=(("yld", yv), ("vc", p.now("vc"))), inc="gld", inc_amt=16)
            rn_b_done = bv
            for k in range(16):
                tv = None
                for c in range(NCH):
                    tv = p.do("t", lambda t, k=k, c=c: t.transpose(
                        pbf[:, c * 128:(c + 1) * 128],
                        self.Bar[:, c * 2048 + k * 128:c * 2048 + (k + 1) * 128],
                        self.ident[:, :]),
                        wait=[("vc", p.now("vc")), ("tc", p.now("tc"))] if c == 0 else (),
                        inc="tc" if c == NCH - 1 else None)
                p.do("v", lambda v, k=k: v.tensor_copy(
                    self.A[:, k * 1024:(k + 1) * 1024], pbf[:, 0:1024]),
                    wait=(("tc", tv),), inc="vc")
            xntm_done = p.now("vc")
            dsl = [self.DT[:, 0:2048], self.DT[:, 2048:4096]]
            wslot = [self.WB[:, 0:4096], self.WB[:, 4096:8192]]
            dtv_gate = ("tc", 0)
            wgate = [("tc", 0), ("tc", 0)]
            eow = self.Bar[:, 12288:13312]
            eow_gate = ("yld", 0)
            for e in range(NE):
                lb = p.do("g", lambda g, e=e: g.dma_start(
                    self.ST[:, ST_B1:ST_B1 + 32], self.b1c[e, :, :]),
                    wait=(("sc", p.now("sc")),), inc="gld", inc_amt=16)
                lb2 = p.do("g", lambda g, e=e: g.dma_start(
                    self.SL[0:1, 0:1024], self.b2r[e, :, :]), inc="gld", inc_amt=16)
                for ms in range(4):
                    mt = 4 * e + ms
                    lv = p.do("y", lambda y, b=b, mt=mt: y.dma_start(
                        self.FSA[:, :], self.Lsp[b, mt, :, :]),
                        wait=(st_gate, ("sc", p.now("sc"))), inc="yld", inc_amt=16)
                    dx = p.do("s", lambda s, ms=ms, mt=mt: s.activation(
                        dsl[ms % 2], self.FSA[:, :], AF.Exp,
                        bias=self.ST[:, ST_NAM + mt:ST_NAM + mt + 1]),
                        wait=(("yld", lv), ("vc", nam_done), dtv_gate), inc="sc")
                    st_gate = ("sc", dx)
                    for grp in range(2):
                        tv = None
                        for j in range(8):
                            k = grp * 8 + j
                            tv = p.do("t", lambda t, ms=ms, k=k, j=j: t.transpose(
                                pbf[:, j * 128:(j + 1) * 128],
                                dsl[ms % 2][:, k * 128:(k + 1) * 128], self.ident[:, :]),
                                wait=[("sc", dx), ("vc", p.now("vc"))] if j == 0 else (),
                                inc="tc" if j == 7 else None)
                        dtv_gate = ("tc", tv)
                        src3 = pbf[:, 0:1024].rearrange("p (k c) -> p k c", c=128)
                        dd = self.Bar[:, :].rearrange(
                            "p (k c) -> p k c", c=512)[:, grp * 8:(grp + 1) * 8,
                                                       ms * 128:(ms + 1) * 128]
                        p.do("v", lambda v, dd=dd, src3=src3: v.tensor_copy(dd, src3),
                             wait=(("tc", tv),), inc="vc")
                dtr_done = p.now("vc")
                sgate = ("sc", 0)
                for ds in range(NCH):
                    ps = [self.PA, self.PB][ds % 2]
                    tv = None
                    for k in range(16):
                        tv = p.do("t", lambda t, ds=ds, k=k, ps=ps: t.matmul(
                            ps[:, :], self.A[:, k * 1024 + ds * 128:k * 1024 + (ds + 1) * 128],
                            self.Bar[:, :].rearrange("p (k c) -> p k c", c=512)[:, k, :],
                            start=(k == 0), stop=(k == 15)),
                            wait=[("vc", dtr_done), sgate] if k == 0 else (),
                            inc="tc" if k == 15 else None)
                    sv = p.do("v", lambda v, ds=ds, ps=ps: v.tensor_copy(
                        self.Bar[:, 8192 + ds * 512:8192 + (ds + 1) * 512], ps[:, :]),
                        wait=(("tc", tv),), inc="vc")
                    sgate = ("vc", sv)
                slots_done = p.now("vc")
                hgate = ("sc", 0)
                for q8 in range(8):
                    wl = p.do("g", lambda g, e=e, q8=q8: g.dma_start(
                        wslot[q8 % 2].rearrange("q (k c) -> q k c", c=512),
                        self.w1[e, :, q8 * 512:(q8 + 1) * 512].rearrange(
                            "(k q) c -> q k c", q=128)),
                        wait=(wgate[q8 % 2],), inc="gld", inc_amt=16)
                    for ht in range(4):
                        ps = [self.PA, self.PB][ht % 2]
                        tv = None
                        for k in range(NCH):
                            tv = p.do("t", lambda t, q8=q8, ht=ht, k=k, ps=ps: t.matmul(
                                ps[:, :],
                                wslot[q8 % 2][:, k * 512 + ht * 128:k * 512 + (ht + 1) * 128],
                                self.Bar[:, 8192 + k * 512:8192 + (k + 1) * 512],
                                start=(k == 0), stop=(k == NCH - 1)),
                                wait=[("gld", wl), ("vc", slots_done),
                                      hgate] if k == 0 else (),
                                inc="tc" if k == NCH - 1 else None)
                        hv = p.do("s", lambda s, q8=q8, ht=ht, ps=ps: s.activation(
                            self.A[:, 16384 + (q8 * 4 + ht) * 512:
                                   16384 + (q8 * 4 + ht + 1) * 512],
                            ps[:, :], AF.Gelu,
                            bias=self.ST[:, ST_B1 + q8 * 4 + ht:ST_B1 + q8 * 4 + ht + 1]),
                            wait=(("tc", tv), ("gld", lb)), inc="sc")
                        hgate = ("sc", hv)
                    wgate[q8 % 2] = ("tc", p.now("tc"))
                h_done = p.now("sc")
                eops = [self.PL[:, 0:512], self.PL[:, 512:1024],
                        self.PL[:, 1024:1536], self.PL[:, 1536:2048],
                        self.PA[:, :], self.PB[:, :], self.PC[:, :], self.PD[:, :]]
                for hq in range(8):
                    wl = p.do("g", lambda g, e=e, hq=hq: g.dma_start(
                        wslot[hq % 2].rearrange("q (k c) -> q k c", c=1024),
                        self.w2[e, hq * 512:(hq + 1) * 512, :].rearrange(
                            "(k q) c -> q k c", q=128)),
                        wait=(wgate[hq % 2],), inc="gld", inc_amt=16)
                    for hc in range(4):
                        kk = hq * 4 + hc
                        for gi in range(8):
                            msub, dh = gi // 2, gi % 2
                            last = (hc == 3 and gi == 7)
                            w_ = ()
                            if hc == 0 and gi == 0:
                                w_ = [("gld", wl)] + (
                                    [("sc", h_done), ("vc", p.now("vc"))] if hq == 0 else [])
                            tq = p.do("t", lambda t, kk=kk, hc=hc, hq=hq, msub=msub, dh=dh, gi=gi: t.matmul(
                                eops[gi],
                                self.A[:, 16384 + kk * 512 + msub * 128:
                                       16384 + kk * 512 + (msub + 1) * 128],
                                wslot[hq % 2][:, hc * 1024 + dh * 512:hc * 1024 + (dh + 1) * 512],
                                start=(kk == 0), stop=False),
                                wait=w_, inc="tc" if last else None)
                    wgate[hq % 2] = ("tc", tq)
                tvb = None
                for gi in range(8):
                    msub, dh = gi // 2, gi % 2
                    tvb = p.do("t", lambda t, msub=msub, dh=dh, gi=gi: t.matmul(
                        eops[gi], self.ones[0:1, 0:128],
                        self.SL[0:1, dh * 512:(dh + 1) * 512],
                        start=False, stop=True), inc="tc" if gi == 7 else None)

                for msub in range(4):
                    mt = 4 * e + msub
                    ev = None
                    for dh in range(2):
                        ev = p.do("v", lambda v, msub=msub, dh=dh: v.tensor_copy(
                            eow[:, dh * 512:(dh + 1) * 512], eops[msub * 2 + dh]),
                            wait=[("tc", tvb), eow_gate, ("gld", lb2)] if dh == 0 else (),
                            inc="vc")
                    yv = p.do("y", lambda y, b=b, mt=mt: y.dma_start(
                        self.eod[b, mt * 128:(mt + 1) * 128, :], eow[:, :]),
                        wait=(("vc", ev),), inc="yld", inc_amt=16)
                    eow_gate = ("yld", yv)
            craw = self.Bar
            rrb = self.SL[:, 0:512]
            el = None
            for mt in range(32):
                el = p.do("g", lambda g, b=b, mt=mt: g.dma_start(
                    self.A[:, mt * 1024:(mt + 1) * 1024],
                    self.eod[b, mt * 128:(mt + 1) * 128, :]),
                    wait=[("tc", p.now("tc")), ("sc", p.now("sc")), eow_gate,
                          ("vc", p.now("vc"))] if mt == 0 else (),
                    inc="gld", inc_amt=16)
            eo_done = el
            lgate = ("vc", 0)
            for nb4 in range(4):
                crv = ("sc", 0)
                lgate = ("vc", p.now("vc")) if nb4 else lgate
                for mt in range(32):
                    fslot = self.FSA[:, (mt % 2) * 512:(mt % 2) * 512 + 512]
                    lv = p.do("y", lambda y, b=b, mt=mt, nb4=nb4, fslot=fslot: y.dma_start(
                        fslot, self.Lsp[b, mt, :, nb4 * 512:(nb4 + 1) * 512]),
                        wait=(lgate, eow_gate, crv), inc="yld", inc_amt=16)
                    sv = p.do("v", lambda v, fslot=fslot, nb4=nb4: v.tensor_sub(
                        fslot, fslot, self.PMRN[:, nb4 * 512:(nb4 + 1) * 512]),
                        wait=(("yld", lv), ("gld", rn_b_done)), inc="vc")
                    lgate = ("vc", sv)
                    cx = p.do("s", lambda s, mt=mt, fslot=fslot: s.activation(
                        craw[:, mt * 512:(mt + 1) * 512], fslot, AF.Exp),
                        wait=(("vc", sv),), inc="sc")
                    crv = ("sc", cx)
                cr_done = p.now("sc")
                tv = None
                for mt in range(32):
                    tv = p.do("t", lambda t, mt=mt: t.matmul(
                        self.PA[0:1, :], self.ones[:, 0:1],
                        craw[:, mt * 512:(mt + 1) * 512],
                        start=(mt == 0), stop=(mt == 31)),
                        wait=[("sc", cr_done), ("vc", p.now("vc"))] if mt == 0 else (),
                        inc="tc" if mt == 31 else None)
                rv = p.do("v", lambda v: v.reciprocal(self.FSA[0:1, 1024:1536],
                                                      self.PA[0:1, :]),
                          wait=(("tc", tv),), inc="vc")
                rv = p.do("v", lambda v: v.tensor_copy(self.SL[0:1, 512:1024],
                                                       self.FSA[0:1, 1024:1536]),
                          wait=(("vc", rv),), inc="vc")
                tv = p.do("t", lambda t: t.matmul(
                    self.PB[:, :], self.ones[0:1, 0:128], self.SL[0:1, 512:1024],
                    start=True, stop=True), wait=(("vc", rv),), inc="tc")
                rv = p.do("v", lambda v: v.tensor_copy(rrb, self.PB[:, :]),
                          wait=(("tc", tv),), inc="vc")
                rrb_done = rv
                dgate = {0: ("vc", 0), 1: ("vc", 0)}
                for dc in range(NCH):
                    pi = dc % 2
                    ps = [self.PC, self.PD][pi]
                    tv = None
                    for mt in range(32):
                        tv = p.do("t", lambda t, dc=dc, mt=mt, ps=ps: t.matmul(
                            ps[:, :], self.A[:, mt * 1024 + dc * 128:mt * 1024 + (dc + 1) * 128],
                            craw[:, mt * 512:(mt + 1) * 512],
                            start=(mt == 0), stop=(mt == 31)),
                            wait=[("gld", eo_done), ("sc", cr_done),
                                  dgate[pi]] if mt == 0 else (),
                            inc="tc" if mt == 31 else None)
                    fs2 = self.FSA[:, (dc % 2) * 512:(dc % 2) * 512 + 512]
                    ev = p.do("v", lambda v, fs2=fs2, ps=ps: v.tensor_mul(fs2, ps[:, :], rrb),
                              wait=(("tc", tv), ("vc", rrb_done)), inc="vc")
                    xs = self.xfm(b, dc)[:, nb4 * 512:(nb4 + 1) * 512]
                    ev = p.do("v", lambda v, xs=xs, fs2=fs2: v.scalar_tensor_tensor(
                        xs, fs2, 0.5, xs, ALU.mult, ALU.add), inc="vc")
                    dgate[pi] = ("vc", ev)
            self.xfm_ready = ("vc", p.now("vc"))


def build_program(parts):
    nc = bass.Bass("TRN2", target_bir_lowering=False)
    ctx = ExitStack()
    with ctx:
        k = Kern(nc, ctx, parts)
        k.declare_io()
        k.alloc()
        block = ctx.enter_context(nc.Block())
        k.emit_init()
        k.emit_load_x()
        for part in parts:
            if part == "l0ffn":
                k.emit_l0ffn()
            elif part == "l0attn":
                k.emit_l0attn()
            elif part == "l1attn":
                k.emit_l1attn()
            elif part == "l1ffn_a":
                k.emit_l1ffn(first=True)
            elif part == "l1ffn_b":
                k.emit_l1ffn(first=False)
        dbg = os.environ.get("KDBG", "")
        if dbg:
            p = k.p
            srcs = {"A": k.A, "Bar": k.Bar, "PMRN": k.PMRN, "DT": k.DT,
                    "SL": k.SL, "WB": k.WB, "XFM": k.XFM, "ST": k.ST}[dbg.split(":")[0]]
            off = int(dbg.split(":")[1]) if ":" in dbg else 0
            nmax = srcs.shape[1]
            w = min(16384, nmax - off)
            cw = min(w, 1024)
            dst0 = k.y_out[0, :, :].rearrange("(a p) c -> p a c", p=128)[:, 0:max(w // cw, 1), 0:cw]
            src3 = srcs[:, off:off + w].rearrange("p (a c) -> p a c", c=cw)
            p.do("g", lambda g: g.dma_start(dst0, src3),
                 wait=(("vc", p.now("vc")), ("sc", p.now("sc")),
                       ("tc", p.now("tc"))), inc="gld", inc_amt=16)
            p.do("g", lambda g: None, wait=(("gld", p.now("gld")),))
        if not dbg:
            k.emit_store_y()
        k.p.build(block)
    return nc


LAUNCH_PLAN = [
    ["l0ffn"], ["l0attn"], ["l0ffn"], ["l1ffn_a"], ["l1attn"], ["l1ffn_a"],
]


def _norm_parts(parts):
    return ["l1ffn" if p.startswith("l1ffn") else p for p in parts]


def prep_inputs(inp):
    f = np.ascontiguousarray
    d = {}
    d["w1"] = f(inp["L0_ffn_w1"])
    d["w2"] = f(inp["L0_ffn_w2"])
    d["se"] = f(inp["L0_ffn_slot_embeds"].reshape(4096, 1024))
    d["gnc"] = f(inp["L0_ffn_norm_gamma"].reshape(8, 128).T)
    d["gsc"] = f(inp["L0_ffn_slot_norm_gamma"].reshape(8, 128).T)
    d["b1c"] = f(inp["L0_ffn_b1"].reshape(8, 32, 128).transpose(0, 2, 1))
    d["b2r"] = f(inp["L0_ffn_b2"].reshape(8, 1, 1024))
    d["u_n"] = f(inp["L0_attn_ternary_factor_u"])
    d["uT"] = f(inp["L0_attn_ternary_factor_u"].T)
    d["v_n"] = f(inp["L0_attn_ternary_factor_v"])
    d["vT"] = f(inp["L0_attn_ternary_factor_v"].T)
    rows, cols = np.tril_indices(64)
    tril = np.zeros((16, 64, 64), dtype=np.float32)
    tril[:, rows, cols] = inp["L1_attn_params"]
    d["trilT"] = f(tril.transpose(0, 2, 1))
    d["WvT"] = f(inp["L1_attn_Wv"].T)
    d["bvr"] = f(inp["L1_attn_bv"].reshape(1, 1024))
    d["WoT"] = f(inp["L1_attn_Wo"].T)
    d["bor"] = f(inp["L1_attn_bo"].reshape(1, 1024))
    g1r = inp["L1_ffn_G1_R"][0].transpose(1, 0, 2).reshape(4, 1024)
    g1i = inp["L1_ffn_G1_I"][0].transpose(1, 0, 2).reshape(4, 1024)
    g2r = inp["L1_ffn_G2_R"][:, :, 0, :].reshape(4, 256)
    g2i = inp["L1_ffn_G2_I"][:, :, 0, :].reshape(4, 256)
    d["G1r"] = f(np.concatenate([g1r, g1i, g2r, g2i], axis=1))
    d["brr"] = f(inp["L1_ffn_bias_real"].reshape(1, 512))
    d["bir"] = f(inp["L1_ffn_bias_imag"].reshape(1, 512))
    return d


def kernel(**inputs):
    x = np.ascontiguousarray(np.asarray(inputs["L0_attn_qz"], dtype=np.float32))
    wd = prep_inputs({k: np.asarray(v, dtype=np.float32) for k, v in inputs.items()})
    shards = [x[2 * i:2 * i + 2] for i in range(8)]
    progs = {}
    for parts in LAUNCH_PLAN:
        key = tuple(parts)
        if key not in progs:
            progs[key] = build_program(parts)
    cur = shards
    for parts in LAUNCH_PLAN:
        nc = progs[tuple(parts)]
        needed = {"l0ffn": ["w1", "w2", "se", "gnc", "gsc", "b1c", "b2r"],
                  "l0attn": ["u_n", "uT", "v_n", "vT"],
                  "l1attn": ["trilT", "WvT", "bvr", "WoT", "bor"],
                  "l1ffn": ["G1r", "brr", "bir"]}
        names = set()
        for pp_ in _norm_parts(parts):
            names.update(needed[pp_])
        in_maps = []
        for ci in range(8):
            m = {"x": cur[ci]}
            for nm in names:
                m[nm] = wd[nm]
            in_maps.append(m)
        res = run_bass_kernel_spmd(nc, in_maps, core_ids=list(range(8)))
        cur = [np.asarray(r["y"]) for r in res.results]
    return np.concatenate(cur, axis=0)


# version 02
# speedup vs baseline: 1.0466x; 1.0466x over previous
import os
import numpy as np
from contextlib import ExitStack
import concourse.bass as bass
import concourse.mybir as mybir
from concourse.bass_utils import run_bass_kernel_spmd

F32 = mybir.dt.float32
BF16 = mybir.dt.bfloat16
AF = mybir.ActivationFunctionType
ALU = mybir.AluOpType
AX = mybir.AxisListType

NB = 2
NT = 2048
DM = 1024
NCH = 8
NE = 8
HID = 4096

ST_RM = 0
ST_CS = 32
ST_NAM = 64
ST_RN = 96
ST_MS1 = 112
ST_NMX = 128
ST_RS = 144
ST_RR = 160
ST_B1 = 176
ST_GN = 208
ST_GS = 216
ST_MS2 = 224


class P:
    ENGS = ("g", "t", "v", "s", "y")

    def __init__(self, nc, ctx):
        self.nc = nc
        self.ops = {e: [] for e in self.ENGS}
        self.cnt = {}
        self.sems = {}
        for name in ("gld", "yld", "gc", "tc", "vc", "sc"):
            self.sems[name] = ctx.enter_context(nc.semaphore(name))
            self.cnt[name] = 0

    def do(self, eng, fn, wait=(), inc=None, inc_amt=1):
        wait = tuple((s, v) for (s, v) in wait if v > 0)
        for s, v in wait:
            assert v <= self.cnt[s], f"wait {s}>={v} but planned only {self.cnt[s]}"
        if inc is not None:
            self.cnt[inc] += inc_amt
        sems = self.sems

        def emit(e, fn=fn, wait=wait, inc=inc, inc_amt=inc_amt):
            for s, v in wait:
                e.wait_ge(sems[s], v)
            inst = fn(e)
            if inc is not None:
                assert inst is not None
                inst.then_inc(sems[inc], inc_amt)
        self.ops[eng].append(emit)
        return self.cnt[inc] if inc is not None else None

    def now(self, name):
        return self.cnt[name]

    def snap(self):
        return dict(self.cnt)

    def build(self, block):
        ops = self.ops

        @block.gpsimd
        def _(g):
            for f in ops["g"]:
                f(g)

        @block.tensor
        def _(t):
            for f in ops["t"]:
                f(t)

        @block.vector
        def _(v):
            for f in ops["v"]:
                f(v)

        @block.scalar
        def _(s):
            for f in ops["s"]:
                f(s)

        @block.sync
        def _(y):
            for f in ops["y"]:
                f(y)


class Kern:
    def __init__(self, nc, ctx, parts):
        self.nc = nc
        self.ctx = ctx
        self.parts = list(parts)
        self.p = P(nc, ctx)
        self.xfm_ready = ("vc", 0)

    def declare_io(self):
        nc = self.nc
        def ap(name, shape, dt, kind):
            return nc.dram_tensor(name, shape, dt, kind=kind).ap()
        I, O, N = "ExternalInput", "ExternalOutput", "Internal"
        self.x_in = ap("x", [NB, NT, DM], F32, I)
        self.y_out = ap("y", [NB, NT, DM], F32, O)
        pset = {"l1ffn" if q.startswith("l1ffn") else q for q in self.parts}
        if "l0ffn" in pset:
            self.w1 = ap("w1", [NE, DM, HID], F32, I)
            self.w2 = ap("w2", [NE, HID, DM], F32, I)
            self.se = ap("se", [NE * 512, DM], F32, I)
            self.gnc = ap("gnc", [128, NCH], F32, I)
            self.gsc = ap("gsc", [128, NCH], F32, I)
            self.b1c = ap("b1c", [NE, 128, 32], F32, I)
            self.b2r = ap("b2r", [NE, 1, DM], F32, I)
            self.Lsp = ap("Lsp", [NB, 32, 128, NT], F32, N)
            self.eod = ap("eod", [NB, NE * 512, DM], BF16, N)
        if "l0attn" in pset:
            self.u_n = ap("u_n", [DM, DM], F32, I)
            self.uT = ap("uT", [DM, DM], F32, I)
            self.v_n = ap("v_n", [DM, DM], F32, I)
            self.vT = ap("vT", [DM, DM], F32, I)
        if "l1attn" in pset:
            self.trilT = ap("trilT", [16, 64, 64], F32, I)
            self.WvT = ap("WvT", [DM, DM], F32, I)
            self.bvr = ap("bvr", [1, DM], F32, I)
            self.WoT = ap("WoT", [DM, DM], F32, I)
            self.bor = ap("bor", [1, DM], F32, I)
        if "l1ffn" in pset:
            self.G1r = ap("G1r", [4, 2560], F32, I)
            self.brr = ap("brr", [1, 512], F32, I)
            self.bir = ap("bir", [1, 512], F32, I)
            self.pdr = ap("pdr", [1024, 256], F32, N)
            self.wts = ap("wts", [128, 6144], BF16, N)
        self.bncb = ap("bncb", [1, 2048], BF16, N)

    def alloc(self):
        nc, E = self.nc, self.ctx.enter_context
        self.XFM = E(nc.sbuf_tensor("XFM", [128, NB * NCH * NT], BF16))
        self.A = E(nc.sbuf_tensor("Abuf", [128, 32768], BF16))
        self.Bar = E(nc.sbuf_tensor("Bar", [128, 16384], BF16))
        self.DT = E(nc.sbuf_tensor("DT", [128, 6144], BF16))
        self.WB = E(nc.sbuf_tensor("WB", [128, 8192], BF16))
        self.SL = E(nc.sbuf_tensor("SL", [128, 2048], BF16))
        self.FSA = E(nc.sbuf_tensor("FSA", [128, 2048], F32))
        self.PMRN = E(nc.sbuf_tensor("PMRN", [128, 2048], BF16))
        self.ST = E(nc.sbuf_tensor("STT", [128, 256], F32))
        self.ident = E(nc.sbuf_tensor("ident", [128, 128], BF16))
        self.ones = E(nc.sbuf_tensor("onesb", [128, 512], BF16))
        self.PL = E(nc.psum_tensor("PL", [128, 2048], F32))
        self.PA = E(nc.psum_tensor("PA", [128, 512], F32))
        self.PB = E(nc.psum_tensor("PB", [128, 512], F32))
        self.PC = E(nc.psum_tensor("PC", [128, 512], F32))
        self.PD = E(nc.psum_tensor("PD", [128, 512], F32))

    def xfm(self, b, c):
        return self.XFM[:, (b * NCH + c) * NT:(b * NCH + c) * NT + NT]

    def emit_init(self):
        p = self.p
        p.do("g", lambda g: g.memset(self.ones[:, :], 1.0), inc="gc")
        p.do("g", lambda g: g.memset(self.ident[:, :], 0.0), inc="gc")
        v = p.now("gc")
        p.do("g", lambda g: g.affine_select(
            out=self.ident[:, :], in_=self.ident[:, :],
            compare_op=ALU.not_equal, fill=1.0, base=0,
            pattern=[[-1, 128]], channel_multiplier=1), wait=(("gc", v),), inc="gc")
        p.do("g", lambda g: g.memset(self.ST[:, ST_MS2 + 4:ST_MS2 + 5], 1e-5),
             inc="gc")
        if "l0ffn" in self.parts:
            p.do("y", lambda y: y.dma_start(self.ST[:, ST_GN:ST_GN + 8], self.gnc[:, :]),
                 inc="yld", inc_amt=16)
            p.do("y", lambda y: y.dma_start(self.ST[:, ST_GS:ST_GS + 8], self.gsc[:, :]),
                 inc="yld", inc_amt=16)
        self.init_gc = p.now("gc")
        self.init_yld = p.now("yld")

    def emit_load_x(self):
        p = self.p
        xin = self.SL[:, 0:1024]
        pbf = self.PL[:, 0:512].bitcast(BF16)
        xfm3 = self.XFM[:, :].rearrange("p (k n) -> p k n", n=NT)
        for b in range(NB):
            for k in range(16):
                ld = p.do("g", lambda g, b=b, k=k: g.dma_start(
                    xin, self.x_in[b, k * 128:(k + 1) * 128, :]),
                    wait=(("tc", p.now("tc")),), inc="gld", inc_amt=16)
                tw = [("gld", ld), ("gc", self.init_gc), ("vc", p.now("vc"))]
                tcv = None
                for c in range(NCH):
                    tcv = p.do("t", lambda t, c=c: t.transpose(
                        pbf[:, c * 128:(c + 1) * 128], xin[:, c * 128:(c + 1) * 128],
                        self.ident[:, :]),
                        wait=tw if c == 0 else (), inc="tc" if c == NCH - 1 else None)
                dst = xfm3[:, b * NCH:(b + 1) * NCH, k * 128:k * 128 + 128]
                src = pbf[:, :].rearrange("p (c j) -> p c j", j=128)
                p.do("v", lambda v, dst=dst, src=src: v.tensor_copy(dst, src),
                     wait=(("tc", tcv),), inc="vc")
        self.xfm_ready = ("vc", p.now("vc"))

    def emit_store_y(self):
        p = self.p
        xout = self.FSA[:, 0:1024]
        pbf = self.PL[:, 0:512].bitcast(BF16)
        xfm3 = self.XFM[:, :].rearrange("p (k n) -> p k n", n=NT)
        rs, rv = self.xfm_ready
        for b in range(NB):
            for k in range(16):
                src3 = xfm3[:, b * NCH:(b + 1) * NCH, k * 128:k * 128 + 128]
                tcv = None
                for c in range(NCH):
                    tw = [(rs, rv), ("vc", p.now("vc")), ("yld", p.now("yld"))] if c == 0 else []
                    tcv = p.do("t", lambda t, c=c, src3=src3: t.transpose(
                        pbf[:, c * 128:(c + 1) * 128], src3[:, c, :], self.ident[:, :]),
                        wait=tw, inc="tc" if c == NCH - 1 else None)
                p.do("v", lambda v: v.tensor_copy(xout, pbf[:, :]),
                     wait=(("tc", tcv),), inc="vc")
                p.do("y", lambda y, b=b, k=k: y.dma_start(
                    self.y_out[b, k * 128:(k + 1) * 128, :], xout),
                    wait=(("vc", p.now("vc")),), inc="yld", inc_amt=16)
        p.do("y", lambda y: None, wait=(("yld", p.now("yld")),))

    def emit_ln(self, b, dst, rms=False):
        p = self.p
        sq = [self.SL[:, 0:512], self.SL[:, 512:1024]]
        rs, rv = self.xfm_ready
        r0 = lambda q: self.FSA[0:1, 0:512]
        r1 = lambda q: self.FSA[0:1, 512:1024]
        r2 = lambda q: self.FSA[0:1, 1024:1536]
        r3 = lambda q: self.FSA[0:1, 1536:2048]
        TMPB = self.A[:, 0:2048]
        SRNB = self.A[:, 2048:4096]
        MSRB = self.A[:, 4096:6144]

        for q in range(4):
            n0 = q * 512
            wA = [(rs, rv), ("vc", p.now("vc")), ("sc", p.now("sc")),
                  ("yld", p.now("yld"))]
            sqv = []
            for c in range(2):
                sqv.append(p.do("s", lambda s, c=c, n0=n0: s.activation(
                    sq[c % 2], self.xfm(b, c)[:, n0:n0 + 512], AF.Square),
                    wait=[(rs, rv), ("tc", p.now("tc")), ("yld", p.now("yld")),
                          ("gld", p.now("gld"))] if c == 0 else (), inc="sc"))
            tsum = None
            for c in range(NCH):
                tsum = p.do("t", lambda t, c=c, n0=n0: t.matmul(
                    self.PA[0:1, :], self.ones[:, 0:1], self.xfm(b, c)[:, n0:n0 + 512],
                    start=(c == 0), stop=(c == NCH - 1)),
                    wait=wA if c == 0 else (), inc="tc" if c == NCH - 1 else None)
            tssq = None
            for c in range(NCH):
                tssq = p.do("t", lambda t, c=c: t.matmul(
                    self.PB[0:1, :], self.ones[:, 0:1], sq[c % 2],
                    start=(c == 0), stop=(c == NCH - 1)),
                    wait=[("sc", sqv[c]), ("vc", p.now("vc"))] if c == 0 else
                         [("sc", sqv[c])], inc="tc")
                if c + 2 < NCH:
                    sqv.append(p.do("s", lambda s, c=c, n0=n0: s.activation(
                        sq[c % 2], self.xfm(b, c + 2)[:, n0:n0 + 512], AF.Square),
                        wait=(("tc", tssq),), inc="sc"))
            p.do("v", lambda v, q=q: v.tensor_copy(r0(q), self.PA[0:1, :]),
                 wait=(("tc", p.now("tc")), ("sc", p.now("sc"))), inc="vc")
            p.do("v", lambda v, q=q: v.tensor_copy(r1(q), self.PB[0:1, :]), inc="vc")
            p.do("v", lambda v, q=q: v.tensor_scalar_mul(r2(q), r0(q), 1.0 / DM),
                 inc="vc")
            p.do("v", lambda v, q=q: v.tensor_scalar_mul(r1(q), r1(q), 1.0 / DM),
                 inc="vc")
            p.do("v", lambda v, q=q: v.tensor_mul(r3(q), r2(q), r2(q)), inc="vc")
            p.do("v", lambda v, q=q: v.tensor_sub(r1(q), r1(q), r3(q)), inc="vc")
            vv = p.now("vc")
            sv = p.do("s", lambda s, q=q: s.activation(
                r3(q), r1(q), AF.Sqrt, bias=self.ST[0:1, ST_MS2 + 4:ST_MS2 + 5]),
                wait=(("vc", vv), ("gc", self.init_gc)), inc="sc")
            p.do("v", lambda v, q=q: v.reciprocal(r0(q), r3(q)),
                 wait=(("sc", sv),), inc="vc")
            if rms:
                sv = p.do("s", lambda s, q=q: s.activation(r3(q), r1(q), AF.Sqrt),
                          wait=(("vc", p.now("vc")),), inc="sc")
                p.do("v", lambda v, q=q: v.tensor_mul(r3(q), r3(q), r0(q)),
                     wait=(("sc", sv),), inc="vc")
                p.do("v", lambda v, q=q: v.tensor_scalar(r3(q), r3(q), 32.0, 1e-12,
                                                         ALU.mult, ALU.max), inc="vc")
                p.do("v", lambda v, q=q: v.reciprocal(r1(q), r3(q)), inc="vc")
                p.do("v", lambda v, q=q: v.tensor_mul(r0(q), r0(q), r1(q)), inc="vc")
                p.do("v", lambda v, q=q: v.tensor_scalar_mul(r0(q), r0(q), 32.0),
                     inc="vc")
            p.do("v", lambda v, q=q: v.tensor_mul(r2(q), r2(q), r0(q)), inc="vc")
            p.do("v", lambda v, q=q: v.tensor_copy(self.SL[0:1, 1024:1536], r0(q)),
                 inc="vc")
            p.do("v", lambda v, q=q: v.tensor_copy(self.SL[0:1, 1536:2048], r2(q)),
                 inc="vc")
            vrow = p.now("vc")
            p.do("t", lambda t, q=q: t.matmul(
                self.PA[:, :], self.ones[0:1, 0:128], self.SL[0:1, 1024:1536],
                start=True, stop=True),
                wait=[("vc", vrow)], inc="tc")
            tb = p.do("t", lambda t, q=q: t.matmul(
                self.PB[:, :], self.ones[0:1, 0:128], self.SL[0:1, 1536:2048],
                start=True, stop=True),
                inc="tc")
            p.do("v", lambda v, q=q: v.tensor_copy(SRNB[:, q * 512:(q + 1) * 512],
                                                   self.PA[:, :]),
                 wait=[("tc", tb)] + ([("tc", p.now("tc")), ("sc", p.now("sc"))]
                                      if q == 0 else []), inc="vc")
            p.do("v", lambda v, q=q: v.tensor_copy(MSRB[:, q * 512:(q + 1) * 512],
                                                   self.PB[:, :]), inc="vc")
        vbc = p.now("vc")
        for c in range(NCH):
            p.do("v", lambda v, c=c: v.tensor_mul(TMPB, self.xfm(b, c), SRNB),
                 wait=[("vc", vbc), ("yld", self.init_yld),
                       ("tc", p.now("tc")), ("sc", p.now("sc"))] if c == 0 else (),
                 inc="vc")
            p.do("v", lambda v, c=c: v.tensor_sub(dst(c), TMPB, MSRB), inc="vc")
            if rms:
                p.do("v", lambda v, c=c: v.tensor_scalar_mul(
                    dst(c), dst(c), self.ST[:, ST_GN + c:ST_GN + c + 1]), inc="vc")
        return p.now("vc")

    def emit_l1ffn(self, first):
        p = self.p
        WRT = self.DT[:, 0:2048]
        WIT = self.DT[:, 2048:4096]
        NIT = self.DT[:, 4096:6144]
        if first:
            g1 = self.WB[0:4, 0:2560]
            l1 = p.do("g", lambda g: g.dma_start(g1, self.G1r[:, :]),
                      wait=(("vc", p.now("vc")), ("sc", p.now("sc"))),
                      inc="gld", inc_amt=16)
            fs = self.FSA[:, 0:512]
            wn = self.WB[:, 2560:4608]
            pbf = self.PC[:, :].bitcast(BF16)
            gq_prev = p.now("gld")
            for ri in range(2):
                for ms in range(8):
                    tv = p.do("t", lambda t, ri=ri, ms=ms: t.matmul(
                        self.PA[:, 0:256],
                        g1[:, ri * 1024 + ms * 128:ri * 1024 + (ms + 1) * 128],
                        g1[:, 2048 + ri * 256:2048 + ri * 256 + 256],
                        start=True, stop=True),
                        wait=[("gld", max(l1, gq_prev)), ("vc", p.now("vc")),
                              ("yld", p.now("yld"))], inc="tc")
                    vv = p.do("v", lambda v: v.tensor_copy(fs[:, 0:256], self.PA[:, 0:256]),
                              wait=(("tc", tv), ("yld", p.now("yld"))), inc="vc")
                    p.do("y", lambda y, ms=ms: y.dma_start(
                        self.pdr[ms * 128:(ms + 1) * 128, :], fs[:, 0:256]),
                        wait=(("vc", vv),), inc="yld", inc_amt=16)
                yv = p.now("yld")
                lw = None
                for b32 in range(32):
                    rc = b32 // 8
                    dst = wn[(b32 % 8) * 16:(b32 % 8) * 16 + 16,
                             rc * 512:(rc + 1) * 512].rearrange("p (d g) -> p d g", g=16)
                    src2 = self.pdr[b32 * 32:(b32 + 1) * 32, :].rearrange(
                        "d (e g) -> e d g", g=16)
                    lw = p.do("g", lambda g, dst=dst, src2=src2: g.dma_start(dst, src2),
                              wait=[("yld", yv), ("tc", p.now("tc"))] if b32 == 0 else (),
                              inc="gld", inc_amt=16)
                gq_prev = lw
                dstT = WRT if ri == 0 else WIT
                for grp in range(2):
                    tv = None
                    for j in range(8):
                        kk = grp * 8 + j
                        rc, jc = kk // 4, kk % 4
                        tv = p.do("t", lambda t, rc=rc, jc=jc, j=j: t.transpose(
                            pbf[:, j * 128:(j + 1) * 128],
                            wn[:, rc * 512 + jc * 128:rc * 512 + (jc + 1) * 128],
                            self.ident[:, :]),
                            wait=[("gld", lw), ("vc", p.now("vc"))] if j == 0 else (),
                            inc="tc" if j == 7 else None)
                    for j in range(8):
                        kk = grp * 8 + j
                        rc, jc = kk // 4, kk % 4
                        p.do("v", lambda v, rc=rc, jc=jc, j=j, dstT=dstT: v.tensor_copy(
                            dstT[:, jc * 512 + rc * 128:jc * 512 + (rc + 1) * 128],
                            pbf[:, j * 128:(j + 1) * 128]),
                            wait=[("tc", tv)] if j == 0 else (), inc="vc")
            p.do("v", lambda v: v.tensor_scalar_mul(NIT, WIT, -1.0), inc="vc")
            wv = p.now("vc")
            p.do("y", lambda y: y.dma_start(self.wts[:, :], self.DT[:, :]),
                 wait=(("vc", wv),), inc="yld", inc_amt=16)
            self.l1ffn_w_done = p.now("yld")
            wgate = ("vc", wv)
        else:
            lw = p.do("g", lambda g: g.dma_start(self.DT[:, :], self.wts[:, :]),
                      wait=(("vc", p.now("vc")), ("sc", p.now("sc")),
                            ("yld", self.l1ffn_w_done)),
                      inc="gld", inc_amt=16)
            wgate = ("gld", lw)

        lbr = p.do("g", lambda g: g.dma_start(self.PMRN[0:1, 0:512], self.brr[:, :]),
                   wait=(("tc", p.now("tc")), ("sc", p.now("sc")),
                         ("vc", p.now("vc"))), inc="gld", inc_amt=16)
        lbr = p.do("g", lambda g: g.dma_start(self.PMRN[0:1, 512:1024], self.bir[:, :]),
                   inc="gld", inc_amt=16)
        for b in range(NB):
            lnv = self.emit_ln(b, lambda c: self.Bar[:, c * 2048:(c + 1) * 2048])
            pp = [self.PA, self.PB]
            gate = {0: ("vc", 0), 1: ("vc", 0)}
            for oc in range(NCH):
                real = oc < 4
                om = oc % 4
                for nb4 in range(4):
                    pi = (oc * 4 + nb4) % 2
                    ps = pp[pi]
                    bias_row = (self.PMRN[0:1, 0:512] if real
                                else self.PMRN[0:1, 512:1024])
                    wa, wb_ = (WRT, NIT) if real else (WIT, WRT)
                    for k in range(4):
                        p.do("t", lambda t, k=k, om=om, nb4=nb4, wa=wa, ps=ps: t.matmul(
                            ps[:, :], wa[:, k * 512 + om * 128:k * 512 + (om + 1) * 128],
                            self.Bar[:, k * 2048 + nb4 * 512:k * 2048 + nb4 * 512 + 512],
                            start=(k == 0), stop=False),
                            wait=[("vc", lnv), wgate, gate[pi]] if k == 0 else ())
                    for k in range(4):
                        p.do("t", lambda t, k=k, om=om, nb4=nb4, wb_=wb_, ps=ps: t.matmul(
                            ps[:, :], wb_[:, k * 512 + om * 128:k * 512 + (om + 1) * 128],
                            self.Bar[:, (4 + k) * 2048 + nb4 * 512:(4 + k) * 2048 + nb4 * 512 + 512],
                            start=False, stop=False))
                    tv = p.do("t", lambda t, om=om, ps=ps, bias_row=bias_row: t.matmul(
                        ps[:, :], bias_row[:, om * 128:(om + 1) * 128],
                        self.ones[0:1, 0:512], start=False, stop=True),
                        wait=(("gld", lbr),), inc="tc")
                    xs = self.xfm(b, oc)[:, nb4 * 512:(nb4 + 1) * 512]
                    ev = p.do("v", lambda v, xs=xs, ps=ps: v.scalar_tensor_tensor(
                        xs, ps[:, :], 0.5, xs, ALU.mult, ALU.add),
                        wait=(("tc", tv),), inc="vc")
                    gate[pi] = ("vc", ev)
            self.xfm_ready = ("vc", p.now("vc"))

    def emit_l0attn(self):
        p = self.p
        for b in range(NB):
            lnv = self.emit_ln(b, lambda c: self.Bar[:, c * 2048:(c + 1) * 2048])
            pp = [self.PA, self.PB]
            gate = {0: ("vc", 0), 1: ("sc", 0)}
            slot_gate = [("tc", 0)] * 4
            pcount = 0
            for m in range(8):
                lds = []
                for ti, wsrc in enumerate((self.uT, self.vT)):
                    si = ti * 2 + (m % 2)
                    slot = self.DT[:, si * 1024:(si + 1) * 1024]
                    src = wsrc[:, m * 128:(m + 1) * 128].rearrange(
                        "(k q) c -> q k c", q=128)
                    dst3 = slot.rearrange("q (k c) -> q k c", c=128)
                    ld = p.do("g", lambda g, dst3=dst3, src=src: g.dma_start(dst3, src),
                              wait=[slot_gate[si], ("vc", lnv)], inc="gld", inc_amt=16)
                    lds.append(ld)
                for ti in range(2):
                    si = ti * 2 + (m % 2)
                    slot = self.DT[:, si * 1024:(si + 1) * 1024]
                    for nb4 in range(4):
                        pi = pcount % 2
                        pcount += 1
                        ps = pp[pi]
                        tv = None
                        for k in range(NCH):
                            tv = p.do("t", lambda t, k=k, m=m, nb4=nb4, slot=slot, ps=ps: t.matmul(
                                ps[:, :], slot[:, k * 128:(k + 1) * 128],
                                self.Bar[:, k * 2048 + nb4 * 512:k * 2048 + nb4 * 512 + 512],
                                start=(k == 0), stop=(k == NCH - 1)),
                                wait=[("gld", lds[ti]), ("vc", lnv), gate[pi]] if k == 0 else (),
                                inc="tc" if k == NCH - 1 else None)
                        dst = self.A[:, ti * 16384 + m * 2048 + nb4 * 512:
                                     ti * 16384 + m * 2048 + nb4 * 512 + 512]
                        ev = p.do("s", lambda s, dst=dst, ps=ps: s.mul(dst, ps[:, :], 0.03125),
                                  wait=(("tc", tv),), inc="sc")
                        gate[pi] = ("sc", ev)
                    slot_gate[si] = ("tc", p.now("tc"))
            SKIP = os.environ.get("L0SKIP", "")
            pbf = [self.PC[:, :].bitcast(BF16), self.PD[:, :].bitcast(BF16)]
            vbank = [self.PA, self.PB, self.PC, self.PD]
            qh_row = self.Bar[:, 4096:6144]
            qhT_row = self.Bar[:, 6144:8192]
            bounce = self.Bar[:, 8192:10240]
            zrow = self.SL[0:1, 0:128]
            zv = p.do("g", lambda g: g.memset(zrow, 0.0),
                      wait=(("sc", p.now("sc")), ("tc", p.now("tc"))), inc="gc")
            prev_pair_done = ("vc", p.now("vc"))
            prev_bounce = ("gld", p.now("gld"))
            for pr in range(0 if "heads" not in SKIP else 8, 8):
                tmv = []
                for ti in range(2 if "notm" not in SKIP else 0):
                    for grp in range(2):
                        tv = None
                        for j in range(8):
                            srcap = self.A[:, ti * 16384 + pr * 2048 + (grp * 8 + j) * 128:
                                           ti * 16384 + pr * 2048 + (grp * 8 + j) * 128 + 128]
                            tv = p.do("t", lambda t, j=j, ti=ti, srcap=srcap: t.transpose(
                                pbf[ti][:, j * 128:(j + 1) * 128], srcap, self.ident[:, :]),
                                wait=([("sc", p.now("sc")), ("vc", prev_pair_done[1]),
                                       ("gld", p.now("gld"))] if grp == 0 and ti == 0 else
                                      [("vc", p.now("vc"))]) if j == 0 else (),
                                inc="tc" if j == 7 else None)
                        dst = self.Bar[:, ti * 2048 + grp * 1024:ti * 2048 + (grp + 1) * 1024]
                        ev = p.do("v", lambda v, dst=dst, ti=ti: v.tensor_copy(
                            dst, pbf[ti][:, 0:1024]), wait=(("tc", tv),), inc="vc")
                        tmv.append(ev)
                tm_done = p.now("vc")
                ev_gate = ("vc", tm_done)
                for h in range(2):
                    base = 64 * h
                    for tb in range(0 if "noz" not in SKIP else 4, 4):
                        p.do("t", lambda t, tb=tb: t.matmul(
                            vbank[tb][:, :], zrow, self.ones[0:1, 0:512],
                            start=True, stop=False),
                            wait=[ev_gate, ("gc", zv)] if tb == 0 else ())
                    sg2 = 0
                    for i in range(0 if "norows" not in SKIP else 16, 16):
                        tv = None
                        for tb in range(4):
                            tv = p.do("t", lambda t, i=i, tb=tb, pr=pr, base=base: t.matmul(
                                self.PL[:, tb * 512:(tb + 1) * 512],
                                self.A[base:base + 64, pr * 2048 + i * 128:pr * 2048 + (i + 1) * 128],
                                self.A[base:base + 64,
                                       16384 + pr * 2048 + tb * 512:16384 + pr * 2048 + tb * 512 + 512],
                                start=True, stop=True),
                                wait=([("sc", p.now("sc")), ("vc", tm_done), ev_gate]
                                      if i == 0 else [("sc", sg2)]) if tb == 0 else (),
                                inc="tc" if tb == 3 else None)
                        sg = p.do("s", lambda s: s.activation(qh_row, self.PL[:, :], AF.Sigmoid),
                                  wait=(("tc", tv), ("gld", p.now("gld"))), inc="sc")
                        for tb in range(0 if "nov2" not in SKIP else 4, 4):
                            p.do("t", lambda t, i=i, tb=tb, base=base: t.matmul(
                                vbank[tb][64:128, :],
                                self.Bar[:, i * 128 + base:i * 128 + base + 64],
                                qh_row[:, tb * 512:(tb + 1) * 512],
                                start=False, stop=False, tile_position=(0, 64)),
                                wait=[("sc", sg)] if tb == 0 else ())
                        tv = None
                        for tb in range(4):
                            tv = p.do("t", lambda t, i=i, tb=tb, pr=pr, base=base: t.matmul(
                                self.PL[:, tb * 512:(tb + 1) * 512],
                                self.A[base:base + 64,
                                       16384 + pr * 2048 + i * 128:16384 + pr * 2048 + (i + 1) * 128],
                                self.A[base:base + 64, pr * 2048 + tb * 512:pr * 2048 + tb * 512 + 512],
                                start=True, stop=True),
                                wait=[("sc", sg)] if tb == 0 else (),
                                inc="tc" if tb == 3 else None)
                        sg2 = p.do("s", lambda s: s.activation(qhT_row, self.PL[:, :], AF.Sigmoid),
                                   wait=(("tc", tv),), inc="sc")
                        for sb in range(0 if "nov1" not in SKIP else 4, 4):
                            p.do("t", lambda t, i=i, sb=sb, base=base: t.matmul(
                                vbank[sb][0:64, :],
                                self.Bar[:, 2048 + i * 128 + base:2048 + i * 128 + base + 64],
                                qhT_row[:, sb * 512:(sb + 1) * 512],
                                start=False, stop=False, tile_position=(0, 0)),
                                wait=[("sc", sg2)] if sb == 0 else ())
                    vdone = None
                    for tb in range(0 if "noz" not in SKIP else 4, 4):
                        vdone = p.do("t", lambda t, tb=tb: t.matmul(
                            vbank[tb][:, :], zrow, self.ones[0:1, 0:512],
                            start=False, stop=True),
                            inc="tc" if tb == 3 else None)
                    evs = []
                    for ti, rows0 in (((0, 0), (1, 64)) if "noevac" not in SKIP else ()):
                        direct = (rows0 == base)
                        for tb in range(4):
                            if direct:
                                dst = self.A[base:base + 64,
                                             ti * 16384 + pr * 2048 + tb * 512:
                                             ti * 16384 + pr * 2048 + tb * 512 + 512]
                            else:
                                dst = bounce[rows0:rows0 + 64, tb * 512:(tb + 1) * 512]
                            ev = p.do("v", lambda v, dst=dst, tb=tb, rows0=rows0: v.tensor_copy(
                                dst, vbank[tb][rows0:rows0 + 64, :]),
                                wait=[("tc", vdone), prev_bounce] if (tb == 0 and ti == 0 and vdone) else [prev_bounce] if tb == 0 and ti == 0 else (),
                                inc="vc")
                        if (not direct) and ("nobounce" not in SKIP):
                            bv = p.do("g", lambda g, ti=ti, pr=pr, rows0=rows0, base=base: g.dma_start(
                                self.A[base:base + 64,
                                       ti * 16384 + pr * 2048:ti * 16384 + (pr + 1) * 2048],
                                bounce[rows0:rows0 + 64, :]),
                                wait=(("vc", p.now("vc")),), inc="gld", inc_amt=16)
                            prev_bounce = ("gld", bv)
                    ev_gate = ("vc", p.now("vc"))
                prev_pair_done = ("vc", p.now("vc"))
            gate = {0: ("vc", 0), 1: ("vc", 0)}
            slot_gate = [("tc", 0)] * 4
            pcount = 0
            for dc in range(8):
                lds = []
                for ti, wsrc in enumerate((self.u_n, self.v_n)):
                    si = ti * 2 + (dc % 2)
                    slot = self.DT[:, si * 1024:(si + 1) * 1024]
                    src = wsrc[:, dc * 128:(dc + 1) * 128].rearrange(
                        "(k q) c -> q k c", q=128)
                    dst3 = slot.rearrange("q (k c) -> q k c", c=128)
                    ld = p.do("g", lambda g, dst3=dst3, src=src: g.dma_start(dst3, src),
                              wait=[slot_gate[si], prev_bounce, ("vc", prev_pair_done[1])],
                              inc="gld", inc_amt=16)
                    lds.append(ld)
                for nb4 in range(4):
                    pi = pcount % 2
                    pcount += 1
                    ps = pp[pi]
                    tv = None
                    for ti in range(2):
                        for k in range(NCH):
                            tv = p.do("t", lambda t, k=k, ti=ti, dc=dc, nb4=nb4, ps=ps: t.matmul(
                                ps[:, :],
                                self.DT[:, (ti * 2 + dc % 2) * 1024 + k * 128:
                                        (ti * 2 + dc % 2) * 1024 + (k + 1) * 128],
                                self.A[:, ti * 16384 + k * 2048 + nb4 * 512:
                                       ti * 16384 + k * 2048 + nb4 * 512 + 512],
                                start=(ti == 0 and k == 0), stop=(ti == 1 and k == NCH - 1)),
                                wait=[("gld", lds[1]), gate[pi],
                                      ("vc", prev_pair_done[1])] if ti == 0 and k == 0 else (),
                                inc="tc" if (ti == 1 and k == NCH - 1) else None)
                    xs = self.xfm(b, dc)[:, nb4 * 512:(nb4 + 1) * 512]
                    ev = p.do("v", lambda v, xs=xs, ps=ps: v.scalar_tensor_tensor(
                        xs, ps[:, :], 0.03125, xs, ALU.mult, ALU.add),
                        wait=(("tc", tv),), inc="vc")
                    gate[pi] = ("vc", ev)
                slot_gate[2 * 0 + dc % 2] = ("tc", p.now("tc"))
                slot_gate[2 * 1 + dc % 2] = ("tc", p.now("tc"))
            self.xfm_ready = ("vc", p.now("vc"))

    def emit_l1attn(self):
        p = self.p
        WQK = self.DT[:, 0:1024]
        att = [self.DT[:, 1024:3072], self.DT[:, 3072:5120]]
        nrow = self.SL[0:1, 0:2048]
        stg1 = self.DT[0:16, 5120:5248]
        stg2 = self.DT[0:16, 5248:5376]
        nmx_bf = self.DT[:, 5376:5392]
        rr_bf = self.DT[:, 5392:5408]
        rrb = self.PMRN
        zrow = self.DT[0:1, 5408:5536]
        SKIP1 = os.environ.get("L1SKIP", "")
        for b in range(NB if "bb0" not in SKIP1 else 1):
            zv = p.do("g", lambda g: g.memset(zrow, 0.0),
                      wait=(("sc", p.now("sc")), ("tc", p.now("tc")),
                            ("vc", p.now("vc"))), inc="gc")
            lbv = p.do("g", lambda g: g.dma_start(self.PMRN[0:1, 0:1024],
                                                  self.bvr[:, :]),
                       inc="gld", inc_amt=16)
            lnv = self.emit_ln(b, lambda c: self.Bar[:, c * 2048:(c + 1) * 2048])
            lv = p.do("g", lambda g: g.dma_start(
                self.WB[:, :].rearrange("q (k c) -> q k c", c=1024),
                self.WvT[:, :].rearrange("(k q) c -> q k c", q=128)),
                wait=(("tc", p.now("tc")), ("sc", p.now("sc"))), inc="gld", inc_amt=16)
            pp = [self.PA, self.PB]
            gate = {0: ("vc", 0), 1: ("vc", 0)}
            pcount = 0
            for nt in range(16):
                for oh in range(2):
                    pi = pcount % 2
                    pcount += 1
                    ps = pp[pi]
                    for k in range(NCH):
                        p.do("t", lambda t, k=k, nt=nt, oh=oh, ps=ps: t.matmul(
                            ps[:, :],
                            self.Bar[:, k * 2048 + nt * 128:k * 2048 + (nt + 1) * 128],
                            self.WB[:, k * 1024 + oh * 512:k * 1024 + oh * 512 + 512],
                            start=(k == 0), stop=False),
                            wait=[("vc", lnv), ("gld", lv), gate[pi]] if k == 0 else ())
                    tv = p.do("t", lambda t, oh=oh, ps=ps: t.matmul(
                        ps[:, :], self.ones[0:1, 0:128],
                        self.PMRN[0:1, oh * 512:oh * 512 + 512],
                        start=False, stop=True), wait=(("gld", lbv),), inc="tc")
                    dst = self.A[:, 16384 + nt * 1024 + oh * 512:16384 + nt * 1024 + oh * 512 + 512]
                    ev = p.do("v", lambda v, dst=dst, ps=ps: v.tensor_copy(dst, ps[:, :]),
                              wait=(("tc", tv),), inc="vc")
                    gate[pi] = ("vc", ev)
            vh_done = p.now("vc")
            wq_gate = ("vc", 0)
            for h in range(16):
                base = 64 * (h % 2)
                ld = p.do("g", lambda g, h=h, base=base: g.dma_start(
                    self.SL[base:base + 64, 0:64], self.trilT[h, :, :]),
                    wait=[wq_gate, ("tc", p.now("tc"))], inc="gld", inc_amt=16)
                tv = p.do("t", lambda t, base=base: t.matmul(
                    self.PC[base:base + 64, 0:64], self.SL[base:base + 64, 0:64],
                    self.SL[base:base + 64, 0:64], start=True, stop=True,
                    tile_position=(base, base)),
                    wait=(("gld", ld), ("vc", p.now("vc"))), inc="tc")
                ev = p.do("v", lambda v, h=h, base=base: v.tensor_copy(
                    WQK[base:base + 64, h * 64:(h + 1) * 64],
                    self.PC[base:base + 64, 0:64]),
                    wait=(("tc", tv),), inc="vc")
                wq_gate = ("vc", ev)
            gate = {0: ("sc", 0), 1: ("sc", 0)}
            pcount = 0
            for pr in range(8):
                for nb4 in range(4):
                    pi = pcount % 2
                    pcount += 1
                    ps0, ps1 = (self.PA, self.PB) if pi == 0 else (self.PC, self.PD)
                    tv = None
                    for par in range(2):
                        h = 2 * pr + par
                        base = 64 * par
                        psx = ps0 if par == 0 else ps1
                        tv = p.do("t", lambda t, h=h, base=base, pr=pr, nb4=nb4, psx=psx: t.matmul(
                            psx[base:base + 64, :], WQK[base:base + 64, h * 64:(h + 1) * 64],
                            self.Bar[base:base + 64, pr * 2048 + nb4 * 512:pr * 2048 + nb4 * 512 + 512],
                            start=True, stop=True, tile_position=(base, base)),
                            wait=[("vc", wq_gate[1]), gate[pi]] if par == 0 else (),
                            inc="tc" if par == 1 else None)
                    for par in range(2):
                        base = 64 * par
                        psx = ps0 if par == 0 else ps1
                        dst = self.A[base:base + 64,
                                     pr * 2048 + nb4 * 512:pr * 2048 + nb4 * 512 + 512]
                        ev = p.do("s", lambda s, dst=dst, psx=psx, base=base: s.mul(
                            dst, psx[base:base + 64, :], 0.125),
                            wait=(("tc", tv),) if par == 0 else (), inc="sc")
                    gate[pi] = ("sc", ev)
            lo = p.do("g", lambda g: g.dma_start(
                self.WB[:, :].rearrange("q (k c) -> q k c", c=1024),
                self.WoT[:, :].rearrange("(k q) c -> q k c", q=128)),
                wait=(("vc", vh_done), ("tc", p.now("tc"))), inc="gld", inc_amt=16)
            pvbank = [self.PA, self.PB, self.PC, self.PD]
            prev_evac = ("vc", 0)
            plbf = self.PL[:, :].bitcast(BF16)
            for h in range(0 if "heads" not in SKIP1 else 16, 1 if "h1" in SKIP1 else 16):
                pr, par = h // 2, h % 2
                base = 64 * par
                sgp = 0
                for i in range(16):
                    tv = None
                    for tb in range(4):
                        tv = p.do("t", lambda t, i=i, tb=tb, pr=pr, base=base: t.matmul(
                            self.PL[:, tb * 512:(tb + 1) * 512],
                            self.A[base:base + 64, pr * 2048 + i * 128:pr * 2048 + (i + 1) * 128],
                            self.Bar[base:base + 64, pr * 2048 + tb * 512:pr * 2048 + tb * 512 + 512],
                            start=True, stop=True),
                            wait=[("sc", p.now("sc")), ("vc", p.now("vc"))] if (i == 0 and tb == 0)
                                 else ([("sc", sgp)] if tb == 0 else ()),
                            inc="tc" if tb == 3 else None)
                    rv = p.do("v", lambda v: v.reduce_max(
                        self.ST[:, ST_MS1:ST_MS1 + 1], self.PL[:, :], axis=AX.X,
                        negate=True), wait=(("tc", tv),), inc="vc")
                    rv = p.do("v", lambda v, i=i: v.tensor_copy(
                        nmx_bf[:, i:i + 1], self.ST[:, ST_MS1:ST_MS1 + 1]),
                        wait=(("vc", rv),), inc="vc")
                    rv = p.do("v", lambda v, i=i: v.tensor_copy(
                        self.ST[:, ST_NMX + i:ST_NMX + i + 1], nmx_bf[:, i:i + 1]),
                        wait=(("vc", rv),), inc="vc")
                    sgp = p.do("s", lambda s, i=i: s.activation(
                        att[0], self.PL[:, :], AF.Exp,
                        bias=self.ST[:, ST_NMX + i:ST_NMX + i + 1],
                        accum_out=self.ST[:, ST_RS + i:ST_RS + i + 1]),
                        wait=(("vc", rv),), inc="sc")
                tv = p.do("t", lambda t: t.transpose(
                    plbf[0:16, 0:128], nmx_bf[:, 0:16], self.ident[:, :]),
                    wait=(("sc", sgp),), inc="tc")
                fv = p.do("v", lambda v: v.tensor_copy(stg1, plbf[0:16, 0:128]),
                          wait=(("tc", tv),), inc="vc")
                yv = p.do("y", lambda y: y.dma_start(
                    self.bncb[0:1, :].rearrange("x (a c) -> (x a) c", c=128), stg1),
                    wait=(("vc", fv), ("yld", p.now("yld"))), inc="yld", inc_amt=16)
                yv = p.do("y", lambda y: y.dma_start(
                    nrow, self.bncb[0:1, :]), wait=(("yld", yv),), inc="yld", inc_amt=16)
                rv = p.do("v", lambda v: v.reciprocal(
                    self.ST[:, ST_RR:ST_RR + 16], self.ST[:, ST_RS:ST_RS + 16]),
                    wait=(("sc", sgp),), inc="vc")
                rv = p.do("v", lambda v: v.tensor_copy(
                    rr_bf[:, 0:16], self.ST[:, ST_RR:ST_RR + 16]),
                    wait=(("vc", rv),), inc="vc")
                tv = p.do("t", lambda t: t.transpose(
                    plbf[0:16, 128:256], rr_bf[:, 0:16], self.ident[:, :]),
                    wait=(("vc", rv),), inc="tc")
                fv = p.do("v", lambda v: v.tensor_copy(stg2, plbf[0:16, 128:256]),
                          wait=(("tc", tv),), inc="vc")
                yv2 = p.do("y", lambda y: y.dma_start(
                    self.bncb[0:1, :].rearrange("x (a c) -> (x a) c", c=128), stg2),
                    wait=(("vc", fv), ("yld", yv)), inc="yld", inc_amt=16)
                yv2 = p.do("g", lambda g, base=base: g.dma_start(
                    rrb[base:base + 64, :],
                    bass.AP(self.bncb.tensor, 0, [[0, 64], [1, 2048]])),
                    wait=(("yld", yv2),
                          ("vc", prev_evac[1] if h else p.now("vc"))),
                    inc="gld", inc_amt=16)
                rrb_done = yv2
                if par == 0:
                    for sb in range(4):
                        p.do("t", lambda t, sb=sb: t.matmul(
                            pvbank[sb][:, :], zrow, self.ones[0:1, 0:512],
                            start=True, stop=False),
                            wait=[prev_evac, ("gc", zv)] if sb == 0 else ())
                sg2 = 0
                pvlast = None
                for j in range(16):
                    tv = None
                    for sb in range(4):
                        p.do("t", lambda t, j=j, sb=sb, pr=pr, base=base: t.matmul(
                            self.PL[:, sb * 512:(sb + 1) * 512],
                            self.Bar[base:base + 64, pr * 2048 + j * 128:pr * 2048 + (j + 1) * 128],
                            self.A[base:base + 64, pr * 2048 + sb * 512:pr * 2048 + sb * 512 + 512],
                            start=True, stop=False),
                            wait=[("yld", p.now("yld")), ("sc", p.now("sc"))]
                                 if (sb == 0 and j == 0) else
                                 ([("sc", sg2)] if sb == 0 else ()))
                        tv = p.do("t", lambda t, sb=sb: t.matmul(
                            self.PL[:, sb * 512:(sb + 1) * 512], self.ones[0:1, 0:128],
                            nrow[0:1, sb * 512:sb * 512 + 512],
                            start=False, stop=True), inc="tc" if sb == 3 else None)
                    sg2 = p.do("s", lambda s, j=j: s.activation(
                        att[j % 2], self.PL[:, :], AF.Exp),
                        wait=(("tc", tv),), inc="sc")
                    for sb in range(4):
                        pvlast = p.do("t", lambda t, j=j, sb=sb, par=par, h=h: t.matmul(
                            pvbank[sb][64 * par:64 * par + 64, :],
                            self.A[:, 16384 + j * 1024 + h * 64:16384 + j * 1024 + (h + 1) * 64],
                            att[j % 2][:, sb * 512:(sb + 1) * 512],
                            start=False, stop=False, tile_position=(0, 64 * par)),
                            wait=[("sc", sg2)] if sb == 0 else ())
                if par == 1:
                    pvlast = None
                    for sb in range(4):
                        pvlast = p.do("t", lambda t, sb=sb: t.matmul(
                            pvbank[sb][:, :], zrow, self.ones[0:1, 0:512],
                            start=False, stop=True),
                            inc="tc" if sb == 3 else None)
                    evs = None
                    for sb in range(4):
                        dst = self.A[:, pr * 2048 + sb * 512:pr * 2048 + sb * 512 + 512]
                        evs = p.do("v", lambda v, dst=dst, sb=sb: v.tensor_tensor(
                            dst, pvbank[sb][:, :], rrb[:, sb * 512:(sb + 1) * 512],
                            ALU.mult),
                            wait=[("tc", pvlast), ("gld", rrb_done)] if sb == 0 else (),
                            inc="vc")
                    prev_evac = ("vc", evs)
            lbo = p.do("g", lambda g: g.dma_start(self.PMRN[0:1, 0:1024],
                                                  self.bor[:, :]),
                       wait=(("vc", p.now("vc")),), inc="gld", inc_amt=16)
            gate = {0: ("vc", 0), 1: ("vc", 0)}
            pcount = 0
            for dc in range(8):
                for nb4 in range(4):
                    pi = pcount % 2
                    pcount += 1
                    ps = pp[pi]
                    for k in range(NCH):
                        p.do("t", lambda t, k=k, dc=dc, nb4=nb4, ps=ps: t.matmul(
                            ps[:, :], self.WB[:, k * 1024 + dc * 128:k * 1024 + (dc + 1) * 128],
                            self.A[:, k * 2048 + nb4 * 512:k * 2048 + nb4 * 512 + 512],
                            start=(k == 0), stop=False),
                            wait=[("gld", lo), ("vc", p.now("vc")), gate[pi]] if k == 0 else ())
                    tv = p.do("t", lambda t, dc=dc, ps=ps: t.matmul(
                        ps[:, :], self.PMRN[0:1, dc * 128:(dc + 1) * 128],
                        self.ones[0:1, 0:512], start=False, stop=True),
                        wait=(("gld", lbo),), inc="tc")
                    xs = self.xfm(b, dc)[:, nb4 * 512:(nb4 + 1) * 512]
                    ev = p.do("v", lambda v, xs=xs, ps=ps: v.tensor_add(xs, ps[:, :], xs),
                              wait=(("tc", tv),), inc="vc")
                    gate[pi] = ("vc", ev)
            self.xfm_ready = ("vc", p.now("vc"))

    def emit_l0ffn(self):
        p = self.p
        for b in range(NB):
            lnv = self.emit_ln(b, lambda c: self.Bar[:, c * 2048:(c + 1) * 2048], rms=True)
            p.do("g", lambda g: g.memset(self.PMRN[:, :], -1e30),
                 wait=(("vc", p.now("vc")),), inc="gc")
            pmg = p.now("gc")
            se_b = self.SL[:, 0:1024]
            sn_b = self.SL[:, 1024:2048]
            snf = self.DT[:, 2048:3072]
            pbf = self.PC[:, :].bitcast(BF16)
            ev_gate = ("vc", 0)
            sp_gate = ("yld", 0)
            st_gate = ("sc", 0)
            for mt in range(32):
                ld = p.do("g", lambda g, mt=mt: g.dma_start(
                    se_b, self.se[mt * 128:(mt + 1) * 128, :]),
                    wait=(("tc", p.now("tc")),), inc="gld", inc_amt=16)
                sv = p.do("v", lambda v: v.scalar_tensor_tensor(
                    sn_b, se_b, 1.0, se_b, ALU.mult, ALU.mult,
                    accum_out=self.ST[:, ST_MS1:ST_MS1 + 1]),
                    wait=(("gld", ld), ("yld", self.init_yld)), inc="vc")
                sq = p.do("s", lambda s: s.activation(
                    self.ST[:, ST_MS1 + 1:ST_MS1 + 2], self.ST[:, ST_MS1:ST_MS1 + 1],
                    AF.Sqrt), wait=(("vc", sv),), inc="sc")
                sv = p.do("v", lambda v: v.tensor_scalar_max(
                    self.ST[:, ST_MS1 + 1:ST_MS1 + 2], self.ST[:, ST_MS1 + 1:ST_MS1 + 2],
                    1e-12), wait=(("sc", sq),), inc="vc")
                sv = p.do("v", lambda v: v.reciprocal(
                    self.ST[:, ST_MS1 + 2:ST_MS1 + 3], self.ST[:, ST_MS1 + 1:ST_MS1 + 2]),
                    wait=(("vc", sv),), inc="vc")
                sv = p.do("v", lambda v: v.tensor_scalar(
                    sn_b, se_b, self.ST[:, ST_MS1 + 2:ST_MS1 + 3], 32.0,
                    ALU.mult, ALU.mult), wait=(("vc", sv),), inc="vc")
                tv = None
                for c in range(NCH):
                    tv = p.do("t", lambda t, c=c: t.transpose(
                        pbf[:, c * 128:(c + 1) * 128], sn_b[:, c * 128:(c + 1) * 128],
                        self.ident[:, :]),
                        wait=[("vc", p.now("vc")), ("gc", self.init_gc),
                              st_gate] if c == 0 else (),
                        inc="tc" if c == NCH - 1 else None)
                snv = None
                for c in range(NCH):
                    snv = p.do("v", lambda v, c=c: v.tensor_scalar_mul(
                        snf[:, c * 128:(c + 1) * 128], pbf[:, c * 128:(c + 1) * 128],
                        self.ST[:, ST_GS + c:ST_GS + c + 1]),
                        wait=[("tc", tv)] if c == 0 else (), inc="vc")
                tv = None
                for nb4 in range(4):
                    for k in range(NCH):
                        tv = p.do("t", lambda t, k=k, nb4=nb4: t.matmul(
                            self.PL[:, nb4 * 512:(nb4 + 1) * 512],
                            snf[:, k * 128:(k + 1) * 128],
                            self.Bar[:, k * 2048 + nb4 * 512:k * 2048 + nb4 * 512 + 512],
                            start=(k == 0), stop=(k == NCH - 1)),
                            wait=[("vc", max(snv, lnv)), ev_gate,
                                  st_gate] if k == 0 and nb4 == 0 else (),
                            inc="tc" if (k == NCH - 1 and nb4 == 3) else None)
                rv = p.do("v", lambda v, mt=mt: v.reduce_max(
                    self.ST[:, ST_MS1 + 3:ST_MS1 + 4], self.PL[:, :], axis=AX.X,
                    negate=True), wait=(("tc", tv), ("gc", pmg)), inc="vc")
                p.do("v", lambda v, mt=mt: v.tensor_scalar_mul(
                    self.ST[:, ST_RM + mt:ST_RM + mt + 1],
                    self.ST[:, ST_MS1 + 3:ST_MS1 + 4], -1.0),
                    wait=(("vc", rv),), inc="vc")
                pmv = p.do("v", lambda v: v.tensor_max(
                    self.PMRN[:, :], self.PMRN[:, :], self.PL[:, :]), inc="vc")
                sx = p.do("s", lambda s, mt=mt: s.activation(
                    self.DT[:, 0:2048], self.PL[:, :], AF.Exp,
                    bias=self.ST[:, ST_MS1 + 3:ST_MS1 + 4],
                    accum_out=self.ST[:, ST_CS + mt:ST_CS + mt + 1]),
                    wait=(("vc", rv),), inc="sc")
                sx = p.do("s", lambda s: s.activation(self.FSA[:, :], self.PL[:, :],
                                                      AF.Copy),
                          wait=(sp_gate,), inc="sc")
                st_gate = ("sc", sx)
                yv = p.do("y", lambda y, b=b, mt=mt: y.dma_start(
                    self.Lsp[b, mt, :, :], self.FSA[:, :]),
                    wait=(("sc", sx),), inc="yld", inc_amt=16)
                sp_gate = ("yld", yv)
                ev_gate = ("vc", pmv)
            sv = p.do("s", lambda s: s.activation(
                self.ST[:, ST_NAM:ST_NAM + 32], self.ST[:, ST_CS:ST_CS + 32], AF.Ln),
                wait=(("sc", p.now("sc")),), inc="sc")
            vv = p.do("v", lambda v: v.tensor_add(
                self.ST[:, ST_NAM:ST_NAM + 32], self.ST[:, ST_NAM:ST_NAM + 32],
                self.ST[:, ST_RM:ST_RM + 32]), wait=(("sc", sv),), inc="vc")
            vv = p.do("v", lambda v: v.tensor_scalar_mul(
                self.ST[:, ST_NAM:ST_NAM + 32], self.ST[:, ST_NAM:ST_NAM + 32], -1.0),
                wait=(("vc", vv),), inc="vc")
            nam_done = vv
            rnv = None
            for grp in range(4):
                tv = None
                for j in range(4):
                    k = grp * 4 + j
                    tv = p.do("t", lambda t, k=k, j=j: t.transpose(
                        pbf[:, j * 128:(j + 1) * 128],
                        self.PMRN[:, k * 128:(k + 1) * 128], self.ident[:, :]),
                        wait=[("vc", p.now("vc"))] if j == 0 else (),
                        inc="tc" if j == 3 else None)
                rnv = p.do("v", lambda v, grp=grp: v.reduce_max(
                    self.ST[:, ST_RN + grp * 4:ST_RN + (grp + 1) * 4],
                    pbf[:, 0:512].rearrange("p (a c) -> p a c", a=4), axis=AX.X),
                    wait=(("tc", tv),), inc="vc")
            cv = p.do("v", lambda v: v.tensor_copy(
                self.SL[:, 0:16], self.ST[:, ST_RN:ST_RN + 16]),
                wait=(("vc", rnv),), inc="vc")
            tv = p.do("t", lambda t: t.transpose(
                self.PL[:, :].bitcast(BF16)[0:16, 0:128], self.SL[:, 0:16],
                self.ident[:, :]), wait=(("vc", cv), st_gate), inc="tc")
            cv = p.do("v", lambda v: v.tensor_copy(
                self.SL[0:16, 128:256], self.PL[:, :].bitcast(BF16)[0:16, 0:128]),
                wait=(("tc", tv),), inc="vc")
            yv = p.do("y", lambda y: y.dma_start(
                self.bncb[0:1, :].rearrange("x (a c) -> (x a) c", c=128),
                self.SL[0:16, 128:256]), wait=(("vc", cv), sp_gate), inc="yld", inc_amt=16)
            bv = p.do("g", lambda g: g.dma_start(
                self.PMRN[:, :], bass.AP(self.bncb.tensor, 0, [[0, 128], [1, 2048]])),
                wait=(("yld", yv), ("vc", p.now("vc"))), inc="gld", inc_amt=16)
            rn_b_done = bv
            for k in range(16):
                tv = None
                for c in range(NCH):
                    tv = p.do("t", lambda t, k=k, c=c: t.transpose(
                        pbf[:, c * 128:(c + 1) * 128],
                        self.Bar[:, c * 2048 + k * 128:c * 2048 + (k + 1) * 128],
                        self.ident[:, :]),
                        wait=[("vc", p.now("vc")), ("tc", p.now("tc"))] if c == 0 else (),
                        inc="tc" if c == NCH - 1 else None)
                p.do("v", lambda v, k=k: v.tensor_copy(
                    self.A[:, k * 1024:(k + 1) * 1024], pbf[:, 0:1024]),
                    wait=(("tc", tv),), inc="vc")
            xntm_done = p.now("vc")
            dsl = [self.DT[:, 0:2048], self.DT[:, 2048:4096]]
            wslot = [self.WB[:, 0:4096], self.WB[:, 4096:8192]]
            dtv_gate = ("tc", 0)
            wgate = [("tc", 0), ("tc", 0)]
            eow = self.Bar[:, 12288:13312]
            eow_gate = ("yld", 0)
            for e in range(NE):
                lb = p.do("g", lambda g, e=e: g.dma_start(
                    self.ST[:, ST_B1:ST_B1 + 32], self.b1c[e, :, :]),
                    wait=(("sc", p.now("sc")),), inc="gld", inc_amt=16)
                lb2 = p.do("g", lambda g, e=e: g.dma_start(
                    self.SL[0:1, 0:1024], self.b2r[e, :, :]), inc="gld", inc_amt=16)
                for ms in range(4):
                    mt = 4 * e + ms
                    lv = p.do("y", lambda y, b=b, mt=mt: y.dma_start(
                        self.FSA[:, :], self.Lsp[b, mt, :, :]),
                        wait=(st_gate, ("sc", p.now("sc"))), inc="yld", inc_amt=16)
                    dx = p.do("s", lambda s, ms=ms, mt=mt: s.activation(
                        dsl[ms % 2], self.FSA[:, :], AF.Exp,
                        bias=self.ST[:, ST_NAM + mt:ST_NAM + mt + 1]),
                        wait=(("yld", lv), ("vc", nam_done), dtv_gate), inc="sc")
                    st_gate = ("sc", dx)
                    for grp in range(2):
                        tv = None
                        for j in range(8):
                            k = grp * 8 + j
                            tv = p.do("t", lambda t, ms=ms, k=k, j=j: t.transpose(
                                pbf[:, j * 128:(j + 1) * 128],
                                dsl[ms % 2][:, k * 128:(k + 1) * 128], self.ident[:, :]),
                                wait=[("sc", dx), ("vc", p.now("vc"))] if j == 0 else (),
                                inc="tc" if j == 7 else None)
                        dtv_gate = ("tc", tv)
                        src3 = pbf[:, 0:1024].rearrange("p (k c) -> p k c", c=128)
                        dd = self.Bar[:, :].rearrange(
                            "p (k c) -> p k c", c=512)[:, grp * 8:(grp + 1) * 8,
                                                       ms * 128:(ms + 1) * 128]
                        p.do("v", lambda v, dd=dd, src3=src3: v.tensor_copy(dd, src3),
                             wait=(("tc", tv),), inc="vc")
                dtr_done = p.now("vc")
                sgate = ("sc", 0)
                for ds in range(NCH):
                    ps = [self.PA, self.PB][ds % 2]
                    tv = None
                    for k in range(16):
                        tv = p.do("t", lambda t, ds=ds, k=k, ps=ps: t.matmul(
                            ps[:, :], self.A[:, k * 1024 + ds * 128:k * 1024 + (ds + 1) * 128],
                            self.Bar[:, :].rearrange("p (k c) -> p k c", c=512)[:, k, :],
                            start=(k == 0), stop=(k == 15)),
                            wait=[("vc", dtr_done), sgate] if k == 0 else (),
                            inc="tc" if k == 15 else None)
                    sv = p.do("v", lambda v, ds=ds, ps=ps: v.tensor_copy(
                        self.Bar[:, 8192 + ds * 512:8192 + (ds + 1) * 512], ps[:, :]),
                        wait=(("tc", tv),), inc="vc")
                    sgate = ("vc", sv)
                slots_done = p.now("vc")
                hgate = ("sc", 0)
                for q8 in range(8):
                    wl = p.do("g", lambda g, e=e, q8=q8: g.dma_start(
                        wslot[q8 % 2].rearrange("q (k c) -> q k c", c=512),
                        self.w1[e, :, q8 * 512:(q8 + 1) * 512].rearrange(
                            "(k q) c -> q k c", q=128)),
                        wait=(wgate[q8 % 2],), inc="gld", inc_amt=16)
                    for ht in range(4):
                        ps = [self.PA, self.PB][ht % 2]
                        tv = None
                        for k in range(NCH):
                            tv = p.do("t", lambda t, q8=q8, ht=ht, k=k, ps=ps: t.matmul(
                                ps[:, :],
                                wslot[q8 % 2][:, k * 512 + ht * 128:k * 512 + (ht + 1) * 128],
                                self.Bar[:, 8192 + k * 512:8192 + (k + 1) * 512],
                                start=(k == 0), stop=(k == NCH - 1)),
                                wait=[("gld", wl), ("vc", slots_done),
                                      hgate] if k == 0 else (),
                                inc="tc" if k == NCH - 1 else None)
                        hv = p.do("s", lambda s, q8=q8, ht=ht, ps=ps: s.activation(
                            self.A[:, 16384 + (q8 * 4 + ht) * 512:
                                   16384 + (q8 * 4 + ht + 1) * 512],
                            ps[:, :], AF.Gelu,
                            bias=self.ST[:, ST_B1 + q8 * 4 + ht:ST_B1 + q8 * 4 + ht + 1]),
                            wait=(("tc", tv), ("gld", lb)), inc="sc")
                        hgate = ("sc", hv)
                    wgate[q8 % 2] = ("tc", p.now("tc"))
                h_done = p.now("sc")
                eops = [self.PL[:, 0:512], self.PL[:, 512:1024],
                        self.PL[:, 1024:1536], self.PL[:, 1536:2048],
                        self.PA[:, :], self.PB[:, :], self.PC[:, :], self.PD[:, :]]
                for hq in range(8):
                    wl = p.do("g", lambda g, e=e, hq=hq: g.dma_start(
                        wslot[hq % 2].rearrange("q (k c) -> q k c", c=1024),
                        self.w2[e, hq * 512:(hq + 1) * 512, :].rearrange(
                            "(k q) c -> q k c", q=128)),
                        wait=(wgate[hq % 2],), inc="gld", inc_amt=16)
                    for hc in range(4):
                        kk = hq * 4 + hc
                        for gi in range(8):
                            msub, dh = gi // 2, gi % 2
                            last = (hc == 3 and gi == 7)
                            w_ = ()
                            if hc == 0 and gi == 0:
                                w_ = [("gld", wl)] + (
                                    [("sc", h_done), ("vc", p.now("vc"))] if hq == 0 else [])
                            tq = p.do("t", lambda t, kk=kk, hc=hc, hq=hq, msub=msub, dh=dh, gi=gi: t.matmul(
                                eops[gi],
                                self.A[:, 16384 + kk * 512 + msub * 128:
                                       16384 + kk * 512 + (msub + 1) * 128],
                                wslot[hq % 2][:, hc * 1024 + dh * 512:hc * 1024 + (dh + 1) * 512],
                                start=(kk == 0), stop=False),
                                wait=w_, inc="tc" if last else None)
                    wgate[hq % 2] = ("tc", tq)
                tvb = None
                for gi in range(8):
                    msub, dh = gi // 2, gi % 2
                    tvb = p.do("t", lambda t, msub=msub, dh=dh, gi=gi: t.matmul(
                        eops[gi], self.ones[0:1, 0:128],
                        self.SL[0:1, dh * 512:(dh + 1) * 512],
                        start=False, stop=True), inc="tc" if gi == 7 else None)

                for msub in range(4):
                    mt = 4 * e + msub
                    ev = None
                    for dh in range(2):
                        ev = p.do("v", lambda v, msub=msub, dh=dh: v.tensor_copy(
                            eow[:, dh * 512:(dh + 1) * 512], eops[msub * 2 + dh]),
                            wait=[("tc", tvb), eow_gate, ("gld", lb2)] if dh == 0 else (),
                            inc="vc")
                    yv = p.do("y", lambda y, b=b, mt=mt: y.dma_start(
                        self.eod[b, mt * 128:(mt + 1) * 128, :], eow[:, :]),
                        wait=(("vc", ev),), inc="yld", inc_amt=16)
                    eow_gate = ("yld", yv)
            craw = self.Bar
            rrb = self.SL[:, 0:512]
            el = None
            for mt in range(32):
                el = p.do("g", lambda g, b=b, mt=mt: g.dma_start(
                    self.A[:, mt * 1024:(mt + 1) * 1024],
                    self.eod[b, mt * 128:(mt + 1) * 128, :]),
                    wait=[("tc", p.now("tc")), ("sc", p.now("sc")), eow_gate,
                          ("vc", p.now("vc"))] if mt == 0 else (),
                    inc="gld", inc_amt=16)
            eo_done = el
            lgate = ("vc", 0)
            for nb4 in range(4):
                crv = ("sc", 0)
                lgate = ("vc", p.now("vc")) if nb4 else lgate
                for mt in range(32):
                    fslot = self.FSA[:, (mt % 2) * 512:(mt % 2) * 512 + 512]
                    lv = p.do("y", lambda y, b=b, mt=mt, nb4=nb4, fslot=fslot: y.dma_start(
                        fslot, self.Lsp[b, mt, :, nb4 * 512:(nb4 + 1) * 512]),
                        wait=(lgate, eow_gate, crv), inc="yld", inc_amt=16)
                    sv = p.do("v", lambda v, fslot=fslot, nb4=nb4: v.tensor_sub(
                        fslot, fslot, self.PMRN[:, nb4 * 512:(nb4 + 1) * 512]),
                        wait=(("yld", lv), ("gld", rn_b_done)), inc="vc")
                    lgate = ("vc", sv)
                    cx = p.do("s", lambda s, mt=mt, fslot=fslot: s.activation(
                        craw[:, mt * 512:(mt + 1) * 512], fslot, AF.Exp),
                        wait=(("vc", sv),), inc="sc")
                    crv = ("sc", cx)
                cr_done = p.now("sc")
                tv = None
                for mt in range(32):
                    tv = p.do("t", lambda t, mt=mt: t.matmul(
                        self.PA[0:1, :], self.ones[:, 0:1],
                        craw[:, mt * 512:(mt + 1) * 512],
                        start=(mt == 0), stop=(mt == 31)),
                        wait=[("sc", cr_done), ("vc", p.now("vc"))] if mt == 0 else (),
                        inc="tc" if mt == 31 else None)
                rv = p.do("v", lambda v: v.reciprocal(self.FSA[0:1, 1024:1536],
                                                      self.PA[0:1, :]),
                          wait=(("tc", tv),), inc="vc")
                rv = p.do("v", lambda v: v.tensor_copy(self.SL[0:1, 512:1024],
                                                       self.FSA[0:1, 1024:1536]),
                          wait=(("vc", rv),), inc="vc")
                tv = p.do("t", lambda t: t.matmul(
                    self.PB[:, :], self.ones[0:1, 0:128], self.SL[0:1, 512:1024],
                    start=True, stop=True), wait=(("vc", rv),), inc="tc")
                rv = p.do("v", lambda v: v.tensor_copy(rrb, self.PB[:, :]),
                          wait=(("tc", tv),), inc="vc")
                rrb_done = rv
                dgate = {0: ("vc", 0), 1: ("vc", 0)}
                for dc in range(NCH):
                    pi = dc % 2
                    ps = [self.PC, self.PD][pi]
                    tv = None
                    for mt in range(32):
                        tv = p.do("t", lambda t, dc=dc, mt=mt, ps=ps: t.matmul(
                            ps[:, :], self.A[:, mt * 1024 + dc * 128:mt * 1024 + (dc + 1) * 128],
                            craw[:, mt * 512:(mt + 1) * 512],
                            start=(mt == 0), stop=(mt == 31)),
                            wait=[("gld", eo_done), ("sc", cr_done),
                                  dgate[pi]] if mt == 0 else (),
                            inc="tc" if mt == 31 else None)
                    fs2 = self.FSA[:, (dc % 2) * 512:(dc % 2) * 512 + 512]
                    ev = p.do("v", lambda v, fs2=fs2, ps=ps: v.tensor_mul(fs2, ps[:, :], rrb),
                              wait=(("tc", tv), ("vc", rrb_done)), inc="vc")
                    xs = self.xfm(b, dc)[:, nb4 * 512:(nb4 + 1) * 512]
                    ev = p.do("v", lambda v, xs=xs, fs2=fs2: v.scalar_tensor_tensor(
                        xs, fs2, 0.5, xs, ALU.mult, ALU.add), inc="vc")
                    dgate[pi] = ("vc", ev)
            self.xfm_ready = ("vc", p.now("vc"))


def build_program(parts):
    nc = bass.Bass("TRN2", target_bir_lowering=False)
    ctx = ExitStack()
    with ctx:
        k = Kern(nc, ctx, parts)
        k.declare_io()
        k.alloc()
        block = ctx.enter_context(nc.Block())
        k.emit_init()
        k.emit_load_x()
        for part in parts:
            if part == "l0ffn":
                k.emit_l0ffn()
            elif part == "l0attn":
                k.emit_l0attn()
            elif part == "l1attn":
                k.emit_l1attn()
            elif part == "l1ffn_a":
                k.emit_l1ffn(first=True)
            elif part == "l1ffn_b":
                k.emit_l1ffn(first=False)
        dbg = os.environ.get("KDBG", "")
        if dbg:
            p = k.p
            srcs = {"A": k.A, "Bar": k.Bar, "PMRN": k.PMRN, "DT": k.DT,
                    "SL": k.SL, "WB": k.WB, "XFM": k.XFM, "ST": k.ST}[dbg.split(":")[0]]
            off = int(dbg.split(":")[1]) if ":" in dbg else 0
            nmax = srcs.shape[1]
            w = min(16384, nmax - off)
            cw = min(w, 1024)
            dst0 = k.y_out[0, :, :].rearrange("(a p) c -> p a c", p=128)[:, 0:max(w // cw, 1), 0:cw]
            src3 = srcs[:, off:off + w].rearrange("p (a c) -> p a c", c=cw)
            p.do("g", lambda g: g.dma_start(dst0, src3),
                 wait=(("vc", p.now("vc")), ("sc", p.now("sc")),
                       ("tc", p.now("tc"))), inc="gld", inc_amt=16)
            p.do("g", lambda g: None, wait=(("gld", p.now("gld")),))
        if not dbg:
            k.emit_store_y()
        k.p.build(block)
    return nc


LAUNCH_PLAN = [
    ["l0ffn", "l0attn", "l0ffn", "l1ffn_a", "l1attn", "l1ffn_b"],
]


def _norm_parts(parts):
    return ["l1ffn" if p.startswith("l1ffn") else p for p in parts]


def prep_inputs(inp):
    f = np.ascontiguousarray
    d = {}
    d["w1"] = f(inp["L0_ffn_w1"])
    d["w2"] = f(inp["L0_ffn_w2"])
    d["se"] = f(inp["L0_ffn_slot_embeds"].reshape(4096, 1024))
    d["gnc"] = f(inp["L0_ffn_norm_gamma"].reshape(8, 128).T)
    d["gsc"] = f(inp["L0_ffn_slot_norm_gamma"].reshape(8, 128).T)
    d["b1c"] = f(inp["L0_ffn_b1"].reshape(8, 32, 128).transpose(0, 2, 1))
    d["b2r"] = f(inp["L0_ffn_b2"].reshape(8, 1, 1024))
    d["u_n"] = f(inp["L0_attn_ternary_factor_u"])
    d["uT"] = f(inp["L0_attn_ternary_factor_u"].T)
    d["v_n"] = f(inp["L0_attn_ternary_factor_v"])
    d["vT"] = f(inp["L0_attn_ternary_factor_v"].T)
    rows, cols = np.tril_indices(64)
    tril = np.zeros((16, 64, 64), dtype=np.float32)
    tril[:, rows, cols] = inp["L1_attn_params"]
    d["trilT"] = f(tril.transpose(0, 2, 1))
    d["WvT"] = f(inp["L1_attn_Wv"].T)
    d["bvr"] = f(inp["L1_attn_bv"].reshape(1, 1024))
    d["WoT"] = f(inp["L1_attn_Wo"].T)
    d["bor"] = f(inp["L1_attn_bo"].reshape(1, 1024))
    g1r = inp["L1_ffn_G1_R"][0].transpose(1, 0, 2).reshape(4, 1024)
    g1i = inp["L1_ffn_G1_I"][0].transpose(1, 0, 2).reshape(4, 1024)
    g2r = inp["L1_ffn_G2_R"][:, :, 0, :].reshape(4, 256)
    g2i = inp["L1_ffn_G2_I"][:, :, 0, :].reshape(4, 256)
    d["G1r"] = f(np.concatenate([g1r, g1i, g2r, g2i], axis=1))
    d["brr"] = f(inp["L1_ffn_bias_real"].reshape(1, 512))
    d["bir"] = f(inp["L1_ffn_bias_imag"].reshape(1, 512))
    return d


def kernel(**inputs):
    x = np.ascontiguousarray(np.asarray(inputs["L0_attn_qz"], dtype=np.float32))
    wd = prep_inputs({k: np.asarray(v, dtype=np.float32) for k, v in inputs.items()})
    shards = [x[2 * i:2 * i + 2] for i in range(8)]
    progs = {}
    for parts in LAUNCH_PLAN:
        key = tuple(parts)
        if key not in progs:
            progs[key] = build_program(parts)
    cur = shards
    for parts in LAUNCH_PLAN:
        nc = progs[tuple(parts)]
        needed = {"l0ffn": ["w1", "w2", "se", "gnc", "gsc", "b1c", "b2r"],
                  "l0attn": ["u_n", "uT", "v_n", "vT"],
                  "l1attn": ["trilT", "WvT", "bvr", "WoT", "bor"],
                  "l1ffn": ["G1r", "brr", "bir"]}
        names = set()
        for pp_ in _norm_parts(parts):
            names.update(needed[pp_])
        in_maps = []
        for ci in range(8):
            m = {"x": cur[ci]}
            for nm in names:
                m[nm] = wd[nm]
            in_maps.append(m)
        res = run_bass_kernel_spmd(nc, in_maps, core_ids=list(range(8)))
        cur = [np.asarray(r["y"]) for r in res.results]
    return np.concatenate(cur, axis=0)


# version 05
# speedup vs baseline: 1.1892x; 1.1362x over previous
import os
import numpy as np
from contextlib import ExitStack
import concourse.bass as bass
import concourse.mybir as mybir
from concourse.bass_utils import run_bass_kernel_spmd

F32 = mybir.dt.float32
BF16 = mybir.dt.bfloat16
AF = mybir.ActivationFunctionType
ALU = mybir.AluOpType
AX = mybir.AxisListType

NB = 2
NT = 2048
DM = 1024
NCH = 8
NE = 8
HID = 4096

ST_RM = 0
ST_CS = 32
ST_NAM = 64
ST_RN = 96
ST_MS1 = 112
ST_NMX = 128
ST_RS = 144
ST_RR = 160
ST_B1 = 176
ST_GN = 208
ST_GS = 216
ST_MS2 = 224


class P:
    ENGS = ("g", "t", "v", "s", "y")

    def __init__(self, nc, ctx):
        self.nc = nc
        self.ops = {e: [] for e in self.ENGS}
        self.cnt = {}
        self.sems = {}
        for name in ("gld", "yld", "gc", "tc", "vc", "sc"):
            self.sems[name] = ctx.enter_context(nc.semaphore(name))
            self.cnt[name] = 0

    def do(self, eng, fn, wait=(), inc=None, inc_amt=1):
        wait = tuple((s, v) for (s, v) in wait if v > 0)
        for s, v in wait:
            assert v <= self.cnt[s], f"wait {s}>={v} but planned only {self.cnt[s]}"
        if inc is not None:
            self.cnt[inc] += inc_amt
        sems = self.sems

        def emit(e, fn=fn, wait=wait, inc=inc, inc_amt=inc_amt):
            for s, v in wait:
                e.wait_ge(sems[s], v)
            inst = fn(e)
            if inc is not None:
                assert inst is not None
                inst.then_inc(sems[inc], inc_amt)
        self.ops[eng].append(emit)
        return self.cnt[inc] if inc is not None else None

    def now(self, name):
        return self.cnt[name]

    def snap(self):
        return dict(self.cnt)

    def build(self, block):
        ops = self.ops

        @block.gpsimd
        def _(g):
            for f in ops["g"]:
                f(g)

        @block.tensor
        def _(t):
            for f in ops["t"]:
                f(t)

        @block.vector
        def _(v):
            for f in ops["v"]:
                f(v)

        @block.scalar
        def _(s):
            for f in ops["s"]:
                f(s)

        @block.sync
        def _(y):
            for f in ops["y"]:
                f(y)


class Kern:
    def __init__(self, nc, ctx, parts):
        self.nc = nc
        self.ctx = ctx
        self.parts = list(parts)
        self.p = P(nc, ctx)
        self.xfm_ready = ("vc", 0)

    def declare_io(self):
        nc = self.nc
        def ap(name, shape, dt, kind):
            return nc.dram_tensor(name, shape, dt, kind=kind).ap()
        I, O, N = "ExternalInput", "ExternalOutput", "Internal"
        self.x_in = ap("x", [NB, NT, DM], F32, I)
        self.y_out = ap("y", [NB, NT, DM], F32, O)
        pset = {"l1ffn" if q.startswith("l1ffn") else q for q in self.parts}
        if "l0ffn" in pset:
            self.w1 = ap("w1", [NE, DM, HID], F32, I)
            self.w2 = ap("w2", [NE, HID, DM], F32, I)
            self.se = ap("se", [NE * 512, DM], F32, I)
            self.gnc = ap("gnc", [128, NCH], F32, I)
            self.gsc = ap("gsc", [128, NCH], F32, I)
            self.b1c = ap("b1c", [NE, 128, 32], F32, I)
            self.b2r = ap("b2r", [NE, 1, DM], F32, I)
            self.Lsp = ap("Lsp", [NB, 32, 128, NT], F32, N)
            self.eod = ap("eod", [NB, NE * 512, DM], BF16, N)
        if "l0attn" in pset:
            self.u_n = ap("u_n", [DM, DM], F32, I)
            self.uT = ap("uT", [DM, DM], F32, I)
            self.v_n = ap("v_n", [DM, DM], F32, I)
            self.vT = ap("vT", [DM, DM], F32, I)
        if "l1attn" in pset:
            self.trilT = ap("trilT", [16, 64, 64], F32, I)
            self.WvT = ap("WvT", [DM, DM], F32, I)
            self.bvr = ap("bvr", [1, DM], F32, I)
            self.WoT = ap("WoT", [DM, DM], F32, I)
            self.bor = ap("bor", [1, DM], F32, I)
        if "l1ffn" in pset:
            self.G1r = ap("G1r", [4, 2560], F32, I)
            self.brr = ap("brr", [1, 512], F32, I)
            self.bir = ap("bir", [1, 512], F32, I)
            self.pdr = ap("pdr", [1024, 256], F32, N)
            self.wts = ap("wts", [128, 6144], BF16, N)
        self.bncb = ap("bncb", [1, 2048], BF16, N)

    def alloc(self):
        nc, E = self.nc, self.ctx.enter_context
        self.XFM = E(nc.sbuf_tensor("XFM", [128, NB * NCH * NT], BF16))
        self.A = E(nc.sbuf_tensor("Abuf", [128, 32768], BF16))
        self.Bar = E(nc.sbuf_tensor("Bar", [128, 16384], BF16))
        self.DT = E(nc.sbuf_tensor("DT", [128, 6144], BF16))
        self.WB = E(nc.sbuf_tensor("WB", [128, 8192], BF16))
        self.SL = E(nc.sbuf_tensor("SL", [128, 2048], BF16))
        self.FSA = E(nc.sbuf_tensor("FSA", [128, 2048], F32))
        self.PMRN = E(nc.sbuf_tensor("PMRN", [128, 2048], BF16))
        self.ST = E(nc.sbuf_tensor("STT", [128, 256], F32))
        self.ident = E(nc.sbuf_tensor("ident", [128, 128], BF16))
        self.ones = E(nc.sbuf_tensor("onesb", [128, 512], BF16))
        self.PL = E(nc.psum_tensor("PL", [128, 2048], F32))
        self.PM = E(nc.psum_tensor("PM", [128, 2048], F32))
        self.PA = self.PM[:, 0:512]
        self.PB = self.PM[:, 512:1024]
        self.PC = self.PM[:, 1024:1536]
        self.PD = self.PM[:, 1536:2048]

    def xfm(self, b, c):
        return self.XFM[:, (b * NCH + c) * NT:(b * NCH + c) * NT + NT]

    def emit_init(self):
        p = self.p
        p.do("g", lambda g: g.memset(self.ones[:, :], 1.0), inc="gc")
        p.do("g", lambda g: g.memset(self.ident[:, :], 0.0), inc="gc")
        v = p.now("gc")
        p.do("g", lambda g: g.affine_select(
            out=self.ident[:, :], in_=self.ident[:, :],
            compare_op=ALU.not_equal, fill=1.0, base=0,
            pattern=[[-1, 128]], channel_multiplier=1), wait=(("gc", v),), inc="gc")
        p.do("g", lambda g: g.memset(self.ST[:, ST_MS2 + 4:ST_MS2 + 5], 1e-5),
             inc="gc")
        if "l0ffn" in self.parts:
            p.do("y", lambda y: y.dma_start(self.ST[:, ST_GN:ST_GN + 8], self.gnc[:, :]),
                 inc="yld", inc_amt=16)
            p.do("y", lambda y: y.dma_start(self.ST[:, ST_GS:ST_GS + 8], self.gsc[:, :]),
                 inc="yld", inc_amt=16)
        self.init_gc = p.now("gc")
        self.init_yld = p.now("yld")

    def emit_load_x(self):
        p = self.p
        xin = self.SL[:, 0:1024]
        pbf = self.PL[:, 0:512].bitcast(BF16)
        xfm3 = self.XFM[:, :].rearrange("p (k n) -> p k n", n=NT)
        for b in range(NB):
            for k in range(16):
                ld = p.do("g", lambda g, b=b, k=k: g.dma_start(
                    xin, self.x_in[b, k * 128:(k + 1) * 128, :]),
                    wait=(("tc", p.now("tc")),), inc="gld", inc_amt=16)
                tw = [("gld", ld), ("gc", self.init_gc), ("vc", p.now("vc"))]
                tcv = None
                for c in range(NCH):
                    tcv = p.do("t", lambda t, c=c: t.transpose(
                        pbf[:, c * 128:(c + 1) * 128], xin[:, c * 128:(c + 1) * 128],
                        self.ident[:, :]),
                        wait=tw if c == 0 else (), inc="tc" if c == NCH - 1 else None)
                dst = xfm3[:, b * NCH:(b + 1) * NCH, k * 128:k * 128 + 128]
                src = pbf[:, :].rearrange("p (c j) -> p c j", j=128)
                p.do("v", lambda v, dst=dst, src=src: v.tensor_copy(dst, src),
                     wait=(("tc", tcv),), inc="vc")
        self.xfm_ready = ("vc", p.now("vc"))

    def emit_store_y(self):
        p = self.p
        xout = self.FSA[:, 0:1024]
        pbf = self.PL[:, 0:512].bitcast(BF16)
        xfm3 = self.XFM[:, :].rearrange("p (k n) -> p k n", n=NT)
        rs, rv = self.xfm_ready
        for b in range(NB):
            for k in range(16):
                src3 = xfm3[:, b * NCH:(b + 1) * NCH, k * 128:k * 128 + 128]
                tcv = None
                for c in range(NCH):
                    tw = [(rs, rv), ("vc", p.now("vc")), ("yld", p.now("yld"))] if c == 0 else []
                    tcv = p.do("t", lambda t, c=c, src3=src3: t.transpose(
                        pbf[:, c * 128:(c + 1) * 128], src3[:, c, :], self.ident[:, :]),
                        wait=tw, inc="tc" if c == NCH - 1 else None)
                p.do("v", lambda v: v.tensor_copy(xout, pbf[:, :]),
                     wait=(("tc", tcv),), inc="vc")
                p.do("y", lambda y, b=b, k=k: y.dma_start(
                    self.y_out[b, k * 128:(k + 1) * 128, :], xout),
                    wait=(("vc", p.now("vc")),), inc="yld", inc_amt=16)
        p.do("y", lambda y: None, wait=(("yld", p.now("yld")),))

    def emit_ln(self, b, dst, rms=False):
        p = self.p
        sq = [self.SL[:, 0:512], self.SL[:, 512:1024]]
        rs, rv = self.xfm_ready
        r0 = lambda q: self.FSA[0:1, 0:512]
        r1 = lambda q: self.FSA[0:1, 512:1024]
        r2 = lambda q: self.FSA[0:1, 1024:1536]
        r3 = lambda q: self.FSA[0:1, 1536:2048]
        TMPB = self.A[:, 0:2048]
        SRNB = self.A[:, 2048:4096]
        MSRB = self.A[:, 4096:6144]

        for q in range(4):
            n0 = q * 512
            wA = [(rs, rv), ("vc", p.now("vc")), ("sc", p.now("sc")),
                  ("yld", p.now("yld"))]
            sqv = []
            for c in range(2):
                sqv.append(p.do("s", lambda s, c=c, n0=n0: s.activation(
                    sq[c % 2], self.xfm(b, c)[:, n0:n0 + 512], AF.Square),
                    wait=[(rs, rv), ("tc", p.now("tc")), ("yld", p.now("yld")),
                          ("gld", p.now("gld"))] if c == 0 else (), inc="sc"))
            tsum = None
            for c in range(NCH):
                tsum = p.do("t", lambda t, c=c, n0=n0: t.matmul(
                    self.PA[0:1, :], self.ones[:, 0:1], self.xfm(b, c)[:, n0:n0 + 512],
                    start=(c == 0), stop=(c == NCH - 1)),
                    wait=wA if c == 0 else (), inc="tc" if c == NCH - 1 else None)
            tssq = None
            for c in range(NCH):
                tssq = p.do("t", lambda t, c=c: t.matmul(
                    self.PB[0:1, :], self.ones[:, 0:1], sq[c % 2],
                    start=(c == 0), stop=(c == NCH - 1)),
                    wait=[("sc", sqv[c]), ("vc", p.now("vc"))] if c == 0 else
                         [("sc", sqv[c])], inc="tc")
                if c + 2 < NCH:
                    sqv.append(p.do("s", lambda s, c=c, n0=n0: s.activation(
                        sq[c % 2], self.xfm(b, c + 2)[:, n0:n0 + 512], AF.Square),
                        wait=(("tc", tssq),), inc="sc"))
            p.do("v", lambda v, q=q: v.tensor_copy(r0(q), self.PA[0:1, :]),
                 wait=(("tc", p.now("tc")), ("sc", p.now("sc"))), inc="vc")
            p.do("v", lambda v, q=q: v.tensor_copy(r1(q), self.PB[0:1, :]), inc="vc")
            p.do("v", lambda v, q=q: v.tensor_scalar_mul(r2(q), r0(q), 1.0 / DM),
                 inc="vc")
            p.do("v", lambda v, q=q: v.tensor_scalar_mul(r1(q), r1(q), 1.0 / DM),
                 inc="vc")
            p.do("v", lambda v, q=q: v.tensor_mul(r3(q), r2(q), r2(q)), inc="vc")
            p.do("v", lambda v, q=q: v.tensor_sub(r1(q), r1(q), r3(q)), inc="vc")
            vv = p.now("vc")
            sv = p.do("s", lambda s, q=q: s.activation(
                r3(q), r1(q), AF.Sqrt, bias=self.ST[0:1, ST_MS2 + 4:ST_MS2 + 5]),
                wait=(("vc", vv), ("gc", self.init_gc)), inc="sc")
            p.do("v", lambda v, q=q: v.reciprocal(r0(q), r3(q)),
                 wait=(("sc", sv),), inc="vc")
            if rms:
                sv = p.do("s", lambda s, q=q: s.activation(r3(q), r1(q), AF.Sqrt),
                          wait=(("vc", p.now("vc")),), inc="sc")
                p.do("v", lambda v, q=q: v.tensor_mul(r3(q), r3(q), r0(q)),
                     wait=(("sc", sv),), inc="vc")
                p.do("v", lambda v, q=q: v.tensor_scalar(r3(q), r3(q), 32.0, 1e-12,
                                                         ALU.mult, ALU.max), inc="vc")
                p.do("v", lambda v, q=q: v.reciprocal(r1(q), r3(q)), inc="vc")
                p.do("v", lambda v, q=q: v.tensor_mul(r0(q), r0(q), r1(q)), inc="vc")
                p.do("v", lambda v, q=q: v.tensor_scalar_mul(r0(q), r0(q), 32.0),
                     inc="vc")
            p.do("v", lambda v, q=q: v.tensor_mul(r2(q), r2(q), r0(q)), inc="vc")
            p.do("v", lambda v, q=q: v.tensor_copy(self.SL[0:1, 1024:1536], r0(q)),
                 inc="vc")
            p.do("v", lambda v, q=q: v.tensor_copy(self.SL[0:1, 1536:2048], r2(q)),
                 inc="vc")
            vrow = p.now("vc")
            p.do("t", lambda t, q=q: t.matmul(
                self.PA[:, :], self.ones[0:1, 0:128], self.SL[0:1, 1024:1536],
                start=True, stop=True),
                wait=[("vc", vrow)], inc="tc")
            tb = p.do("t", lambda t, q=q: t.matmul(
                self.PB[:, :], self.ones[0:1, 0:128], self.SL[0:1, 1536:2048],
                start=True, stop=True),
                inc="tc")
            p.do("v", lambda v, q=q: v.tensor_copy(SRNB[:, q * 512:(q + 1) * 512],
                                                   self.PA[:, :]),
                 wait=[("tc", tb)] + ([("tc", p.now("tc")), ("sc", p.now("sc"))]
                                      if q == 0 else []), inc="vc")
            p.do("v", lambda v, q=q: v.tensor_copy(MSRB[:, q * 512:(q + 1) * 512],
                                                   self.PB[:, :]), inc="vc")
        vbc = p.now("vc")
        for c in range(NCH):
            p.do("v", lambda v, c=c: v.tensor_mul(TMPB, self.xfm(b, c), SRNB),
                 wait=[("vc", vbc), ("yld", self.init_yld),
                       ("tc", p.now("tc")), ("sc", p.now("sc"))] if c == 0 else (),
                 inc="vc")
            p.do("v", lambda v, c=c: v.tensor_sub(dst(c), TMPB, MSRB), inc="vc")
            if rms:
                p.do("v", lambda v, c=c: v.tensor_scalar_mul(
                    dst(c), dst(c), self.ST[:, ST_GN + c:ST_GN + c + 1]), inc="vc")
        return p.now("vc")

    def emit_l1ffn(self, first):
        p = self.p
        WRT = self.DT[:, 0:2048]
        WIT = self.DT[:, 2048:4096]
        NIT = self.DT[:, 4096:6144]
        if first:
            g1 = self.WB[0:4, 0:2560]
            l1 = p.do("g", lambda g: g.dma_start(g1, self.G1r[:, :]),
                      wait=(("vc", p.now("vc")), ("sc", p.now("sc"))),
                      inc="gld", inc_amt=16)
            fs = self.FSA[:, 0:512]
            wn = self.WB[:, 2560:4608]
            pbf = self.PC[:, :].bitcast(BF16)
            gq_prev = p.now("gld")
            for ri in range(2):
                for ms in range(8):
                    tv = p.do("t", lambda t, ri=ri, ms=ms: t.matmul(
                        self.PA[:, 0:256],
                        g1[:, ri * 1024 + ms * 128:ri * 1024 + (ms + 1) * 128],
                        g1[:, 2048 + ri * 256:2048 + ri * 256 + 256],
                        start=True, stop=True),
                        wait=[("gld", max(l1, gq_prev)), ("vc", p.now("vc")),
                              ("yld", p.now("yld"))], inc="tc")
                    vv = p.do("v", lambda v: v.tensor_copy(fs[:, 0:256], self.PA[:, 0:256]),
                              wait=(("tc", tv), ("yld", p.now("yld"))), inc="vc")
                    p.do("y", lambda y, ms=ms: y.dma_start(
                        self.pdr[ms * 128:(ms + 1) * 128, :], fs[:, 0:256]),
                        wait=(("vc", vv),), inc="yld", inc_amt=16)
                yv = p.now("yld")
                lw = None
                for b32 in range(32):
                    rc = b32 // 8
                    dst = wn[(b32 % 8) * 16:(b32 % 8) * 16 + 16,
                             rc * 512:(rc + 1) * 512].rearrange("p (d g) -> p d g", g=16)
                    src2 = self.pdr[b32 * 32:(b32 + 1) * 32, :].rearrange(
                        "d (e g) -> e d g", g=16)
                    lw = p.do("g", lambda g, dst=dst, src2=src2: g.dma_start(dst, src2),
                              wait=[("yld", yv), ("tc", p.now("tc"))] if b32 == 0 else (),
                              inc="gld", inc_amt=16)
                gq_prev = lw
                dstT = WRT if ri == 0 else WIT
                for grp in range(2):
                    tv = None
                    for j in range(8):
                        kk = grp * 8 + j
                        rc, jc = kk // 4, kk % 4
                        tv = p.do("t", lambda t, rc=rc, jc=jc, j=j: t.transpose(
                            pbf[:, j * 128:(j + 1) * 128],
                            wn[:, rc * 512 + jc * 128:rc * 512 + (jc + 1) * 128],
                            self.ident[:, :]),
                            wait=[("gld", lw), ("vc", p.now("vc"))] if j == 0 else (),
                            inc="tc" if j == 7 else None)
                    for j in range(8):
                        kk = grp * 8 + j
                        rc, jc = kk // 4, kk % 4
                        p.do("v", lambda v, rc=rc, jc=jc, j=j, dstT=dstT: v.tensor_copy(
                            dstT[:, jc * 512 + rc * 128:jc * 512 + (rc + 1) * 128],
                            pbf[:, j * 128:(j + 1) * 128]),
                            wait=[("tc", tv)] if j == 0 else (), inc="vc")
            p.do("v", lambda v: v.tensor_scalar_mul(NIT, WIT, -1.0), inc="vc")
            wv = p.now("vc")
            p.do("y", lambda y: y.dma_start(self.wts[:, :], self.DT[:, :]),
                 wait=(("vc", wv),), inc="yld", inc_amt=16)
            self.l1ffn_w_done = p.now("yld")
            wgate = ("vc", wv)
        else:
            lw = p.do("g", lambda g: g.dma_start(self.DT[:, :], self.wts[:, :]),
                      wait=(("vc", p.now("vc")), ("sc", p.now("sc")),
                            ("yld", self.l1ffn_w_done)),
                      inc="gld", inc_amt=16)
            wgate = ("gld", lw)

        lbr = p.do("g", lambda g: g.dma_start(self.PMRN[0:1, 0:512], self.brr[:, :]),
                   wait=(("tc", p.now("tc")), ("sc", p.now("sc")),
                         ("vc", p.now("vc"))), inc="gld", inc_amt=16)
        lbr = p.do("g", lambda g: g.dma_start(self.PMRN[0:1, 512:1024], self.bir[:, :]),
                   inc="gld", inc_amt=16)
        for b in range(NB):
            lnv = self.emit_ln(b, lambda c: self.Bar[:, c * 2048:(c + 1) * 2048])
            pp = [self.PA, self.PB]
            gate = {0: ("vc", 0), 1: ("vc", 0)}
            for oc in range(NCH):
                real = oc < 4
                om = oc % 4
                for nb4 in range(4):
                    pi = (oc * 4 + nb4) % 2
                    ps = pp[pi]
                    bias_row = (self.PMRN[0:1, 0:512] if real
                                else self.PMRN[0:1, 512:1024])
                    wa, wb_ = (WRT, NIT) if real else (WIT, WRT)
                    for k in range(4):
                        p.do("t", lambda t, k=k, om=om, nb4=nb4, wa=wa, ps=ps: t.matmul(
                            ps[:, :], wa[:, k * 512 + om * 128:k * 512 + (om + 1) * 128],
                            self.Bar[:, k * 2048 + nb4 * 512:k * 2048 + nb4 * 512 + 512],
                            start=(k == 0), stop=False),
                            wait=[("vc", lnv), wgate, gate[pi]] if k == 0 else ())
                    for k in range(4):
                        p.do("t", lambda t, k=k, om=om, nb4=nb4, wb_=wb_, ps=ps: t.matmul(
                            ps[:, :], wb_[:, k * 512 + om * 128:k * 512 + (om + 1) * 128],
                            self.Bar[:, (4 + k) * 2048 + nb4 * 512:(4 + k) * 2048 + nb4 * 512 + 512],
                            start=False, stop=False))
                    tv = p.do("t", lambda t, om=om, ps=ps, bias_row=bias_row: t.matmul(
                        ps[:, :], bias_row[:, om * 128:(om + 1) * 128],
                        self.ones[0:1, 0:512], start=False, stop=True),
                        wait=(("gld", lbr),), inc="tc")
                    xs = self.xfm(b, oc)[:, nb4 * 512:(nb4 + 1) * 512]
                    ev = p.do("v", lambda v, xs=xs, ps=ps: v.scalar_tensor_tensor(
                        xs, ps[:, :], 0.5, xs, ALU.mult, ALU.add),
                        wait=(("tc", tv),), inc="vc")
                    gate[pi] = ("vc", ev)
            self.xfm_ready = ("vc", p.now("vc"))

    def emit_l0attn(self):
        p = self.p
        for b in range(NB):
            lnv = self.emit_ln(b, lambda c: self.Bar[:, c * 2048:(c + 1) * 2048])
            pp = [self.PA, self.PB]
            gate = {0: ("vc", 0), 1: ("sc", 0)}
            slot_gate = [("tc", 0)] * 4
            pcount = 0
            for m in range(8):
                lds = []
                for ti, wsrc in enumerate((self.uT, self.vT)):
                    si = ti * 2 + (m % 2)
                    slot = self.DT[:, si * 1024:(si + 1) * 1024]
                    src = wsrc[:, m * 128:(m + 1) * 128].rearrange(
                        "(k q) c -> q k c", q=128)
                    dst3 = slot.rearrange("q (k c) -> q k c", c=128)
                    ld = p.do("g", lambda g, dst3=dst3, src=src: g.dma_start(dst3, src),
                              wait=[slot_gate[si], ("vc", lnv)], inc="gld", inc_amt=16)
                    lds.append(ld)
                for ti in range(2):
                    si = ti * 2 + (m % 2)
                    slot = self.DT[:, si * 1024:(si + 1) * 1024]
                    for nb4 in range(4):
                        pi = pcount % 2
                        pcount += 1
                        ps = pp[pi]
                        tv = None
                        for k in range(NCH):
                            tv = p.do("t", lambda t, k=k, m=m, nb4=nb4, slot=slot, ps=ps: t.matmul(
                                ps[:, :], slot[:, k * 128:(k + 1) * 128],
                                self.Bar[:, k * 2048 + nb4 * 512:k * 2048 + nb4 * 512 + 512],
                                start=(k == 0), stop=(k == NCH - 1)),
                                wait=[("gld", lds[ti]), ("vc", lnv), gate[pi]] if k == 0 else (),
                                inc="tc" if k == NCH - 1 else None)
                        dst = self.A[:, ti * 16384 + m * 2048 + nb4 * 512:
                                     ti * 16384 + m * 2048 + nb4 * 512 + 512]
                        ev = p.do("s", lambda s, dst=dst, ps=ps: s.mul(dst, ps[:, :], 0.03125),
                                  wait=(("tc", tv),), inc="sc")
                        gate[pi] = ("sc", ev)
                    slot_gate[si] = ("tc", p.now("tc"))
            SKIP = os.environ.get("L0SKIP", "")
            pbf = [self.PC[:, :].bitcast(BF16), self.PD[:, :].bitcast(BF16)]
            vbank = [self.PA, self.PB, self.PC, self.PD]
            qh_row = self.Bar[:, 4096:6144]
            qhT_row = self.Bar[:, 6144:8192]
            bounce = self.Bar[:, 8192:10240]
            zrow = self.SL[0:1, 0:128]
            zv = p.do("g", lambda g: g.memset(zrow, 0.0),
                      wait=(("sc", p.now("sc")), ("tc", p.now("tc"))), inc="gc")
            prev_pair_done = ("vc", p.now("vc"))
            prev_bounce = ("gld", p.now("gld"))
            for pr in range(0 if "heads" not in SKIP else 8, 8):
                tmv = []
                for ti in range(2 if "notm" not in SKIP else 0):
                    for grp in range(2):
                        tv = None
                        for j in range(8):
                            srcap = self.A[:, ti * 16384 + pr * 2048 + (grp * 8 + j) * 128:
                                           ti * 16384 + pr * 2048 + (grp * 8 + j) * 128 + 128]
                            tv = p.do("t", lambda t, j=j, ti=ti, srcap=srcap: t.transpose(
                                pbf[ti][:, j * 128:(j + 1) * 128], srcap, self.ident[:, :]),
                                wait=([("sc", p.now("sc")), ("vc", prev_pair_done[1]),
                                       ("gld", p.now("gld"))] if grp == 0 and ti == 0 else
                                      [("vc", p.now("vc"))]) if j == 0 else (),
                                inc="tc" if j == 7 else None)
                        dst = self.Bar[:, ti * 2048 + grp * 1024:ti * 2048 + (grp + 1) * 1024]
                        ev = p.do("v", lambda v, dst=dst, ti=ti: v.tensor_copy(
                            dst, pbf[ti][:, 0:1024]), wait=(("tc", tv),), inc="vc")
                        tmv.append(ev)
                tm_done = p.now("vc")
                ev_gate = ("vc", tm_done)
                for h in range(2):
                    base = 64 * h
                    for tb in range(0 if "noz" not in SKIP else 4, 4):
                        p.do("t", lambda t, tb=tb: t.matmul(
                            vbank[tb][:, :], zrow, self.ones[0:1, 0:512],
                            start=True, stop=False),
                            wait=[ev_gate, ("gc", zv)] if tb == 0 else ())
                    sg2 = 0
                    for i in range(0 if "norows" not in SKIP else 16, 16):
                        tv = None
                        for tb in range(4):
                            tv = p.do("t", lambda t, i=i, tb=tb, pr=pr, base=base: t.matmul(
                                self.PL[:, tb * 512:(tb + 1) * 512],
                                self.A[base:base + 64, pr * 2048 + i * 128:pr * 2048 + (i + 1) * 128],
                                self.A[base:base + 64,
                                       16384 + pr * 2048 + tb * 512:16384 + pr * 2048 + tb * 512 + 512],
                                start=True, stop=True),
                                wait=([("sc", p.now("sc")), ("vc", tm_done), ev_gate]
                                      if i == 0 else [("sc", sg2)]) if tb == 0 else (),
                                inc="tc" if tb == 3 else None)
                        sg = p.do("s", lambda s: s.activation(qh_row, self.PL[:, :], AF.Sigmoid),
                                  wait=(("tc", tv), ("gld", p.now("gld"))), inc="sc")
                        for tb in range(0 if "nov2" not in SKIP else 4, 4):
                            p.do("t", lambda t, i=i, tb=tb, base=base: t.matmul(
                                vbank[tb][64:128, :],
                                self.Bar[:, i * 128 + base:i * 128 + base + 64],
                                qh_row[:, tb * 512:(tb + 1) * 512],
                                start=False, stop=False, tile_position=(0, 64)),
                                wait=[("sc", sg)] if tb == 0 else ())
                        tv = None
                        for tb in range(4):
                            tv = p.do("t", lambda t, i=i, tb=tb, pr=pr, base=base: t.matmul(
                                self.PL[:, tb * 512:(tb + 1) * 512],
                                self.A[base:base + 64,
                                       16384 + pr * 2048 + i * 128:16384 + pr * 2048 + (i + 1) * 128],
                                self.A[base:base + 64, pr * 2048 + tb * 512:pr * 2048 + tb * 512 + 512],
                                start=True, stop=True),
                                wait=[("sc", sg)] if tb == 0 else (),
                                inc="tc" if tb == 3 else None)
                        sg2 = p.do("s", lambda s: s.activation(qhT_row, self.PL[:, :], AF.Sigmoid),
                                   wait=(("tc", tv),), inc="sc")
                        for sb in range(0 if "nov1" not in SKIP else 4, 4):
                            p.do("t", lambda t, i=i, sb=sb, base=base: t.matmul(
                                vbank[sb][0:64, :],
                                self.Bar[:, 2048 + i * 128 + base:2048 + i * 128 + base + 64],
                                qhT_row[:, sb * 512:(sb + 1) * 512],
                                start=False, stop=False, tile_position=(0, 0)),
                                wait=[("sc", sg2)] if sb == 0 else ())
                    vdone = None
                    for tb in range(0 if "noz" not in SKIP else 4, 4):
                        vdone = p.do("t", lambda t, tb=tb: t.matmul(
                            vbank[tb][:, :], zrow, self.ones[0:1, 0:512],
                            start=False, stop=True),
                            inc="tc" if tb == 3 else None)
                    evs = []
                    for ti, rows0 in (((0, 0), (1, 64)) if "noevac" not in SKIP else ()):
                        direct = (rows0 == base)
                        for tb in range(4):
                            if direct:
                                dst = self.A[base:base + 64,
                                             ti * 16384 + pr * 2048 + tb * 512:
                                             ti * 16384 + pr * 2048 + tb * 512 + 512]
                            else:
                                dst = bounce[rows0:rows0 + 64, tb * 512:(tb + 1) * 512]
                            ev = p.do("v", lambda v, dst=dst, tb=tb, rows0=rows0: v.tensor_copy(
                                dst, vbank[tb][rows0:rows0 + 64, :]),
                                wait=[("tc", vdone), prev_bounce] if (tb == 0 and ti == 0 and vdone) else [prev_bounce] if tb == 0 and ti == 0 else (),
                                inc="vc")
                        if (not direct) and ("nobounce" not in SKIP):
                            bv = p.do("g", lambda g, ti=ti, pr=pr, rows0=rows0, base=base: g.dma_start(
                                self.A[base:base + 64,
                                       ti * 16384 + pr * 2048:ti * 16384 + (pr + 1) * 2048],
                                bounce[rows0:rows0 + 64, :]),
                                wait=(("vc", p.now("vc")),), inc="gld", inc_amt=16)
                            prev_bounce = ("gld", bv)
                    ev_gate = ("vc", p.now("vc"))
                prev_pair_done = ("vc", p.now("vc"))
            gate = {0: ("vc", 0), 1: ("vc", 0)}
            slot_gate = [("tc", 0)] * 4
            pcount = 0
            for dc in range(8):
                lds = []
                for ti, wsrc in enumerate((self.u_n, self.v_n)):
                    si = ti * 2 + (dc % 2)
                    slot = self.DT[:, si * 1024:(si + 1) * 1024]
                    src = wsrc[:, dc * 128:(dc + 1) * 128].rearrange(
                        "(k q) c -> q k c", q=128)
                    dst3 = slot.rearrange("q (k c) -> q k c", c=128)
                    ld = p.do("g", lambda g, dst3=dst3, src=src: g.dma_start(dst3, src),
                              wait=[slot_gate[si], prev_bounce, ("vc", prev_pair_done[1])],
                              inc="gld", inc_amt=16)
                    lds.append(ld)
                for nb4 in range(4):
                    pi = pcount % 2
                    pcount += 1
                    ps = pp[pi]
                    tv = None
                    for ti in range(2):
                        for k in range(NCH):
                            tv = p.do("t", lambda t, k=k, ti=ti, dc=dc, nb4=nb4, ps=ps: t.matmul(
                                ps[:, :],
                                self.DT[:, (ti * 2 + dc % 2) * 1024 + k * 128:
                                        (ti * 2 + dc % 2) * 1024 + (k + 1) * 128],
                                self.A[:, ti * 16384 + k * 2048 + nb4 * 512:
                                       ti * 16384 + k * 2048 + nb4 * 512 + 512],
                                start=(ti == 0 and k == 0), stop=(ti == 1 and k == NCH - 1)),
                                wait=[("gld", lds[1]), gate[pi],
                                      ("vc", prev_pair_done[1])] if ti == 0 and k == 0 else (),
                                inc="tc" if (ti == 1 and k == NCH - 1) else None)
                    xs = self.xfm(b, dc)[:, nb4 * 512:(nb4 + 1) * 512]
                    ev = p.do("v", lambda v, xs=xs, ps=ps: v.scalar_tensor_tensor(
                        xs, ps[:, :], 0.03125, xs, ALU.mult, ALU.add),
                        wait=(("tc", tv),), inc="vc")
                    gate[pi] = ("vc", ev)
                slot_gate[2 * 0 + dc % 2] = ("tc", p.now("tc"))
                slot_gate[2 * 1 + dc % 2] = ("tc", p.now("tc"))
            self.xfm_ready = ("vc", p.now("vc"))

    def emit_l1attn(self):
        p = self.p
        WQK = self.DT[:, 0:1024]
        att = [self.DT[:, 1024:3072], self.DT[:, 3072:5120]]
        nrow = self.SL[0:1, 0:2048]
        stg1 = self.DT[0:16, 5120:5248]
        stg2 = self.DT[0:16, 5248:5376]
        nmx_bf = self.DT[:, 5376:5392]
        rr_bf = self.DT[:, 5392:5408]
        rrb = self.PMRN
        zrow = self.DT[0:1, 5408:5536]
        SKIP1 = os.environ.get("L1SKIP", "")
        for b in range(NB if "bb0" not in SKIP1 else 1):
            zv = p.do("g", lambda g: g.memset(zrow, 0.0),
                      wait=(("sc", p.now("sc")), ("tc", p.now("tc")),
                            ("vc", p.now("vc"))), inc="gc")
            lbv = p.do("g", lambda g: g.dma_start(self.PMRN[0:1, 0:1024],
                                                  self.bvr[:, :]),
                       inc="gld", inc_amt=16)
            lnv = self.emit_ln(b, lambda c: self.Bar[:, c * 2048:(c + 1) * 2048])
            lv = p.do("g", lambda g: g.dma_start(
                self.WB[:, :].rearrange("q (k c) -> q k c", c=1024),
                self.WvT[:, :].rearrange("(k q) c -> q k c", q=128)),
                wait=(("tc", p.now("tc")), ("sc", p.now("sc"))), inc="gld", inc_amt=16)
            pp = [self.PA, self.PB]
            gate = {0: ("vc", 0), 1: ("vc", 0)}
            pcount = 0
            for nt in range(16):
                for oh in range(2):
                    pi = pcount % 2
                    pcount += 1
                    ps = pp[pi]
                    for k in range(NCH):
                        p.do("t", lambda t, k=k, nt=nt, oh=oh, ps=ps: t.matmul(
                            ps[:, :],
                            self.Bar[:, k * 2048 + nt * 128:k * 2048 + (nt + 1) * 128],
                            self.WB[:, k * 1024 + oh * 512:k * 1024 + oh * 512 + 512],
                            start=(k == 0), stop=False),
                            wait=[("vc", lnv), ("gld", lv), gate[pi]] if k == 0 else ())
                    tv = p.do("t", lambda t, oh=oh, ps=ps: t.matmul(
                        ps[:, :], self.ones[0:1, 0:128],
                        self.PMRN[0:1, oh * 512:oh * 512 + 512],
                        start=False, stop=True), wait=(("gld", lbv),), inc="tc")
                    dst = self.A[:, 16384 + nt * 1024 + oh * 512:16384 + nt * 1024 + oh * 512 + 512]
                    ev = p.do("v", lambda v, dst=dst, ps=ps: v.tensor_copy(dst, ps[:, :]),
                              wait=(("tc", tv),), inc="vc")
                    gate[pi] = ("vc", ev)
            vh_done = p.now("vc")
            wq_gate = ("vc", 0)
            for h in range(16):
                base = 64 * (h % 2)
                ld = p.do("g", lambda g, h=h, base=base: g.dma_start(
                    self.SL[base:base + 64, 0:64], self.trilT[h, :, :]),
                    wait=[wq_gate, ("tc", p.now("tc"))], inc="gld", inc_amt=16)
                tv = p.do("t", lambda t, base=base: t.matmul(
                    self.PC[base:base + 64, 0:64], self.SL[base:base + 64, 0:64],
                    self.SL[base:base + 64, 0:64], start=True, stop=True,
                    tile_position=(base, base)),
                    wait=(("gld", ld), ("vc", p.now("vc"))), inc="tc")
                ev = p.do("v", lambda v, h=h, base=base: v.tensor_copy(
                    WQK[base:base + 64, h * 64:(h + 1) * 64],
                    self.PC[base:base + 64, 0:64]),
                    wait=(("tc", tv),), inc="vc")
                wq_gate = ("vc", ev)
            gate = {0: ("sc", 0), 1: ("sc", 0)}
            pcount = 0
            for pr in range(8):
                for nb4 in range(4):
                    pi = pcount % 2
                    pcount += 1
                    ps0, ps1 = (self.PA, self.PB) if pi == 0 else (self.PC, self.PD)
                    tv = None
                    for par in range(2):
                        h = 2 * pr + par
                        base = 64 * par
                        psx = ps0 if par == 0 else ps1
                        tv = p.do("t", lambda t, h=h, base=base, pr=pr, nb4=nb4, psx=psx: t.matmul(
                            psx[base:base + 64, :], WQK[base:base + 64, h * 64:(h + 1) * 64],
                            self.Bar[base:base + 64, pr * 2048 + nb4 * 512:pr * 2048 + nb4 * 512 + 512],
                            start=True, stop=True, tile_position=(base, base)),
                            wait=[("vc", wq_gate[1]), gate[pi]] if par == 0 else (),
                            inc="tc" if par == 1 else None)
                    for par in range(2):
                        base = 64 * par
                        psx = ps0 if par == 0 else ps1
                        dst = self.A[base:base + 64,
                                     pr * 2048 + nb4 * 512:pr * 2048 + nb4 * 512 + 512]
                        ev = p.do("s", lambda s, dst=dst, psx=psx, base=base: s.mul(
                            dst, psx[base:base + 64, :], 0.125),
                            wait=(("tc", tv),) if par == 0 else (), inc="sc")
                    gate[pi] = ("sc", ev)
            lo = p.do("g", lambda g: g.dma_start(
                self.WB[:, :].rearrange("q (k c) -> q k c", c=1024),
                self.WoT[:, :].rearrange("(k q) c -> q k c", q=128)),
                wait=(("vc", vh_done), ("tc", p.now("tc"))), inc="gld", inc_amt=16)
            pvbank = [self.PA, self.PB, self.PC, self.PD]
            prev_evac = ("vc", 0)
            plbf = self.PL[:, :].bitcast(BF16)
            for h in range(0 if "heads" not in SKIP1 else 16, 1 if "h1" in SKIP1 else 16):
                pr, par = h // 2, h % 2
                base = 64 * par
                sgp = 0
                sgd = {0: ("sc", 0), 1: ("sc", 0)}
                for i in range(16):
                    PX = [self.PL, self.PM][i % 2] if par == 0 else self.PL
                    pxk = (i % 2) if par == 0 else 0
                    tv = None
                    for tb in range(4):
                        tv = p.do("t", lambda t, i=i, tb=tb, pr=pr, base=base, PX=PX: t.matmul(
                            PX[:, tb * 512:(tb + 1) * 512],
                            self.A[base:base + 64, pr * 2048 + i * 128:pr * 2048 + (i + 1) * 128],
                            self.Bar[base:base + 64, pr * 2048 + tb * 512:pr * 2048 + tb * 512 + 512],
                            start=True, stop=True),
                            wait=[("sc", p.now("sc")), ("vc", p.now("vc"))] if (i <= 1 and tb == 0)
                                 else ([sgd[pxk]] if tb == 0 else ()),
                            inc="tc" if tb == 3 else None)
                    rv = p.do("v", lambda v, PX=PX: v.reduce_max(
                        self.ST[:, ST_MS1:ST_MS1 + 1], PX[:, :], axis=AX.X,
                        negate=True), wait=(("tc", tv),), inc="vc")
                    rv = p.do("v", lambda v, i=i: v.tensor_copy(
                        nmx_bf[:, i:i + 1], self.ST[:, ST_MS1:ST_MS1 + 1]),
                        wait=(("vc", rv),), inc="vc")
                    rv = p.do("v", lambda v, i=i: v.tensor_copy(
                        self.ST[:, ST_NMX + i:ST_NMX + i + 1], nmx_bf[:, i:i + 1]),
                        wait=(("vc", rv),), inc="vc")
                    sgp = p.do("s", lambda s, i=i, PX=PX: s.activation(
                        att[0], PX[:, :], AF.Exp,
                        bias=self.ST[:, ST_NMX + i:ST_NMX + i + 1],
                        accum_out=self.ST[:, ST_RS + i:ST_RS + i + 1]),
                        wait=(("vc", rv),), inc="sc")
                    sgd[pxk] = ("sc", sgp)
                tv = p.do("t", lambda t: t.transpose(
                    plbf[0:16, 0:128], nmx_bf[:, 0:16], self.ident[:, :]),
                    wait=(("sc", sgp),), inc="tc")
                fv = p.do("v", lambda v: v.tensor_copy(stg1, plbf[0:16, 0:128]),
                          wait=(("tc", tv),), inc="vc")
                yv = p.do("y", lambda y: y.dma_start(
                    self.bncb[0:1, :].rearrange("x (a c) -> (x a) c", c=128), stg1),
                    wait=(("vc", fv), ("yld", p.now("yld"))), inc="yld", inc_amt=16)
                yv = p.do("y", lambda y: y.dma_start(
                    nrow, self.bncb[0:1, :]), wait=(("yld", yv),), inc="yld", inc_amt=16)
                rv = p.do("v", lambda v: v.reciprocal(
                    self.ST[:, ST_RR:ST_RR + 16], self.ST[:, ST_RS:ST_RS + 16]),
                    wait=(("sc", sgp),), inc="vc")
                rv = p.do("v", lambda v: v.tensor_copy(
                    rr_bf[:, 0:16], self.ST[:, ST_RR:ST_RR + 16]),
                    wait=(("vc", rv),), inc="vc")
                tv = p.do("t", lambda t: t.transpose(
                    plbf[0:16, 128:256], rr_bf[:, 0:16], self.ident[:, :]),
                    wait=(("vc", rv),), inc="tc")
                fv = p.do("v", lambda v: v.tensor_copy(stg2, plbf[0:16, 128:256]),
                          wait=(("tc", tv),), inc="vc")
                yv2 = p.do("y", lambda y: y.dma_start(
                    self.bncb[0:1, :].rearrange("x (a c) -> (x a) c", c=128), stg2),
                    wait=(("vc", fv), ("yld", yv)), inc="yld", inc_amt=16)
                yv2 = p.do("g", lambda g, base=base: g.dma_start(
                    rrb[base:base + 64, :],
                    bass.AP(self.bncb.tensor, 0, [[0, 64], [1, 2048]])),
                    wait=(("yld", yv2),
                          ("vc", prev_evac[1] if h else p.now("vc"))),
                    inc="gld", inc_amt=16)
                rrb_done = yv2
                if par == 0:
                    for sb in range(4):
                        p.do("t", lambda t, sb=sb: t.matmul(
                            pvbank[sb][:, :], zrow, self.ones[0:1, 0:512],
                            start=True, stop=False),
                            wait=[prev_evac, ("gc", zv)] if sb == 0 else ())
                sg2 = 0
                pvlast = None
                for j in range(16):
                    tv = None
                    for sb in range(4):
                        p.do("t", lambda t, j=j, sb=sb, pr=pr, base=base: t.matmul(
                            self.PL[:, sb * 512:(sb + 1) * 512],
                            self.Bar[base:base + 64, pr * 2048 + j * 128:pr * 2048 + (j + 1) * 128],
                            self.A[base:base + 64, pr * 2048 + sb * 512:pr * 2048 + sb * 512 + 512],
                            start=True, stop=False),
                            wait=[("yld", p.now("yld")), ("sc", p.now("sc"))]
                                 if (sb == 0 and j == 0) else
                                 ([("sc", sg2)] if sb == 0 else ()))
                        tv = p.do("t", lambda t, sb=sb: t.matmul(
                            self.PL[:, sb * 512:(sb + 1) * 512], self.ones[0:1, 0:128],
                            nrow[0:1, sb * 512:sb * 512 + 512],
                            start=False, stop=True), inc="tc" if sb == 3 else None)
                    sg2 = p.do("s", lambda s, j=j: s.activation(
                        att[j % 2], self.PL[:, :], AF.Exp),
                        wait=(("tc", tv),), inc="sc")
                    for sb in range(4):
                        pvlast = p.do("t", lambda t, j=j, sb=sb, par=par, h=h: t.matmul(
                            pvbank[sb][64 * par:64 * par + 64, :],
                            self.A[:, 16384 + j * 1024 + h * 64:16384 + j * 1024 + (h + 1) * 64],
                            att[j % 2][:, sb * 512:(sb + 1) * 512],
                            start=False, stop=False, tile_position=(0, 64 * par)),
                            wait=[("sc", sg2)] if sb == 0 else ())
                if par == 1:
                    pvlast = None
                    for sb in range(4):
                        pvlast = p.do("t", lambda t, sb=sb: t.matmul(
                            pvbank[sb][:, :], zrow, self.ones[0:1, 0:512],
                            start=False, stop=True),
                            inc="tc" if sb == 3 else None)
                    evs = None
                    for sb in range(4):
                        dst = self.A[:, pr * 2048 + sb * 512:pr * 2048 + sb * 512 + 512]
                        evs = p.do("v", lambda v, dst=dst, sb=sb: v.tensor_tensor(
                            dst, pvbank[sb][:, :], rrb[:, sb * 512:(sb + 1) * 512],
                            ALU.mult),
                            wait=[("tc", pvlast), ("gld", rrb_done)] if sb == 0 else (),
                            inc="vc")
                    prev_evac = ("vc", evs)
            lbo = p.do("g", lambda g: g.dma_start(self.PMRN[0:1, 0:1024],
                                                  self.bor[:, :]),
                       wait=(("vc", p.now("vc")),), inc="gld", inc_amt=16)
            gate = {0: ("vc", 0), 1: ("vc", 0)}
            pcount = 0
            for dc in range(8):
                for nb4 in range(4):
                    pi = pcount % 2
                    pcount += 1
                    ps = pp[pi]
                    for k in range(NCH):
                        p.do("t", lambda t, k=k, dc=dc, nb4=nb4, ps=ps: t.matmul(
                            ps[:, :], self.WB[:, k * 1024 + dc * 128:k * 1024 + (dc + 1) * 128],
                            self.A[:, k * 2048 + nb4 * 512:k * 2048 + nb4 * 512 + 512],
                            start=(k == 0), stop=False),
                            wait=[("gld", lo), ("vc", p.now("vc")), gate[pi]] if k == 0 else ())
                    tv = p.do("t", lambda t, dc=dc, ps=ps: t.matmul(
                        ps[:, :], self.PMRN[0:1, dc * 128:(dc + 1) * 128],
                        self.ones[0:1, 0:512], start=False, stop=True),
                        wait=(("gld", lbo),), inc="tc")
                    xs = self.xfm(b, dc)[:, nb4 * 512:(nb4 + 1) * 512]
                    ev = p.do("v", lambda v, xs=xs, ps=ps: v.tensor_add(xs, ps[:, :], xs),
                              wait=(("tc", tv),), inc="vc")
                    gate[pi] = ("vc", ev)
            self.xfm_ready = ("vc", p.now("vc"))

    def emit_l0ffn(self):
        p = self.p
        for b in range(NB):
            lnv = self.emit_ln(b, lambda c: self.Bar[:, c * 2048:(c + 1) * 2048], rms=True)
            p.do("g", lambda g: g.memset(self.PMRN[:, :], -1e30),
                 wait=(("vc", p.now("vc")),), inc="gc")
            pmg = p.now("gc")
            se_b = self.SL[:, 0:1024]
            sn_b = self.SL[:, 1024:2048]
            snf = self.DT[:, 2048:3072]
            pbf = self.PC[:, :].bitcast(BF16)
            ev_gate = {0: ("vc", 0), 1: ("vc", 0)}
            st_gate = {0: ("sc", 0), 1: ("sc", 0)}
            sp_gate = ("yld", 0)
            se_free = ("vc", 0)
            for mt in range(32):
                PP = [self.PL, self.PM][mt % 2]
                ppbf = PP[:, 1024:1536].bitcast(BF16)
                ld = p.do("g", lambda g, mt=mt: g.dma_start(
                    se_b, self.se[mt * 128:(mt + 1) * 128, :]),
                    wait=(se_free,), inc="gld", inc_amt=16)
                sv = p.do("v", lambda v: v.scalar_tensor_tensor(
                    sn_b, se_b, 1.0, se_b, ALU.mult, ALU.mult,
                    accum_out=self.ST[:, ST_MS1:ST_MS1 + 1]),
                    wait=(("gld", ld), ("yld", self.init_yld)), inc="vc")
                sq = p.do("s", lambda s: s.activation(
                    self.ST[:, ST_MS1 + 1:ST_MS1 + 2], self.ST[:, ST_MS1:ST_MS1 + 1],
                    AF.Sqrt), wait=(("vc", sv),), inc="sc")
                sv = p.do("v", lambda v: v.tensor_scalar_max(
                    self.ST[:, ST_MS1 + 1:ST_MS1 + 2], self.ST[:, ST_MS1 + 1:ST_MS1 + 2],
                    1e-12), wait=(("sc", sq),), inc="vc")
                sv = p.do("v", lambda v: v.reciprocal(
                    self.ST[:, ST_MS1 + 2:ST_MS1 + 3], self.ST[:, ST_MS1 + 1:ST_MS1 + 2]),
                    wait=(("vc", sv),), inc="vc")
                sv = p.do("v", lambda v: v.tensor_scalar(
                    sn_b, se_b, self.ST[:, ST_MS1 + 2:ST_MS1 + 3], 32.0,
                    ALU.mult, ALU.mult), wait=(("vc", sv),), inc="vc")
                se_free = ("vc", sv)
                tv = None
                for c in range(NCH):
                    tv = p.do("t", lambda t, c=c, ppbf=ppbf: t.transpose(
                        ppbf[:, c * 128:(c + 1) * 128], sn_b[:, c * 128:(c + 1) * 128],
                        self.ident[:, :]),
                        wait=[("vc", p.now("vc")), ("gc", self.init_gc),
                              st_gate[mt % 2], ev_gate[mt % 2]] if c == 0 else (),
                        inc="tc" if c == NCH - 1 else None)
                snv = None
                for c in range(NCH):
                    snv = p.do("v", lambda v, c=c, ppbf=ppbf: v.tensor_scalar_mul(
                        snf[:, c * 128:(c + 1) * 128], ppbf[:, c * 128:(c + 1) * 128],
                        self.ST[:, ST_GS + c:ST_GS + c + 1]),
                        wait=[("tc", tv)] if c == 0 else (), inc="vc")
                tv = None
                for nb4 in range(4):
                    for k in range(NCH):
                        tv = p.do("t", lambda t, k=k, nb4=nb4, PP=PP: t.matmul(
                            PP[:, nb4 * 512:(nb4 + 1) * 512],
                            snf[:, k * 128:(k + 1) * 128],
                            self.Bar[:, k * 2048 + nb4 * 512:k * 2048 + nb4 * 512 + 512],
                            start=(k == 0), stop=(k == NCH - 1)),
                            wait=[("vc", max(snv, lnv))] if k == 0 and nb4 == 0 else (),
                            inc="tc" if (k == NCH - 1 and nb4 == 3) else None)
                rv = p.do("v", lambda v, PP=PP: v.reduce_max(
                    self.ST[:, ST_MS1 + 3:ST_MS1 + 4], PP[:, :], axis=AX.X,
                    negate=True), wait=(("tc", tv), ("gc", pmg)), inc="vc")
                p.do("v", lambda v, mt=mt: v.tensor_scalar_mul(
                    self.ST[:, ST_RM + mt:ST_RM + mt + 1],
                    self.ST[:, ST_MS1 + 3:ST_MS1 + 4], -1.0),
                    wait=(("vc", rv),), inc="vc")
                pmv = p.do("v", lambda v, PP=PP: v.tensor_max(
                    self.PMRN[:, :], self.PMRN[:, :], PP[:, :]), inc="vc")
                ev_gate[mt % 2] = ("vc", pmv)
                sx = p.do("s", lambda s, mt=mt, PP=PP: s.activation(
                    self.DT[:, 0:2048], PP[:, :], AF.Exp,
                    bias=self.ST[:, ST_MS1 + 3:ST_MS1 + 4],
                    accum_out=self.ST[:, ST_CS + mt:ST_CS + mt + 1]),
                    wait=(("vc", rv),), inc="sc")
                sx = p.do("s", lambda s, PP=PP: s.activation(self.FSA[:, :], PP[:, :],
                                                             AF.Copy),
                          wait=(sp_gate,), inc="sc")
                st_gate[mt % 2] = ("sc", sx)
                yv = p.do("y", lambda y, b=b, mt=mt: y.dma_start(
                    self.Lsp[b, mt, :, :], self.FSA[:, :]),
                    wait=(("sc", sx),), inc="yld", inc_amt=16)
                sp_gate = ("yld", yv)
            st_gate = st_gate[31 % 2]
            sv = p.do("s", lambda s: s.activation(
                self.ST[:, ST_NAM:ST_NAM + 32], self.ST[:, ST_CS:ST_CS + 32], AF.Ln),
                wait=(("sc", p.now("sc")),), inc="sc")
            vv = p.do("v", lambda v: v.tensor_add(
                self.ST[:, ST_NAM:ST_NAM + 32], self.ST[:, ST_NAM:ST_NAM + 32],
                self.ST[:, ST_RM:ST_RM + 32]), wait=(("sc", sv),), inc="vc")
            vv = p.do("v", lambda v: v.tensor_scalar_mul(
                self.ST[:, ST_NAM:ST_NAM + 32], self.ST[:, ST_NAM:ST_NAM + 32], -1.0),
                wait=(("vc", vv),), inc="vc")
            nam_done = vv
            rnv = None
            for grp in range(4):
                tv = None
                for j in range(4):
                    k = grp * 4 + j
                    tv = p.do("t", lambda t, k=k, j=j: t.transpose(
                        pbf[:, j * 128:(j + 1) * 128],
                        self.PMRN[:, k * 128:(k + 1) * 128], self.ident[:, :]),
                        wait=[("vc", p.now("vc"))] if j == 0 else (),
                        inc="tc" if j == 3 else None)
                rnv = p.do("v", lambda v, grp=grp: v.reduce_max(
                    self.ST[:, ST_RN + grp * 4:ST_RN + (grp + 1) * 4],
                    pbf[:, 0:512].rearrange("p (a c) -> p a c", a=4), axis=AX.X),
                    wait=(("tc", tv),), inc="vc")
            cv = p.do("v", lambda v: v.tensor_copy(
                self.SL[:, 0:16], self.ST[:, ST_RN:ST_RN + 16]),
                wait=(("vc", rnv),), inc="vc")
            tv = p.do("t", lambda t: t.transpose(
                self.PL[:, :].bitcast(BF16)[0:16, 0:128], self.SL[:, 0:16],
                self.ident[:, :]), wait=(("vc", cv), st_gate), inc="tc")
            cv = p.do("v", lambda v: v.tensor_copy(
                self.SL[0:16, 128:256], self.PL[:, :].bitcast(BF16)[0:16, 0:128]),
                wait=(("tc", tv),), inc="vc")
            yv = p.do("y", lambda y: y.dma_start(
                self.bncb[0:1, :].rearrange("x (a c) -> (x a) c", c=128),
                self.SL[0:16, 128:256]), wait=(("vc", cv), sp_gate), inc="yld", inc_amt=16)
            bv = p.do("g", lambda g: g.dma_start(
                self.PMRN[:, :], bass.AP(self.bncb.tensor, 0, [[0, 128], [1, 2048]])),
                wait=(("yld", yv), ("vc", p.now("vc"))), inc="gld", inc_amt=16)
            rn_b_done = bv
            for k in range(16):
                tv = None
                for c in range(NCH):
                    tv = p.do("t", lambda t, k=k, c=c: t.transpose(
                        pbf[:, c * 128:(c + 1) * 128],
                        self.Bar[:, c * 2048 + k * 128:c * 2048 + (k + 1) * 128],
                        self.ident[:, :]),
                        wait=[("vc", p.now("vc")), ("tc", p.now("tc"))] if c == 0 else (),
                        inc="tc" if c == NCH - 1 else None)
                p.do("v", lambda v, k=k: v.tensor_copy(
                    self.A[:, k * 1024:(k + 1) * 1024], pbf[:, 0:1024]),
                    wait=(("tc", tv),), inc="vc")
            xntm_done = p.now("vc")
            dsl = [self.DT[:, 0:2048], self.DT[:, 2048:4096]]
            wslot = [self.WB[:, 0:4096], self.WB[:, 4096:8192]]
            dtv_gate = ("tc", 0)
            wgate = [("tc", 0), ("tc", 0)]
            eow = self.Bar[:, 12288:13312]
            eow_gate = ("yld", 0)
            b1gate = {0: ("sc", p.now("sc")), 1: ("sc", p.now("sc"))}
            b2gate = {0: ("vc", p.now("vc")), 1: ("vc", p.now("vc"))}
            for e in range(NE):
                b1s = ST_B1 if e % 2 == 0 else 128
                b2s = 0 if e % 2 == 0 else 1024
                lb = p.do("g", lambda g, e=e, b1s=b1s: g.dma_start(
                    self.ST[:, b1s:b1s + 32], self.b1c[e, :, :]),
                    wait=(b1gate[e % 2],), inc="gld", inc_amt=16)
                lb2 = p.do("g", lambda g, e=e, b2s=b2s: g.dma_start(
                    self.SL[0:1, b2s:b2s + 1024], self.b2r[e, :, :]),
                    wait=(b2gate[e % 2],), inc="gld", inc_amt=16)
                for ms in range(4):
                    mt = 4 * e + ms
                    lv = p.do("y", lambda y, b=b, mt=mt: y.dma_start(
                        self.FSA[:, :], self.Lsp[b, mt, :, :]),
                        wait=(st_gate, ("sc", p.now("sc"))), inc="yld", inc_amt=16)
                    dx = p.do("s", lambda s, ms=ms, mt=mt: s.activation(
                        dsl[ms % 2], self.FSA[:, :], AF.Exp,
                        bias=self.ST[:, ST_NAM + mt:ST_NAM + mt + 1]),
                        wait=(("yld", lv), ("vc", nam_done), dtv_gate), inc="sc")
                    st_gate = ("sc", dx)
                    for grp in range(2):
                        tv = None
                        for j in range(8):
                            k = grp * 8 + j
                            tv = p.do("t", lambda t, ms=ms, k=k, j=j: t.transpose(
                                pbf[:, j * 128:(j + 1) * 128],
                                dsl[ms % 2][:, k * 128:(k + 1) * 128], self.ident[:, :]),
                                wait=[("sc", dx), ("vc", p.now("vc"))] if j == 0 else (),
                                inc="tc" if j == 7 else None)
                        dtv_gate = ("tc", tv)
                        src3 = pbf[:, 0:1024].rearrange("p (k c) -> p k c", c=128)
                        dd = self.Bar[:, :].rearrange(
                            "p (k c) -> p k c", c=512)[:, grp * 8:(grp + 1) * 8,
                                                       ms * 128:(ms + 1) * 128]
                        p.do("v", lambda v, dd=dd, src3=src3: v.tensor_copy(dd, src3),
                             wait=(("tc", tv),), inc="vc")
                dtr_done = p.now("vc")
                sgate = {0: ("vc", 0), 1: ("vc", 0)}
                for ds in range(NCH):
                    ps = [self.PA, self.PB][ds % 2]
                    tv = None
                    for k in range(16):
                        tv = p.do("t", lambda t, ds=ds, k=k, ps=ps: t.matmul(
                            ps[:, :], self.A[:, k * 1024 + ds * 128:k * 1024 + (ds + 1) * 128],
                            self.Bar[:, :].rearrange("p (k c) -> p k c", c=512)[:, k, :],
                            start=(k == 0), stop=(k == 15)),
                            wait=[("vc", dtr_done), sgate[ds % 2]] if k == 0 else (),
                            inc="tc" if k == 15 else None)
                    sv = p.do("v", lambda v, ds=ds, ps=ps: v.tensor_copy(
                        self.Bar[:, 8192 + ds * 512:8192 + (ds + 1) * 512], ps[:, :]),
                        wait=(("tc", tv),), inc="vc")
                    sgate[ds % 2] = ("vc", sv)
                slots_done = p.now("vc")
                hgate = {0: ("sc", 0), 1: ("sc", 0)}
                for q8 in range(8):
                    wl = p.do("g", lambda g, e=e, q8=q8: g.dma_start(
                        wslot[q8 % 2].rearrange("q (k c) -> q k c", c=512),
                        self.w1[e, :, q8 * 512:(q8 + 1) * 512].rearrange(
                            "(k q) c -> q k c", q=128)),
                        wait=(wgate[q8 % 2],), inc="gld", inc_amt=16)
                    for ht in range(4):
                        ps = [self.PA, self.PB][ht % 2]
                        tv = None
                        for k in range(NCH):
                            tv = p.do("t", lambda t, q8=q8, ht=ht, k=k, ps=ps: t.matmul(
                                ps[:, :],
                                wslot[q8 % 2][:, k * 512 + ht * 128:k * 512 + (ht + 1) * 128],
                                self.Bar[:, 8192 + k * 512:8192 + (k + 1) * 512],
                                start=(k == 0), stop=(k == NCH - 1)),
                                wait=[("gld", wl), ("vc", slots_done),
                                      hgate[ht % 2]] if k == 0 else (),
                                inc="tc" if k == NCH - 1 else None)
                        hv = p.do("s", lambda s, q8=q8, ht=ht, ps=ps, b1s=b1s: s.activation(
                            self.A[:, 16384 + (q8 * 4 + ht) * 512:
                                   16384 + (q8 * 4 + ht + 1) * 512],
                            ps[:, :], AF.Gelu,
                            bias=self.ST[:, b1s + q8 * 4 + ht:b1s + q8 * 4 + ht + 1]),
                            wait=(("tc", tv), ("gld", lb)), inc="sc")
                        hgate[ht % 2] = ("sc", hv)
                    wgate[q8 % 2] = ("tc", p.now("tc"))
                h_done = p.now("sc")
                eops = [self.PL[:, 0:512], self.PL[:, 512:1024],
                        self.PL[:, 1024:1536], self.PL[:, 1536:2048],
                        self.PA[:, :], self.PB[:, :], self.PC[:, :], self.PD[:, :]]
                for hq in range(8):
                    wl = p.do("g", lambda g, e=e, hq=hq: g.dma_start(
                        wslot[hq % 2].rearrange("q (k c) -> q k c", c=1024),
                        self.w2[e, hq * 512:(hq + 1) * 512, :].rearrange(
                            "(k q) c -> q k c", q=128)),
                        wait=(wgate[hq % 2],), inc="gld", inc_amt=16)
                    for hc in range(4):
                        kk = hq * 4 + hc
                        for gi in range(8):
                            msub, dh = gi // 2, gi % 2
                            last = (hc == 3 and gi == 7)
                            w_ = ()
                            if hc == 0 and gi == 0:
                                w_ = [("gld", wl)] + (
                                    [("sc", h_done), ("vc", p.now("vc"))] if hq == 0 else [])
                            tq = p.do("t", lambda t, kk=kk, hc=hc, hq=hq, msub=msub, dh=dh, gi=gi: t.matmul(
                                eops[gi],
                                self.A[:, 16384 + kk * 512 + msub * 128:
                                       16384 + kk * 512 + (msub + 1) * 128],
                                wslot[hq % 2][:, hc * 1024 + dh * 512:hc * 1024 + (dh + 1) * 512],
                                start=(kk == 0), stop=False),
                                wait=w_, inc="tc" if last else None)
                    wgate[hq % 2] = ("tc", tq)
                tvb = None
                for gi in range(8):
                    msub, dh = gi // 2, gi % 2
                    tvb = p.do("t", lambda t, msub=msub, dh=dh, gi=gi, b2s=b2s: t.matmul(
                        eops[gi], self.ones[0:1, 0:128],
                        self.SL[0:1, b2s + dh * 512:b2s + (dh + 1) * 512],
                        start=False, stop=True),
                        wait=(("gld", lb2),) if gi == 0 else (),
                        inc="tc" if gi == 7 else None)

                b1gate[e % 2] = ("sc", p.now("sc"))
                b2gate[e % 2] = ("tc", tvb)
                for msub in range(4):
                    mt = 4 * e + msub
                    ev = None
                    for dh in range(2):
                        ev = p.do("v", lambda v, msub=msub, dh=dh: v.tensor_copy(
                            eow[:, dh * 512:(dh + 1) * 512], eops[msub * 2 + dh]),
                            wait=[("tc", tvb), eow_gate] if dh == 0 else (),
                            inc="vc")
                    yv = p.do("y", lambda y, b=b, mt=mt: y.dma_start(
                        self.eod[b, mt * 128:(mt + 1) * 128, :], eow[:, :]),
                        wait=(("vc", ev),), inc="yld", inc_amt=16)
                    eow_gate = ("yld", yv)
            craw = self.Bar
            rrb = self.SL[:, 0:512]
            el = None
            for mt in range(32):
                el = p.do("g", lambda g, b=b, mt=mt: g.dma_start(
                    self.A[:, mt * 1024:(mt + 1) * 1024],
                    self.eod[b, mt * 128:(mt + 1) * 128, :]),
                    wait=[("tc", p.now("tc")), ("sc", p.now("sc")), eow_gate,
                          ("vc", p.now("vc"))] if mt == 0 else (),
                    inc="gld", inc_amt=16)
            eo_done = el
            lgate = ("vc", 0)
            hgate = {0: ("sc", 0), 1: ("sc", 0)}
            for nb4 in range(4):
                lgate = ("vc", p.now("vc")) if nb4 else lgate
                tgate = ("tc", p.now("tc")) if nb4 else ("tc", 0)
                for g8 in range(16):
                    mt0 = g8 * 2
                    half = g8 % 2
                    fsl = self.FSA[:, half * 1024:half * 1024 + 1024]
                    dst3 = fsl.rearrange("p (a c) -> p a c", c=512)
                    src3 = self.Lsp[b, mt0:mt0 + 2, :, nb4 * 512:(nb4 + 1) * 512] \
                        .rearrange("a p c -> p a c")
                    lv = p.do("y", lambda y, dst3=dst3, src3=src3: y.dma_start(dst3, src3),
                        wait=(lgate, hgate[half], eow_gate) if g8 <= 1
                             else (hgate[half],),
                        inc="yld", inc_amt=16)
                    sv = None
                    for a2 in range(2):
                        sv = p.do("v", lambda v, fsl=fsl, a2=a2, nb4=nb4: v.tensor_sub(
                            fsl[:, a2 * 512:(a2 + 1) * 512], fsl[:, a2 * 512:(a2 + 1) * 512],
                            self.PMRN[:, nb4 * 512:(nb4 + 1) * 512]),
                            wait=[("yld", lv), ("gld", rn_b_done)] if a2 == 0 else (),
                            inc="vc")
                    cx = p.do("s", lambda s, mt0=mt0, fsl=fsl: s.activation(
                        craw[:, mt0 * 512:(mt0 + 2) * 512], fsl, AF.Exp),
                        wait=(("vc", sv), tgate) if g8 == 0 else (("vc", sv),),
                        inc="sc")
                    hgate[half] = ("sc", cx)
                cr_done = p.now("sc")
                tv = None
                for mt in range(32):
                    tv = p.do("t", lambda t, mt=mt: t.matmul(
                        self.PA[0:1, :], self.ones[:, 0:1],
                        craw[:, mt * 512:(mt + 1) * 512],
                        start=(mt == 0), stop=(mt == 31)),
                        wait=[("sc", cr_done), ("vc", p.now("vc"))] if mt == 0 else (),
                        inc="tc" if mt == 31 else None)
                rv = p.do("v", lambda v: v.reciprocal(self.FSA[0:1, 1024:1536],
                                                      self.PA[0:1, :]),
                          wait=(("tc", tv),), inc="vc")
                rv = p.do("v", lambda v: v.tensor_copy(self.SL[0:1, 512:1024],
                                                       self.FSA[0:1, 1024:1536]),
                          wait=(("vc", rv),), inc="vc")
                tv = p.do("t", lambda t: t.matmul(
                    self.PB[:, :], self.ones[0:1, 0:128], self.SL[0:1, 512:1024],
                    start=True, stop=True), wait=(("vc", rv),), inc="tc")
                rv = p.do("v", lambda v: v.tensor_copy(rrb, self.PB[:, :]),
                          wait=(("tc", tv),), inc="vc")
                rrb_done = rv
                dgate = {0: ("vc", 0), 1: ("vc", 0)}
                for dc in range(NCH):
                    pi = dc % 2
                    ps = [self.PC, self.PD][pi]
                    tv = None
                    for mt in range(32):
                        tv = p.do("t", lambda t, dc=dc, mt=mt, ps=ps: t.matmul(
                            ps[:, :], self.A[:, mt * 1024 + dc * 128:mt * 1024 + (dc + 1) * 128],
                            craw[:, mt * 512:(mt + 1) * 512],
                            start=(mt == 0), stop=(mt == 31)),
                            wait=[("gld", eo_done), ("sc", cr_done),
                                  dgate[pi]] if mt == 0 else (),
                            inc="tc" if mt == 31 else None)
                    fs2 = self.FSA[:, (dc % 2) * 512:(dc % 2) * 512 + 512]
                    ev = p.do("v", lambda v, fs2=fs2, ps=ps: v.tensor_mul(fs2, ps[:, :], rrb),
                              wait=(("tc", tv), ("vc", rrb_done)), inc="vc")
                    xs = self.xfm(b, dc)[:, nb4 * 512:(nb4 + 1) * 512]
                    ev = p.do("v", lambda v, xs=xs, fs2=fs2: v.scalar_tensor_tensor(
                        xs, fs2, 0.5, xs, ALU.mult, ALU.add), inc="vc")
                    dgate[pi] = ("vc", ev)
            self.xfm_ready = ("vc", p.now("vc"))


def build_program(parts):
    nc = bass.Bass("TRN2", target_bir_lowering=False)
    ctx = ExitStack()
    with ctx:
        k = Kern(nc, ctx, parts)
        k.declare_io()
        k.alloc()
        block = ctx.enter_context(nc.Block())
        k.emit_init()
        k.emit_load_x()
        for part in parts:
            if part == "l0ffn":
                k.emit_l0ffn()
            elif part == "l0attn":
                k.emit_l0attn()
            elif part == "l1attn":
                k.emit_l1attn()
            elif part == "l1ffn_a":
                k.emit_l1ffn(first=True)
            elif part == "l1ffn_b":
                k.emit_l1ffn(first=False)
        dbg = os.environ.get("KDBG", "")
        if dbg:
            p = k.p
            srcs = {"A": k.A, "Bar": k.Bar, "PMRN": k.PMRN, "DT": k.DT,
                    "SL": k.SL, "WB": k.WB, "XFM": k.XFM, "ST": k.ST}[dbg.split(":")[0]]
            off = int(dbg.split(":")[1]) if ":" in dbg else 0
            nmax = srcs.shape[1]
            w = min(16384, nmax - off)
            cw = min(w, 1024)
            dst0 = k.y_out[0, :, :].rearrange("(a p) c -> p a c", p=128)[:, 0:max(w // cw, 1), 0:cw]
            src3 = srcs[:, off:off + w].rearrange("p (a c) -> p a c", c=cw)
            p.do("g", lambda g: g.dma_start(dst0, src3),
                 wait=(("vc", p.now("vc")), ("sc", p.now("sc")),
                       ("tc", p.now("tc"))), inc="gld", inc_amt=16)
            p.do("g", lambda g: None, wait=(("gld", p.now("gld")),))
        if not dbg:
            k.emit_store_y()
        k.p.build(block)
    return nc


LAUNCH_PLAN = [
    ["l0ffn", "l0attn", "l0ffn", "l1ffn_a", "l1attn", "l1ffn_b"],
]


def _norm_parts(parts):
    return ["l1ffn" if p.startswith("l1ffn") else p for p in parts]


def prep_inputs(inp):
    f = np.ascontiguousarray
    d = {}
    d["w1"] = f(inp["L0_ffn_w1"])
    d["w2"] = f(inp["L0_ffn_w2"])
    d["se"] = f(inp["L0_ffn_slot_embeds"].reshape(4096, 1024))
    d["gnc"] = f(inp["L0_ffn_norm_gamma"].reshape(8, 128).T)
    d["gsc"] = f(inp["L0_ffn_slot_norm_gamma"].reshape(8, 128).T)
    d["b1c"] = f(inp["L0_ffn_b1"].reshape(8, 32, 128).transpose(0, 2, 1))
    d["b2r"] = f(inp["L0_ffn_b2"].reshape(8, 1, 1024))
    d["u_n"] = f(inp["L0_attn_ternary_factor_u"])
    d["uT"] = f(inp["L0_attn_ternary_factor_u"].T)
    d["v_n"] = f(inp["L0_attn_ternary_factor_v"])
    d["vT"] = f(inp["L0_attn_ternary_factor_v"].T)
    rows, cols = np.tril_indices(64)
    tril = np.zeros((16, 64, 64), dtype=np.float32)
    tril[:, rows, cols] = inp["L1_attn_params"]
    d["trilT"] = f(tril.transpose(0, 2, 1))
    d["WvT"] = f(inp["L1_attn_Wv"].T)
    d["bvr"] = f(inp["L1_attn_bv"].reshape(1, 1024))
    d["WoT"] = f(inp["L1_attn_Wo"].T)
    d["bor"] = f(inp["L1_attn_bo"].reshape(1, 1024))
    g1r = inp["L1_ffn_G1_R"][0].transpose(1, 0, 2).reshape(4, 1024)
    g1i = inp["L1_ffn_G1_I"][0].transpose(1, 0, 2).reshape(4, 1024)
    g2r = inp["L1_ffn_G2_R"][:, :, 0, :].reshape(4, 256)
    g2i = inp["L1_ffn_G2_I"][:, :, 0, :].reshape(4, 256)
    d["G1r"] = f(np.concatenate([g1r, g1i, g2r, g2i], axis=1))
    d["brr"] = f(inp["L1_ffn_bias_real"].reshape(1, 512))
    d["bir"] = f(inp["L1_ffn_bias_imag"].reshape(1, 512))
    return d


def kernel(**inputs):
    x = np.ascontiguousarray(np.asarray(inputs["L0_attn_qz"], dtype=np.float32))
    wd = prep_inputs({k: np.asarray(v, dtype=np.float32) for k, v in inputs.items()})
    shards = [x[2 * i:2 * i + 2] for i in range(8)]
    progs = {}
    for parts in LAUNCH_PLAN:
        key = tuple(parts)
        if key not in progs:
            progs[key] = build_program(parts)
    cur = shards
    for parts in LAUNCH_PLAN:
        nc = progs[tuple(parts)]
        needed = {"l0ffn": ["w1", "w2", "se", "gnc", "gsc", "b1c", "b2r"],
                  "l0attn": ["u_n", "uT", "v_n", "vT"],
                  "l1attn": ["trilT", "WvT", "bvr", "WoT", "bor"],
                  "l1ffn": ["G1r", "brr", "bir"]}
        names = set()
        for pp_ in _norm_parts(parts):
            names.update(needed[pp_])
        in_maps = []
        for ci in range(8):
            m = {"x": cur[ci]}
            for nm in names:
                m[nm] = wd[nm]
            in_maps.append(m)
        res = run_bass_kernel_spmd(nc, in_maps, core_ids=list(range(8)))
        cur = [np.asarray(r["y"]) for r in res.results]
    return np.concatenate(cur, axis=0)


# version 06
# speedup vs baseline: 1.2564x; 1.0566x over previous
import os
import numpy as np
from contextlib import ExitStack
import concourse.bass as bass
import concourse.mybir as mybir
from concourse.bass_utils import run_bass_kernel_spmd

F32 = mybir.dt.float32
BF16 = mybir.dt.bfloat16
AF = mybir.ActivationFunctionType
ALU = mybir.AluOpType
AX = mybir.AxisListType

NB = 2
NT = 2048
DM = 1024
NCH = 8
NE = 8
HID = 4096

ST_RM = 0
ST_CS = 32
ST_NAM = 64
ST_RN = 96
ST_MS1 = 112
ST_NMX = 128
ST_RS = 144
ST_RR = 160
ST_B1 = 176
ST_GN = 208
ST_GS = 216
ST_MS2 = 224


class P:
    ENGS = ("g", "t", "v", "s", "y")

    def __init__(self, nc, ctx):
        self.nc = nc
        self.ops = {e: [] for e in self.ENGS}
        self.cnt = {}
        self.sems = {}
        for name in ("gld", "yld", "gc", "tc", "vc", "sc"):
            self.sems[name] = ctx.enter_context(nc.semaphore(name))
            self.cnt[name] = 0

    def do(self, eng, fn, wait=(), inc=None, inc_amt=1):
        wait = tuple((s, v) for (s, v) in wait if v > 0)
        for s, v in wait:
            assert v <= self.cnt[s], f"wait {s}>={v} but planned only {self.cnt[s]}"
        if inc is not None:
            self.cnt[inc] += inc_amt
        sems = self.sems

        def emit(e, fn=fn, wait=wait, inc=inc, inc_amt=inc_amt):
            for s, v in wait:
                e.wait_ge(sems[s], v)
            inst = fn(e)
            if inc is not None:
                assert inst is not None
                inst.then_inc(sems[inc], inc_amt)
        self.ops[eng].append(emit)
        return self.cnt[inc] if inc is not None else None

    def now(self, name):
        return self.cnt[name]

    def snap(self):
        return dict(self.cnt)

    def build(self, block):
        ops = self.ops

        @block.gpsimd
        def _(g):
            for f in ops["g"]:
                f(g)

        @block.tensor
        def _(t):
            for f in ops["t"]:
                f(t)

        @block.vector
        def _(v):
            for f in ops["v"]:
                f(v)

        @block.scalar
        def _(s):
            for f in ops["s"]:
                f(s)

        @block.sync
        def _(y):
            for f in ops["y"]:
                f(y)


class Kern:
    def __init__(self, nc, ctx, parts):
        self.nc = nc
        self.ctx = ctx
        self.parts = list(parts)
        self.p = P(nc, ctx)
        self.xfm_ready = ("vc", 0)

    def declare_io(self):
        nc = self.nc
        def ap(name, shape, dt, kind):
            return nc.dram_tensor(name, shape, dt, kind=kind).ap()
        I, O, N = "ExternalInput", "ExternalOutput", "Internal"
        self.x_in = ap("x", [NB, NT, DM], F32, I)
        self.y_out = ap("y", [NB, NT, DM], F32, O)
        pset = {"l1ffn" if q.startswith("l1ffn") else q for q in self.parts}
        if "l0ffn" in pset:
            self.w1 = ap("w1", [NE, DM, HID], F32, I)
            self.w2 = ap("w2", [NE, HID, DM], F32, I)
            self.se = ap("se", [NE * 512, DM], F32, I)
            self.gnc = ap("gnc", [128, NCH], F32, I)
            self.gsc = ap("gsc", [128, NCH], F32, I)
            self.b1c = ap("b1c", [NE, 128, 32], F32, I)
            self.b2r = ap("b2r", [NE, 1, DM], F32, I)
            self.Lsp = ap("Lsp", [NB, 32, 128, NT], F32, N)
            self.eod = ap("eod", [NB, NE * 512, DM], BF16, N)
        if "l0attn" in pset:
            self.u_n = ap("u_n", [DM, DM], F32, I)
            self.uT = ap("uT", [DM, DM], F32, I)
            self.v_n = ap("v_n", [DM, DM], F32, I)
            self.vT = ap("vT", [DM, DM], F32, I)
        if "l1attn" in pset:
            self.trilT = ap("trilT", [16, 64, 64], F32, I)
            self.WvT = ap("WvT", [DM, DM], F32, I)
            self.bvr = ap("bvr", [1, DM], F32, I)
            self.WoT = ap("WoT", [DM, DM], F32, I)
            self.bor = ap("bor", [1, DM], F32, I)
        if "l1ffn" in pset:
            self.G1r = ap("G1r", [4, 2560], F32, I)
            self.brr = ap("brr", [1, 512], F32, I)
            self.bir = ap("bir", [1, 512], F32, I)
            self.pdr = ap("pdr", [1024, 256], F32, N)
            self.wts = ap("wts", [128, 6144], BF16, N)
        self.bncb = ap("bncb", [1, 2048], BF16, N)

    def alloc(self):
        nc, E = self.nc, self.ctx.enter_context
        self.XFM = E(nc.sbuf_tensor("XFM", [128, NB * NCH * NT], BF16))
        self.A = E(nc.sbuf_tensor("Abuf", [128, 32768], BF16))
        self.Bar = E(nc.sbuf_tensor("Bar", [128, 16384], BF16))
        self.DT = E(nc.sbuf_tensor("DT", [128, 6144], BF16))
        self.WB = E(nc.sbuf_tensor("WB", [128, 8192], BF16))
        self.SL = E(nc.sbuf_tensor("SL", [128, 2048], BF16))
        self.FSA = E(nc.sbuf_tensor("FSA", [128, 2048], F32))
        self.PMRN = E(nc.sbuf_tensor("PMRN", [128, 2048], BF16))
        self.ST = E(nc.sbuf_tensor("STT", [128, 256], F32))
        self.ident = E(nc.sbuf_tensor("ident", [128, 128], BF16))
        self.ones = E(nc.sbuf_tensor("onesb", [128, 512], BF16))
        self.PL = E(nc.psum_tensor("PL", [128, 2048], F32))
        self.PM = E(nc.psum_tensor("PM", [128, 2048], F32))
        self.PA = self.PM[:, 0:512]
        self.PB = self.PM[:, 512:1024]
        self.PC = self.PM[:, 1024:1536]
        self.PD = self.PM[:, 1536:2048]

    def xfm(self, b, c):
        return self.XFM[:, (b * NCH + c) * NT:(b * NCH + c) * NT + NT]

    def emit_init(self):
        p = self.p
        p.do("g", lambda g: g.memset(self.ones[:, :], 1.0), inc="gc")
        p.do("g", lambda g: g.memset(self.ident[:, :], 0.0), inc="gc")
        v = p.now("gc")
        p.do("g", lambda g: g.affine_select(
            out=self.ident[:, :], in_=self.ident[:, :],
            compare_op=ALU.not_equal, fill=1.0, base=0,
            pattern=[[-1, 128]], channel_multiplier=1), wait=(("gc", v),), inc="gc")
        p.do("g", lambda g: g.memset(self.ST[:, ST_MS2 + 4:ST_MS2 + 5], 1e-5),
             inc="gc")
        if "l0ffn" in self.parts:
            p.do("y", lambda y: y.dma_start(self.ST[:, ST_GN:ST_GN + 8], self.gnc[:, :]),
                 inc="yld", inc_amt=16)
            p.do("y", lambda y: y.dma_start(self.ST[:, ST_GS:ST_GS + 8], self.gsc[:, :]),
                 inc="yld", inc_amt=16)
        self.init_gc = p.now("gc")
        self.init_yld = p.now("yld")

    def emit_load_x(self):
        p = self.p
        xin = self.SL[:, 0:1024]
        pbf = self.PL[:, 0:512].bitcast(BF16)
        xfm3 = self.XFM[:, :].rearrange("p (k n) -> p k n", n=NT)
        for b in range(NB):
            for k in range(16):
                ld = p.do("g", lambda g, b=b, k=k: g.dma_start(
                    xin, self.x_in[b, k * 128:(k + 1) * 128, :]),
                    wait=(("tc", p.now("tc")),), inc="gld", inc_amt=16)
                tw = [("gld", ld), ("gc", self.init_gc), ("vc", p.now("vc"))]
                tcv = None
                for c in range(NCH):
                    tcv = p.do("t", lambda t, c=c: t.transpose(
                        pbf[:, c * 128:(c + 1) * 128], xin[:, c * 128:(c + 1) * 128],
                        self.ident[:, :]),
                        wait=tw if c == 0 else (), inc="tc" if c == NCH - 1 else None)
                dst = xfm3[:, b * NCH:(b + 1) * NCH, k * 128:k * 128 + 128]
                src = pbf[:, :].rearrange("p (c j) -> p c j", j=128)
                p.do("v", lambda v, dst=dst, src=src: v.tensor_copy(dst, src),
                     wait=(("tc", tcv),), inc="vc")
        self.xfm_ready = ("vc", p.now("vc"))

    def emit_store_y(self):
        p = self.p
        xout = self.FSA[:, 0:1024]
        pbf = self.PL[:, 0:512].bitcast(BF16)
        xfm3 = self.XFM[:, :].rearrange("p (k n) -> p k n", n=NT)
        rs, rv = self.xfm_ready
        for b in range(NB):
            for k in range(16):
                src3 = xfm3[:, b * NCH:(b + 1) * NCH, k * 128:k * 128 + 128]
                tcv = None
                for c in range(NCH):
                    tw = [(rs, rv), ("vc", p.now("vc")), ("yld", p.now("yld"))] if c == 0 else []
                    tcv = p.do("t", lambda t, c=c, src3=src3: t.transpose(
                        pbf[:, c * 128:(c + 1) * 128], src3[:, c, :], self.ident[:, :]),
                        wait=tw, inc="tc" if c == NCH - 1 else None)
                p.do("v", lambda v: v.tensor_copy(xout, pbf[:, :]),
                     wait=(("tc", tcv),), inc="vc")
                p.do("y", lambda y, b=b, k=k: y.dma_start(
                    self.y_out[b, k * 128:(k + 1) * 128, :], xout),
                    wait=(("vc", p.now("vc")),), inc="yld", inc_amt=16)
        p.do("y", lambda y: None, wait=(("yld", p.now("yld")),))

    def emit_ln(self, b, dst, rms=False):
        p = self.p
        sq = [self.SL[:, 0:512], self.SL[:, 512:1024]]
        rs, rv = self.xfm_ready
        r0 = lambda q: self.FSA[0:1, 0:512]
        r1 = lambda q: self.FSA[0:1, 512:1024]
        r2 = lambda q: self.FSA[0:1, 1024:1536]
        r3 = lambda q: self.FSA[0:1, 1536:2048]
        TMPB = self.A[:, 0:2048]
        SRNB = self.A[:, 2048:4096]
        MSRB = self.A[:, 4096:6144]

        for q in range(4):
            n0 = q * 512
            wA = [(rs, rv), ("vc", p.now("vc")), ("sc", p.now("sc")),
                  ("yld", p.now("yld"))]
            sqv = []
            for c in range(2):
                sqv.append(p.do("s", lambda s, c=c, n0=n0: s.activation(
                    sq[c % 2], self.xfm(b, c)[:, n0:n0 + 512], AF.Square),
                    wait=[(rs, rv), ("tc", p.now("tc")), ("yld", p.now("yld")),
                          ("gld", p.now("gld"))] if c == 0 else (), inc="sc"))
            tsum = None
            for c in range(NCH):
                tsum = p.do("t", lambda t, c=c, n0=n0: t.matmul(
                    self.PA[0:1, :], self.ones[:, 0:1], self.xfm(b, c)[:, n0:n0 + 512],
                    start=(c == 0), stop=(c == NCH - 1)),
                    wait=wA if c == 0 else (), inc="tc" if c == NCH - 1 else None)
            tssq = None
            for c in range(NCH):
                tssq = p.do("t", lambda t, c=c: t.matmul(
                    self.PB[0:1, :], self.ones[:, 0:1], sq[c % 2],
                    start=(c == 0), stop=(c == NCH - 1)),
                    wait=[("sc", sqv[c]), ("vc", p.now("vc"))] if c == 0 else
                         [("sc", sqv[c])], inc="tc")
                if c + 2 < NCH:
                    sqv.append(p.do("s", lambda s, c=c, n0=n0: s.activation(
                        sq[c % 2], self.xfm(b, c + 2)[:, n0:n0 + 512], AF.Square),
                        wait=(("tc", tssq),), inc="sc"))
            p.do("v", lambda v, q=q: v.tensor_copy(r0(q), self.PA[0:1, :]),
                 wait=(("tc", p.now("tc")), ("sc", p.now("sc"))), inc="vc")
            p.do("v", lambda v, q=q: v.tensor_copy(r1(q), self.PB[0:1, :]), inc="vc")
            p.do("v", lambda v, q=q: v.tensor_scalar_mul(r2(q), r0(q), 1.0 / DM),
                 inc="vc")
            p.do("v", lambda v, q=q: v.tensor_scalar_mul(r1(q), r1(q), 1.0 / DM),
                 inc="vc")
            p.do("v", lambda v, q=q: v.tensor_mul(r3(q), r2(q), r2(q)), inc="vc")
            p.do("v", lambda v, q=q: v.tensor_sub(r1(q), r1(q), r3(q)), inc="vc")
            vv = p.now("vc")
            sv = p.do("s", lambda s, q=q: s.activation(
                r3(q), r1(q), AF.Sqrt, bias=self.ST[0:1, ST_MS2 + 4:ST_MS2 + 5]),
                wait=(("vc", vv), ("gc", self.init_gc)), inc="sc")
            p.do("v", lambda v, q=q: v.reciprocal(r0(q), r3(q)),
                 wait=(("sc", sv),), inc="vc")
            if rms:
                sv = p.do("s", lambda s, q=q: s.activation(r3(q), r1(q), AF.Sqrt),
                          wait=(("vc", p.now("vc")),), inc="sc")
                p.do("v", lambda v, q=q: v.tensor_mul(r3(q), r3(q), r0(q)),
                     wait=(("sc", sv),), inc="vc")
                p.do("v", lambda v, q=q: v.tensor_scalar(r3(q), r3(q), 32.0, 1e-12,
                                                         ALU.mult, ALU.max), inc="vc")
                p.do("v", lambda v, q=q: v.reciprocal(r1(q), r3(q)), inc="vc")
                p.do("v", lambda v, q=q: v.tensor_mul(r0(q), r0(q), r1(q)), inc="vc")
                p.do("v", lambda v, q=q: v.tensor_scalar_mul(r0(q), r0(q), 32.0),
                     inc="vc")
            p.do("v", lambda v, q=q: v.tensor_mul(r2(q), r2(q), r0(q)), inc="vc")
            p.do("v", lambda v, q=q: v.tensor_copy(self.SL[0:1, 1024:1536], r0(q)),
                 inc="vc")
            p.do("v", lambda v, q=q: v.tensor_copy(self.SL[0:1, 1536:2048], r2(q)),
                 inc="vc")
            vrow = p.now("vc")
            p.do("t", lambda t, q=q: t.matmul(
                self.PA[:, :], self.ones[0:1, 0:128], self.SL[0:1, 1024:1536],
                start=True, stop=True),
                wait=[("vc", vrow)], inc="tc")
            tb = p.do("t", lambda t, q=q: t.matmul(
                self.PB[:, :], self.ones[0:1, 0:128], self.SL[0:1, 1536:2048],
                start=True, stop=True),
                inc="tc")
            p.do("v", lambda v, q=q: v.tensor_copy(SRNB[:, q * 512:(q + 1) * 512],
                                                   self.PA[:, :]),
                 wait=[("tc", tb)] + ([("tc", p.now("tc")), ("sc", p.now("sc"))]
                                      if q == 0 else []), inc="vc")
            p.do("v", lambda v, q=q: v.tensor_copy(MSRB[:, q * 512:(q + 1) * 512],
                                                   self.PB[:, :]), inc="vc")
        vbc = p.now("vc")
        for c in range(NCH):
            p.do("v", lambda v, c=c: v.tensor_mul(TMPB, self.xfm(b, c), SRNB),
                 wait=[("vc", vbc), ("yld", self.init_yld),
                       ("tc", p.now("tc")), ("sc", p.now("sc"))] if c == 0 else (),
                 inc="vc")
            p.do("v", lambda v, c=c: v.tensor_sub(dst(c), TMPB, MSRB), inc="vc")
            if rms:
                p.do("v", lambda v, c=c: v.tensor_scalar_mul(
                    dst(c), dst(c), self.ST[:, ST_GN + c:ST_GN + c + 1]), inc="vc")
        return p.now("vc")

    def emit_l1ffn(self, first):
        p = self.p
        WRT = self.DT[:, 0:2048]
        WIT = self.DT[:, 2048:4096]
        NIT = self.DT[:, 4096:6144]
        if first:
            g1 = self.WB[0:4, 0:2560]
            l1 = p.do("g", lambda g: g.dma_start(g1, self.G1r[:, :]),
                      wait=(("vc", p.now("vc")), ("sc", p.now("sc"))),
                      inc="gld", inc_amt=16)
            fs = self.FSA[:, 0:512]
            wn = self.WB[:, 2560:4608]
            pbf = self.PC[:, :].bitcast(BF16)
            gq_prev = p.now("gld")
            for ri in range(2):
                for ms in range(8):
                    tv = p.do("t", lambda t, ri=ri, ms=ms: t.matmul(
                        self.PA[:, 0:256],
                        g1[:, ri * 1024 + ms * 128:ri * 1024 + (ms + 1) * 128],
                        g1[:, 2048 + ri * 256:2048 + ri * 256 + 256],
                        start=True, stop=True),
                        wait=[("gld", max(l1, gq_prev)), ("vc", p.now("vc")),
                              ("yld", p.now("yld"))], inc="tc")
                    vv = p.do("v", lambda v: v.tensor_copy(fs[:, 0:256], self.PA[:, 0:256]),
                              wait=(("tc", tv), ("yld", p.now("yld"))), inc="vc")
                    p.do("y", lambda y, ms=ms: y.dma_start(
                        self.pdr[ms * 128:(ms + 1) * 128, :], fs[:, 0:256]),
                        wait=(("vc", vv),), inc="yld", inc_amt=16)
                yv = p.now("yld")
                lw = None
                for b32 in range(32):
                    rc = b32 // 8
                    dst = wn[(b32 % 8) * 16:(b32 % 8) * 16 + 16,
                             rc * 512:(rc + 1) * 512].rearrange("p (d g) -> p d g", g=16)
                    src2 = self.pdr[b32 * 32:(b32 + 1) * 32, :].rearrange(
                        "d (e g) -> e d g", g=16)
                    lw = p.do("g", lambda g, dst=dst, src2=src2: g.dma_start(dst, src2),
                              wait=[("yld", yv), ("tc", p.now("tc"))] if b32 == 0 else (),
                              inc="gld", inc_amt=16)
                gq_prev = lw
                dstT = WRT if ri == 0 else WIT
                for grp in range(2):
                    tv = None
                    for j in range(8):
                        kk = grp * 8 + j
                        rc, jc = kk // 4, kk % 4
                        tv = p.do("t", lambda t, rc=rc, jc=jc, j=j: t.transpose(
                            pbf[:, j * 128:(j + 1) * 128],
                            wn[:, rc * 512 + jc * 128:rc * 512 + (jc + 1) * 128],
                            self.ident[:, :]),
                            wait=[("gld", lw), ("vc", p.now("vc"))] if j == 0 else (),
                            inc="tc" if j == 7 else None)
                    for j in range(8):
                        kk = grp * 8 + j
                        rc, jc = kk // 4, kk % 4
                        p.do("v", lambda v, rc=rc, jc=jc, j=j, dstT=dstT: v.tensor_copy(
                            dstT[:, jc * 512 + rc * 128:jc * 512 + (rc + 1) * 128],
                            pbf[:, j * 128:(j + 1) * 128]),
                            wait=[("tc", tv)] if j == 0 else (), inc="vc")
            p.do("v", lambda v: v.tensor_scalar_mul(NIT, WIT, -1.0), inc="vc")
            wv = p.now("vc")
            p.do("y", lambda y: y.dma_start(self.wts[:, :], self.DT[:, :]),
                 wait=(("vc", wv),), inc="yld", inc_amt=16)
            self.l1ffn_w_done = p.now("yld")
            wgate = ("vc", wv)
        else:
            lw = p.do("g", lambda g: g.dma_start(self.DT[:, :], self.wts[:, :]),
                      wait=(("vc", p.now("vc")), ("sc", p.now("sc")),
                            ("yld", self.l1ffn_w_done)),
                      inc="gld", inc_amt=16)
            wgate = ("gld", lw)

        lbr = p.do("g", lambda g: g.dma_start(self.PMRN[0:1, 0:512], self.brr[:, :]),
                   wait=(("tc", p.now("tc")), ("sc", p.now("sc")),
                         ("vc", p.now("vc"))), inc="gld", inc_amt=16)
        lbr = p.do("g", lambda g: g.dma_start(self.PMRN[0:1, 512:1024], self.bir[:, :]),
                   inc="gld", inc_amt=16)
        for b in range(NB):
            lnv = self.emit_ln(b, lambda c: self.Bar[:, c * 2048:(c + 1) * 2048])
            pp = [self.PA, self.PB]
            gate = {0: ("vc", 0), 1: ("vc", 0)}
            for oc in range(NCH):
                real = oc < 4
                om = oc % 4
                for nb4 in range(4):
                    pi = (oc * 4 + nb4) % 2
                    ps = pp[pi]
                    bias_row = (self.PMRN[0:1, 0:512] if real
                                else self.PMRN[0:1, 512:1024])
                    wa, wb_ = (WRT, NIT) if real else (WIT, WRT)
                    for k in range(4):
                        p.do("t", lambda t, k=k, om=om, nb4=nb4, wa=wa, ps=ps: t.matmul(
                            ps[:, :], wa[:, k * 512 + om * 128:k * 512 + (om + 1) * 128],
                            self.Bar[:, k * 2048 + nb4 * 512:k * 2048 + nb4 * 512 + 512],
                            start=(k == 0), stop=False),
                            wait=[("vc", lnv), wgate, gate[pi]] if k == 0 else ())
                    for k in range(4):
                        p.do("t", lambda t, k=k, om=om, nb4=nb4, wb_=wb_, ps=ps: t.matmul(
                            ps[:, :], wb_[:, k * 512 + om * 128:k * 512 + (om + 1) * 128],
                            self.Bar[:, (4 + k) * 2048 + nb4 * 512:(4 + k) * 2048 + nb4 * 512 + 512],
                            start=False, stop=False))
                    tv = p.do("t", lambda t, om=om, ps=ps, bias_row=bias_row: t.matmul(
                        ps[:, :], bias_row[:, om * 128:(om + 1) * 128],
                        self.ones[0:1, 0:512], start=False, stop=True),
                        wait=(("gld", lbr),), inc="tc")
                    xs = self.xfm(b, oc)[:, nb4 * 512:(nb4 + 1) * 512]
                    ev = p.do("v", lambda v, xs=xs, ps=ps: v.scalar_tensor_tensor(
                        xs, ps[:, :], 0.5, xs, ALU.mult, ALU.add),
                        wait=(("tc", tv),), inc="vc")
                    gate[pi] = ("vc", ev)
            self.xfm_ready = ("vc", p.now("vc"))

    def emit_l0attn(self):
        p = self.p
        for b in range(NB):
            lnv = self.emit_ln(b, lambda c: self.Bar[:, c * 2048:(c + 1) * 2048])
            pp = [self.PA, self.PB]
            gate = {0: ("vc", 0), 1: ("sc", 0)}
            slot_gate = [("tc", 0)] * 4
            pcount = 0
            for m in range(8):
                lds = []
                for ti, wsrc in enumerate((self.uT, self.vT)):
                    si = ti * 2 + (m % 2)
                    slot = self.DT[:, si * 1024:(si + 1) * 1024]
                    src = wsrc[:, m * 128:(m + 1) * 128].rearrange(
                        "(k q) c -> q k c", q=128)
                    dst3 = slot.rearrange("q (k c) -> q k c", c=128)
                    ld = p.do("g", lambda g, dst3=dst3, src=src: g.dma_start(dst3, src),
                              wait=[slot_gate[si], ("vc", lnv)], inc="gld", inc_amt=16)
                    lds.append(ld)
                for ti in range(2):
                    si = ti * 2 + (m % 2)
                    slot = self.DT[:, si * 1024:(si + 1) * 1024]
                    for nb4 in range(4):
                        pi = pcount % 2
                        pcount += 1
                        ps = pp[pi]
                        tv = None
                        for k in range(NCH):
                            tv = p.do("t", lambda t, k=k, m=m, nb4=nb4, slot=slot, ps=ps: t.matmul(
                                ps[:, :], slot[:, k * 128:(k + 1) * 128],
                                self.Bar[:, k * 2048 + nb4 * 512:k * 2048 + nb4 * 512 + 512],
                                start=(k == 0), stop=(k == NCH - 1)),
                                wait=[("gld", lds[ti]), ("vc", lnv), gate[pi]] if k == 0 else (),
                                inc="tc" if k == NCH - 1 else None)
                        dst = self.A[:, ti * 16384 + m * 2048 + nb4 * 512:
                                     ti * 16384 + m * 2048 + nb4 * 512 + 512]
                        ev = p.do("s", lambda s, dst=dst, ps=ps: s.mul(dst, ps[:, :], 0.03125),
                                  wait=(("tc", tv),), inc="sc")
                        gate[pi] = ("sc", ev)
                    slot_gate[si] = ("tc", p.now("tc"))
            SKIP = os.environ.get("L0SKIP", "")
            pbf = [self.PC[:, :].bitcast(BF16), self.PD[:, :].bitcast(BF16)]
            vbank = [self.PA, self.PB, self.PC, self.PD]
            qh_row = self.Bar[:, 4096:6144]
            qhT_row = self.Bar[:, 6144:8192]
            bounce = self.Bar[:, 8192:10240]
            zrow = self.SL[0:1, 0:128]
            zv = p.do("g", lambda g: g.memset(zrow, 0.0),
                      wait=(("sc", p.now("sc")), ("tc", p.now("tc"))), inc="gc")
            prev_pair_done = ("vc", p.now("vc"))
            prev_bounce = ("gld", p.now("gld"))
            for pr in range(0 if "heads" not in SKIP else 8, 8):
                tmv = []
                for ti in range(2 if "notm" not in SKIP else 0):
                    for grp in range(2):
                        tv = None
                        for j in range(8):
                            srcap = self.A[:, ti * 16384 + pr * 2048 + (grp * 8 + j) * 128:
                                           ti * 16384 + pr * 2048 + (grp * 8 + j) * 128 + 128]
                            tv = p.do("t", lambda t, j=j, ti=ti, srcap=srcap: t.transpose(
                                pbf[ti][:, j * 128:(j + 1) * 128], srcap, self.ident[:, :]),
                                wait=([("sc", p.now("sc")), ("vc", prev_pair_done[1]),
                                       ("gld", p.now("gld"))] if grp == 0 and ti == 0 else
                                      [("vc", p.now("vc"))]) if j == 0 else (),
                                inc="tc" if j == 7 else None)
                        dst = self.Bar[:, ti * 2048 + grp * 1024:ti * 2048 + (grp + 1) * 1024]
                        ev = p.do("v", lambda v, dst=dst, ti=ti: v.tensor_copy(
                            dst, pbf[ti][:, 0:1024]), wait=(("tc", tv),), inc="vc")
                        tmv.append(ev)
                tm_done = p.now("vc")
                ev_gate = ("vc", tm_done)
                for h in range(2):
                    base = 64 * h
                    for tb in range(0 if "noz" not in SKIP else 4, 4):
                        p.do("t", lambda t, tb=tb: t.matmul(
                            vbank[tb][:, :], zrow, self.ones[0:1, 0:512],
                            start=True, stop=False),
                            wait=[ev_gate, ("gc", zv)] if tb == 0 else ())
                    plg = {0: ("sc", 0), 1: ("sc", 0)}
                    for i in range(0 if "norows" not in SKIP else 16, 16):
                        sg = {}
                        sgT = {}
                        tvf = {}
                        tvft = {}
                        for th in range(2):
                            for q in range(2):
                                tb = th * 2 + q
                                tvf[th] = p.do("t", lambda t, i=i, tb=tb, pr=pr, base=base: t.matmul(
                                    self.PL[:, tb * 512:(tb + 1) * 512],
                                    self.A[base:base + 64, pr * 2048 + i * 128:pr * 2048 + (i + 1) * 128],
                                    self.A[base:base + 64,
                                           16384 + pr * 2048 + tb * 512:16384 + pr * 2048 + tb * 512 + 512],
                                    start=True, stop=True),
                                    wait=([("sc", p.now("sc")), ("vc", tm_done), ev_gate]
                                          if i == 0 and th == 0 and q == 0 else
                                          ([plg[th]] if q == 0 else ())),
                                    inc="tc" if q == 1 else None)
                        for th in range(2):
                            sg[th] = p.do("s", lambda s, th=th: s.activation(
                                qh_row[:, th * 1024:(th + 1) * 1024],
                                self.PL[:, th * 1024:(th + 1) * 1024], AF.Sigmoid),
                                wait=(("tc", tvf[th]), ("gld", p.now("gld"))), inc="sc")
                        for th in range(2):
                            for q in range(2):
                                tb = th * 2 + q
                                p.do("t", lambda t, i=i, tb=tb, base=base: t.matmul(
                                    vbank[tb][64:128, :],
                                    self.Bar[:, i * 128 + base:i * 128 + base + 64],
                                    qh_row[:, tb * 512:(tb + 1) * 512],
                                    start=False, stop=False, tile_position=(0, 64)),
                                    wait=[("sc", sg[th])] if q == 0 else ())
                            for q in range(2):
                                tb = th * 2 + q
                                tvft[th] = p.do("t", lambda t, i=i, tb=tb, pr=pr, base=base: t.matmul(
                                    self.PL[:, tb * 512:(tb + 1) * 512],
                                    self.A[base:base + 64,
                                           16384 + pr * 2048 + i * 128:16384 + pr * 2048 + (i + 1) * 128],
                                    self.A[base:base + 64, pr * 2048 + tb * 512:pr * 2048 + tb * 512 + 512],
                                    start=True, stop=True),
                                    inc="tc" if q == 1 else None)
                        for th in range(2):
                            sgT[th] = p.do("s", lambda s, th=th: s.activation(
                                qhT_row[:, th * 1024:(th + 1) * 1024],
                                self.PL[:, th * 1024:(th + 1) * 1024], AF.Sigmoid),
                                wait=(("tc", tvft[th]),), inc="sc")
                            plg[th] = ("sc", sgT[th])
                        for th in range(2):
                            for q in range(2):
                                sb = th * 2 + q
                                p.do("t", lambda t, i=i, sb=sb, base=base: t.matmul(
                                    vbank[sb][0:64, :],
                                    self.Bar[:, 2048 + i * 128 + base:2048 + i * 128 + base + 64],
                                    qhT_row[:, sb * 512:(sb + 1) * 512],
                                    start=False, stop=False, tile_position=(0, 0)),
                                    wait=[("sc", sgT[th])] if q == 0 else ())

                    vdone = None
                    for tb in range(0 if "noz" not in SKIP else 4, 4):
                        vdone = p.do("t", lambda t, tb=tb: t.matmul(
                            vbank[tb][:, :], zrow, self.ones[0:1, 0:512],
                            start=False, stop=True),
                            inc="tc" if tb == 3 else None)
                    evs = []
                    for ti, rows0 in (((0, 0), (1, 64)) if "noevac" not in SKIP else ()):
                        direct = (rows0 == base)
                        for tb in range(4):
                            if direct:
                                dst = self.A[base:base + 64,
                                             ti * 16384 + pr * 2048 + tb * 512:
                                             ti * 16384 + pr * 2048 + tb * 512 + 512]
                            else:
                                dst = bounce[rows0:rows0 + 64, tb * 512:(tb + 1) * 512]
                            ev = p.do("v", lambda v, dst=dst, tb=tb, rows0=rows0: v.tensor_copy(
                                dst, vbank[tb][rows0:rows0 + 64, :]),
                                wait=[("tc", vdone), prev_bounce] if (tb == 0 and ti == 0 and vdone) else [prev_bounce] if tb == 0 and ti == 0 else (),
                                inc="vc")
                        if (not direct) and ("nobounce" not in SKIP):
                            bv = p.do("g", lambda g, ti=ti, pr=pr, rows0=rows0, base=base: g.dma_start(
                                self.A[base:base + 64,
                                       ti * 16384 + pr * 2048:ti * 16384 + (pr + 1) * 2048],
                                bounce[rows0:rows0 + 64, :]),
                                wait=(("vc", p.now("vc")),), inc="gld", inc_amt=16)
                            prev_bounce = ("gld", bv)
                    ev_gate = ("vc", p.now("vc"))
                prev_pair_done = ("vc", p.now("vc"))
            gate = {0: ("vc", 0), 1: ("vc", 0)}
            slot_gate = [("tc", 0)] * 4
            pcount = 0
            for dc in range(8):
                lds = []
                for ti, wsrc in enumerate((self.u_n, self.v_n)):
                    si = ti * 2 + (dc % 2)
                    slot = self.DT[:, si * 1024:(si + 1) * 1024]
                    src = wsrc[:, dc * 128:(dc + 1) * 128].rearrange(
                        "(k q) c -> q k c", q=128)
                    dst3 = slot.rearrange("q (k c) -> q k c", c=128)
                    ld = p.do("g", lambda g, dst3=dst3, src=src: g.dma_start(dst3, src),
                              wait=[slot_gate[si], prev_bounce, ("vc", prev_pair_done[1])],
                              inc="gld", inc_amt=16)
                    lds.append(ld)
                for nb4 in range(4):
                    pi = pcount % 2
                    pcount += 1
                    ps = pp[pi]
                    tv = None
                    for ti in range(2):
                        for k in range(NCH):
                            tv = p.do("t", lambda t, k=k, ti=ti, dc=dc, nb4=nb4, ps=ps: t.matmul(
                                ps[:, :],
                                self.DT[:, (ti * 2 + dc % 2) * 1024 + k * 128:
                                        (ti * 2 + dc % 2) * 1024 + (k + 1) * 128],
                                self.A[:, ti * 16384 + k * 2048 + nb4 * 512:
                                       ti * 16384 + k * 2048 + nb4 * 512 + 512],
                                start=(ti == 0 and k == 0), stop=(ti == 1 and k == NCH - 1)),
                                wait=[("gld", lds[1]), gate[pi],
                                      ("vc", prev_pair_done[1])] if ti == 0 and k == 0 else (),
                                inc="tc" if (ti == 1 and k == NCH - 1) else None)
                    xs = self.xfm(b, dc)[:, nb4 * 512:(nb4 + 1) * 512]
                    ev = p.do("v", lambda v, xs=xs, ps=ps: v.scalar_tensor_tensor(
                        xs, ps[:, :], 0.03125, xs, ALU.mult, ALU.add),
                        wait=(("tc", tv),), inc="vc")
                    gate[pi] = ("vc", ev)
                slot_gate[2 * 0 + dc % 2] = ("tc", p.now("tc"))
                slot_gate[2 * 1 + dc % 2] = ("tc", p.now("tc"))
            self.xfm_ready = ("vc", p.now("vc"))

    def emit_l1attn(self):
        p = self.p
        WQK = self.DT[:, 0:1024]
        att = [self.DT[:, 1024:3072], self.DT[:, 3072:5120]]
        nrow = self.SL[0:1, 0:2048]
        stg1 = self.DT[0:16, 5120:5248]
        stg2 = self.DT[0:16, 5248:5376]
        nmx_bf = self.DT[:, 5376:5392]
        rr_bf = self.DT[:, 5392:5408]
        rrb = self.PMRN
        zrow = self.DT[0:1, 5408:5536]
        SKIP1 = os.environ.get("L1SKIP", "")
        for b in range(NB if "bb0" not in SKIP1 else 1):
            zv = p.do("g", lambda g: g.memset(zrow, 0.0),
                      wait=(("sc", p.now("sc")), ("tc", p.now("tc")),
                            ("vc", p.now("vc"))), inc="gc")
            lbv = p.do("g", lambda g: g.dma_start(self.PMRN[0:1, 0:1024],
                                                  self.bvr[:, :]),
                       inc="gld", inc_amt=16)
            lnv = self.emit_ln(b, lambda c: self.Bar[:, c * 2048:(c + 1) * 2048])
            lv = p.do("g", lambda g: g.dma_start(
                self.WB[:, :].rearrange("q (k c) -> q k c", c=1024),
                self.WvT[:, :].rearrange("(k q) c -> q k c", q=128)),
                wait=(("tc", p.now("tc")), ("sc", p.now("sc"))), inc="gld", inc_amt=16)
            pp = [self.PA, self.PB]
            gate = {0: ("vc", 0), 1: ("vc", 0)}
            pcount = 0
            for nt in range(16):
                for oh in range(2):
                    pi = pcount % 2
                    pcount += 1
                    ps = pp[pi]
                    for k in range(NCH):
                        p.do("t", lambda t, k=k, nt=nt, oh=oh, ps=ps: t.matmul(
                            ps[:, :],
                            self.Bar[:, k * 2048 + nt * 128:k * 2048 + (nt + 1) * 128],
                            self.WB[:, k * 1024 + oh * 512:k * 1024 + oh * 512 + 512],
                            start=(k == 0), stop=False),
                            wait=[("vc", lnv), ("gld", lv), gate[pi]] if k == 0 else ())
                    tv = p.do("t", lambda t, oh=oh, ps=ps: t.matmul(
                        ps[:, :], self.ones[0:1, 0:128],
                        self.PMRN[0:1, oh * 512:oh * 512 + 512],
                        start=False, stop=True), wait=(("gld", lbv),), inc="tc")
                    dst = self.A[:, 16384 + nt * 1024 + oh * 512:16384 + nt * 1024 + oh * 512 + 512]
                    ev = p.do("v", lambda v, dst=dst, ps=ps: v.tensor_copy(dst, ps[:, :]),
                              wait=(("tc", tv),), inc="vc")
                    gate[pi] = ("vc", ev)
            vh_done = p.now("vc")
            wq_gate = ("vc", 0)
            for h in range(16):
                base = 64 * (h % 2)
                ld = p.do("g", lambda g, h=h, base=base: g.dma_start(
                    self.SL[base:base + 64, 0:64], self.trilT[h, :, :]),
                    wait=[wq_gate, ("tc", p.now("tc"))], inc="gld", inc_amt=16)
                tv = p.do("t", lambda t, base=base: t.matmul(
                    self.PC[base:base + 64, 0:64], self.SL[base:base + 64, 0:64],
                    self.SL[base:base + 64, 0:64], start=True, stop=True,
                    tile_position=(base, base)),
                    wait=(("gld", ld), ("vc", p.now("vc"))), inc="tc")
                ev = p.do("v", lambda v, h=h, base=base: v.tensor_copy(
                    WQK[base:base + 64, h * 64:(h + 1) * 64],
                    self.PC[base:base + 64, 0:64]),
                    wait=(("tc", tv),), inc="vc")
                wq_gate = ("vc", ev)
            gate = {0: ("sc", 0), 1: ("sc", 0)}
            pcount = 0
            for pr in range(8):
                for nb4 in range(4):
                    pi = pcount % 2
                    pcount += 1
                    ps0, ps1 = (self.PA, self.PB) if pi == 0 else (self.PC, self.PD)
                    tv = None
                    for par in range(2):
                        h = 2 * pr + par
                        base = 64 * par
                        psx = ps0 if par == 0 else ps1
                        tv = p.do("t", lambda t, h=h, base=base, pr=pr, nb4=nb4, psx=psx: t.matmul(
                            psx[base:base + 64, :], WQK[base:base + 64, h * 64:(h + 1) * 64],
                            self.Bar[base:base + 64, pr * 2048 + nb4 * 512:pr * 2048 + nb4 * 512 + 512],
                            start=True, stop=True, tile_position=(base, base)),
                            wait=[("vc", wq_gate[1]), gate[pi]] if par == 0 else (),
                            inc="tc" if par == 1 else None)
                    for par in range(2):
                        base = 64 * par
                        psx = ps0 if par == 0 else ps1
                        dst = self.A[base:base + 64,
                                     pr * 2048 + nb4 * 512:pr * 2048 + nb4 * 512 + 512]
                        ev = p.do("s", lambda s, dst=dst, psx=psx, base=base: s.mul(
                            dst, psx[base:base + 64, :], 0.125),
                            wait=(("tc", tv),) if par == 0 else (), inc="sc")
                    gate[pi] = ("sc", ev)
            lo = p.do("g", lambda g: g.dma_start(
                self.WB[:, :].rearrange("q (k c) -> q k c", c=1024),
                self.WoT[:, :].rearrange("(k q) c -> q k c", q=128)),
                wait=(("vc", vh_done), ("tc", p.now("tc"))), inc="gld", inc_amt=16)
            pvbank = [self.PA, self.PB, self.PC, self.PD]
            prev_evac = ("vc", 0)
            plbf = self.PL[:, :].bitcast(BF16)
            for h in range(0 if "heads" not in SKIP1 else 16, 1 if "h1" in SKIP1 else 16):
                pr, par = h // 2, h % 2
                base = 64 * par
                sgp = 0
                sgd = {0: ("sc", 0), 1: ("sc", 0)}
                for i in range(16):
                    PX = [self.PL, self.PM][i % 2] if par == 0 else self.PL
                    pxk = (i % 2) if par == 0 else 0
                    tv = None
                    for tb in range(4):
                        tv = p.do("t", lambda t, i=i, tb=tb, pr=pr, base=base, PX=PX: t.matmul(
                            PX[:, tb * 512:(tb + 1) * 512],
                            self.A[base:base + 64, pr * 2048 + i * 128:pr * 2048 + (i + 1) * 128],
                            self.Bar[base:base + 64, pr * 2048 + tb * 512:pr * 2048 + tb * 512 + 512],
                            start=True, stop=True),
                            wait=[("sc", p.now("sc")), ("vc", p.now("vc"))] if (i <= 1 and tb == 0)
                                 else ([sgd[pxk]] if tb == 0 else ()),
                            inc="tc" if tb == 3 else None)
                    rv = p.do("v", lambda v, PX=PX: v.reduce_max(
                        self.ST[:, ST_MS1:ST_MS1 + 1], PX[:, :], axis=AX.X,
                        negate=True), wait=(("tc", tv),), inc="vc")
                    rv = p.do("v", lambda v, i=i: v.tensor_copy(
                        nmx_bf[:, i:i + 1], self.ST[:, ST_MS1:ST_MS1 + 1]),
                        wait=(("vc", rv),), inc="vc")
                    rv = p.do("v", lambda v, i=i: v.tensor_copy(
                        self.ST[:, ST_NMX + i:ST_NMX + i + 1], nmx_bf[:, i:i + 1]),
                        wait=(("vc", rv),), inc="vc")
                    sgp = p.do("s", lambda s, i=i, PX=PX: s.activation(
                        att[0], PX[:, :], AF.Exp,
                        bias=self.ST[:, ST_NMX + i:ST_NMX + i + 1],
                        accum_out=self.ST[:, ST_RS + i:ST_RS + i + 1]),
                        wait=(("vc", rv),), inc="sc")
                    sgd[pxk] = ("sc", sgp)
                tv = p.do("t", lambda t: t.transpose(
                    plbf[0:16, 0:128], nmx_bf[:, 0:16], self.ident[:, :]),
                    wait=(("sc", sgp),), inc="tc")
                fv = p.do("v", lambda v: v.tensor_copy(stg1, plbf[0:16, 0:128]),
                          wait=(("tc", tv),), inc="vc")
                yv = p.do("y", lambda y: y.dma_start(
                    self.bncb[0:1, :].rearrange("x (a c) -> (x a) c", c=128), stg1),
                    wait=(("vc", fv), ("yld", p.now("yld"))), inc="yld", inc_amt=16)
                yv = p.do("y", lambda y: y.dma_start(
                    nrow, self.bncb[0:1, :]), wait=(("yld", yv),), inc="yld", inc_amt=16)
                rv = p.do("v", lambda v: v.reciprocal(
                    self.ST[:, ST_RR:ST_RR + 16], self.ST[:, ST_RS:ST_RS + 16]),
                    wait=(("sc", sgp),), inc="vc")
                rv = p.do("v", lambda v: v.tensor_copy(
                    rr_bf[:, 0:16], self.ST[:, ST_RR:ST_RR + 16]),
                    wait=(("vc", rv),), inc="vc")
                tv = p.do("t", lambda t: t.transpose(
                    plbf[0:16, 128:256], rr_bf[:, 0:16], self.ident[:, :]),
                    wait=(("vc", rv),), inc="tc")
                fv = p.do("v", lambda v: v.tensor_copy(stg2, plbf[0:16, 128:256]),
                          wait=(("tc", tv),), inc="vc")
                yv2 = p.do("y", lambda y: y.dma_start(
                    self.bncb[0:1, :].rearrange("x (a c) -> (x a) c", c=128), stg2),
                    wait=(("vc", fv), ("yld", yv)), inc="yld", inc_amt=16)
                yv2 = p.do("g", lambda g, base=base: g.dma_start(
                    rrb[base:base + 64, :],
                    bass.AP(self.bncb.tensor, 0, [[0, 64], [1, 2048]])),
                    wait=(("yld", yv2),
                          ("vc", prev_evac[1] if h else p.now("vc"))),
                    inc="gld", inc_amt=16)
                rrb_done = yv2
                if par == 0:
                    for sb in range(4):
                        p.do("t", lambda t, sb=sb: t.matmul(
                            pvbank[sb][:, :], zrow, self.ones[0:1, 0:512],
                            start=True, stop=False),
                            wait=[prev_evac, ("gc", zv)] if sb == 0 else ())
                sg2 = 0
                pvlast = None
                for j in range(16):
                    tv = None
                    for sb in range(4):
                        p.do("t", lambda t, j=j, sb=sb, pr=pr, base=base: t.matmul(
                            self.PL[:, sb * 512:(sb + 1) * 512],
                            self.Bar[base:base + 64, pr * 2048 + j * 128:pr * 2048 + (j + 1) * 128],
                            self.A[base:base + 64, pr * 2048 + sb * 512:pr * 2048 + sb * 512 + 512],
                            start=True, stop=False),
                            wait=[("yld", p.now("yld")), ("sc", p.now("sc"))]
                                 if (sb == 0 and j == 0) else
                                 ([("sc", sg2)] if sb == 0 else ()))
                        tv = p.do("t", lambda t, sb=sb: t.matmul(
                            self.PL[:, sb * 512:(sb + 1) * 512], self.ones[0:1, 0:128],
                            nrow[0:1, sb * 512:sb * 512 + 512],
                            start=False, stop=True), inc="tc" if sb == 3 else None)
                    sg2 = p.do("s", lambda s, j=j: s.activation(
                        att[j % 2], self.PL[:, :], AF.Exp),
                        wait=(("tc", tv),), inc="sc")
                    for sb in range(4):
                        pvlast = p.do("t", lambda t, j=j, sb=sb, par=par, h=h: t.matmul(
                            pvbank[sb][64 * par:64 * par + 64, :],
                            self.A[:, 16384 + j * 1024 + h * 64:16384 + j * 1024 + (h + 1) * 64],
                            att[j % 2][:, sb * 512:(sb + 1) * 512],
                            start=False, stop=False, tile_position=(0, 64 * par)),
                            wait=[("sc", sg2)] if sb == 0 else ())
                if par == 1:
                    pvlast = None
                    for sb in range(4):
                        pvlast = p.do("t", lambda t, sb=sb: t.matmul(
                            pvbank[sb][:, :], zrow, self.ones[0:1, 0:512],
                            start=False, stop=True),
                            inc="tc" if sb == 3 else None)
                    evs = None
                    for sb in range(4):
                        dst = self.A[:, pr * 2048 + sb * 512:pr * 2048 + sb * 512 + 512]
                        evs = p.do("v", lambda v, dst=dst, sb=sb: v.tensor_tensor(
                            dst, pvbank[sb][:, :], rrb[:, sb * 512:(sb + 1) * 512],
                            ALU.mult),
                            wait=[("tc", pvlast), ("gld", rrb_done)] if sb == 0 else (),
                            inc="vc")
                    prev_evac = ("vc", evs)
            lbo = p.do("g", lambda g: g.dma_start(self.PMRN[0:1, 0:1024],
                                                  self.bor[:, :]),
                       wait=(("vc", p.now("vc")),), inc="gld", inc_amt=16)
            gate = {0: ("vc", 0), 1: ("vc", 0)}
            pcount = 0
            for dc in range(8):
                for nb4 in range(4):
                    pi = pcount % 2
                    pcount += 1
                    ps = pp[pi]
                    for k in range(NCH):
                        p.do("t", lambda t, k=k, dc=dc, nb4=nb4, ps=ps: t.matmul(
                            ps[:, :], self.WB[:, k * 1024 + dc * 128:k * 1024 + (dc + 1) * 128],
                            self.A[:, k * 2048 + nb4 * 512:k * 2048 + nb4 * 512 + 512],
                            start=(k == 0), stop=False),
                            wait=[("gld", lo), ("vc", p.now("vc")), gate[pi]] if k == 0 else ())
                    tv = p.do("t", lambda t, dc=dc, ps=ps: t.matmul(
                        ps[:, :], self.PMRN[0:1, dc * 128:(dc + 1) * 128],
                        self.ones[0:1, 0:512], start=False, stop=True),
                        wait=(("gld", lbo),), inc="tc")
                    xs = self.xfm(b, dc)[:, nb4 * 512:(nb4 + 1) * 512]
                    ev = p.do("v", lambda v, xs=xs, ps=ps: v.tensor_add(xs, ps[:, :], xs),
                              wait=(("tc", tv),), inc="vc")
                    gate[pi] = ("vc", ev)
            self.xfm_ready = ("vc", p.now("vc"))

    def emit_l0ffn(self):
        p = self.p
        for b in range(NB):
            lnv = self.emit_ln(b, lambda c: self.Bar[:, c * 2048:(c + 1) * 2048], rms=True)
            p.do("g", lambda g: g.memset(self.PMRN[:, :], -1e30),
                 wait=(("vc", p.now("vc")),), inc="gc")
            pmg = p.now("gc")
            se_b = self.SL[:, 0:1024]
            sn_b = self.SL[:, 1024:2048]
            snf = self.DT[:, 2048:3072]
            pbf = self.PC[:, :].bitcast(BF16)
            ev_gate = {0: ("vc", 0), 1: ("vc", 0)}
            st_gate = {0: ("sc", 0), 1: ("sc", 0)}
            sp_gate = ("yld", 0)
            se_free = ("vc", 0)
            for mt in range(32):
                PP = [self.PL, self.PM][mt % 2]
                ppbf = PP[:, 1024:1536].bitcast(BF16)
                ld = p.do("g", lambda g, mt=mt: g.dma_start(
                    se_b, self.se[mt * 128:(mt + 1) * 128, :]),
                    wait=(se_free,), inc="gld", inc_amt=16)
                sv = p.do("v", lambda v: v.scalar_tensor_tensor(
                    sn_b, se_b, 1.0, se_b, ALU.mult, ALU.mult,
                    accum_out=self.ST[:, ST_MS1:ST_MS1 + 1]),
                    wait=(("gld", ld), ("yld", self.init_yld)), inc="vc")
                sq = p.do("s", lambda s: s.activation(
                    self.ST[:, ST_MS1 + 1:ST_MS1 + 2], self.ST[:, ST_MS1:ST_MS1 + 1],
                    AF.Sqrt), wait=(("vc", sv),), inc="sc")
                sv = p.do("v", lambda v: v.tensor_scalar_max(
                    self.ST[:, ST_MS1 + 1:ST_MS1 + 2], self.ST[:, ST_MS1 + 1:ST_MS1 + 2],
                    1e-12), wait=(("sc", sq),), inc="vc")
                sv = p.do("v", lambda v: v.reciprocal(
                    self.ST[:, ST_MS1 + 2:ST_MS1 + 3], self.ST[:, ST_MS1 + 1:ST_MS1 + 2]),
                    wait=(("vc", sv),), inc="vc")
                sv = p.do("v", lambda v: v.tensor_scalar(
                    sn_b, se_b, self.ST[:, ST_MS1 + 2:ST_MS1 + 3], 32.0,
                    ALU.mult, ALU.mult), wait=(("vc", sv),), inc="vc")
                se_free = ("vc", sv)
                tv = None
                for c in range(NCH):
                    tv = p.do("t", lambda t, c=c, ppbf=ppbf: t.transpose(
                        ppbf[:, c * 128:(c + 1) * 128], sn_b[:, c * 128:(c + 1) * 128],
                        self.ident[:, :]),
                        wait=[("vc", p.now("vc")), ("gc", self.init_gc),
                              st_gate[mt % 2], ev_gate[mt % 2]] if c == 0 else (),
                        inc="tc" if c == NCH - 1 else None)
                snv = None
                for c in range(NCH):
                    snv = p.do("v", lambda v, c=c, ppbf=ppbf: v.tensor_scalar_mul(
                        snf[:, c * 128:(c + 1) * 128], ppbf[:, c * 128:(c + 1) * 128],
                        self.ST[:, ST_GS + c:ST_GS + c + 1]),
                        wait=[("tc", tv)] if c == 0 else (), inc="vc")
                tv = None
                for nb4 in range(4):
                    for k in range(NCH):
                        tv = p.do("t", lambda t, k=k, nb4=nb4, PP=PP: t.matmul(
                            PP[:, nb4 * 512:(nb4 + 1) * 512],
                            snf[:, k * 128:(k + 1) * 128],
                            self.Bar[:, k * 2048 + nb4 * 512:k * 2048 + nb4 * 512 + 512],
                            start=(k == 0), stop=(k == NCH - 1)),
                            wait=[("vc", max(snv, lnv))] if k == 0 and nb4 == 0 else (),
                            inc="tc" if (k == NCH - 1 and nb4 == 3) else None)
                rv = p.do("v", lambda v, PP=PP: v.reduce_max(
                    self.ST[:, ST_MS1 + 3:ST_MS1 + 4], PP[:, :], axis=AX.X,
                    negate=True), wait=(("tc", tv), ("gc", pmg)), inc="vc")
                p.do("v", lambda v, mt=mt: v.tensor_scalar_mul(
                    self.ST[:, ST_RM + mt:ST_RM + mt + 1],
                    self.ST[:, ST_MS1 + 3:ST_MS1 + 4], -1.0),
                    wait=(("vc", rv),), inc="vc")
                pmv = p.do("v", lambda v, PP=PP: v.tensor_max(
                    self.PMRN[:, :], self.PMRN[:, :], PP[:, :]), inc="vc")
                ev_gate[mt % 2] = ("vc", pmv)
                sx = p.do("s", lambda s, mt=mt, PP=PP: s.activation(
                    self.DT[:, 0:2048], PP[:, :], AF.Exp,
                    bias=self.ST[:, ST_MS1 + 3:ST_MS1 + 4],
                    accum_out=self.ST[:, ST_CS + mt:ST_CS + mt + 1]),
                    wait=(("vc", rv),), inc="sc")
                sx = p.do("s", lambda s, PP=PP: s.activation(self.FSA[:, :], PP[:, :],
                                                             AF.Copy),
                          wait=(sp_gate,), inc="sc")
                st_gate[mt % 2] = ("sc", sx)
                yv = p.do("y", lambda y, b=b, mt=mt: y.dma_start(
                    self.Lsp[b, mt, :, :], self.FSA[:, :]),
                    wait=(("sc", sx),), inc="yld", inc_amt=16)
                sp_gate = ("yld", yv)
            st_gate = st_gate[31 % 2]
            sv = p.do("s", lambda s: s.activation(
                self.ST[:, ST_NAM:ST_NAM + 32], self.ST[:, ST_CS:ST_CS + 32], AF.Ln),
                wait=(("sc", p.now("sc")),), inc="sc")
            vv = p.do("v", lambda v: v.tensor_add(
                self.ST[:, ST_NAM:ST_NAM + 32], self.ST[:, ST_NAM:ST_NAM + 32],
                self.ST[:, ST_RM:ST_RM + 32]), wait=(("sc", sv),), inc="vc")
            vv = p.do("v", lambda v: v.tensor_scalar_mul(
                self.ST[:, ST_NAM:ST_NAM + 32], self.ST[:, ST_NAM:ST_NAM + 32], -1.0),
                wait=(("vc", vv),), inc="vc")
            nam_done = vv
            rnv = None
            for grp in range(4):
                tv = None
                for j in range(4):
                    k = grp * 4 + j
                    tv = p.do("t", lambda t, k=k, j=j: t.transpose(
                        pbf[:, j * 128:(j + 1) * 128],
                        self.PMRN[:, k * 128:(k + 1) * 128], self.ident[:, :]),
                        wait=[("vc", p.now("vc"))] if j == 0 else (),
                        inc="tc" if j == 3 else None)
                rnv = p.do("v", lambda v, grp=grp: v.reduce_max(
                    self.ST[:, ST_RN + grp * 4:ST_RN + (grp + 1) * 4],
                    pbf[:, 0:512].rearrange("p (a c) -> p a c", a=4), axis=AX.X),
                    wait=(("tc", tv),), inc="vc")
            cv = p.do("v", lambda v: v.tensor_copy(
                self.SL[:, 0:16], self.ST[:, ST_RN:ST_RN + 16]),
                wait=(("vc", rnv),), inc="vc")
            tv = p.do("t", lambda t: t.transpose(
                self.PL[:, :].bitcast(BF16)[0:16, 0:128], self.SL[:, 0:16],
                self.ident[:, :]), wait=(("vc", cv), st_gate), inc="tc")
            cv = p.do("v", lambda v: v.tensor_copy(
                self.SL[0:16, 128:256], self.PL[:, :].bitcast(BF16)[0:16, 0:128]),
                wait=(("tc", tv),), inc="vc")
            yv = p.do("y", lambda y: y.dma_start(
                self.bncb[0:1, :].rearrange("x (a c) -> (x a) c", c=128),
                self.SL[0:16, 128:256]), wait=(("vc", cv), sp_gate), inc="yld", inc_amt=16)
            bv = p.do("g", lambda g: g.dma_start(
                self.PMRN[:, :], bass.AP(self.bncb.tensor, 0, [[0, 128], [1, 2048]])),
                wait=(("yld", yv), ("vc", p.now("vc"))), inc="gld", inc_amt=16)
            rn_b_done = bv
            for k in range(16):
                tv = None
                for c in range(NCH):
                    tv = p.do("t", lambda t, k=k, c=c: t.transpose(
                        pbf[:, c * 128:(c + 1) * 128],
                        self.Bar[:, c * 2048 + k * 128:c * 2048 + (k + 1) * 128],
                        self.ident[:, :]),
                        wait=[("vc", p.now("vc")), ("tc", p.now("tc"))] if c == 0 else (),
                        inc="tc" if c == NCH - 1 else None)
                p.do("v", lambda v, k=k: v.tensor_copy(
                    self.A[:, k * 1024:(k + 1) * 1024], pbf[:, 0:1024]),
                    wait=(("tc", tv),), inc="vc")
            xntm_done = p.now("vc")
            dsl = [self.DT[:, 0:2048], self.DT[:, 2048:4096]]
            wslot = [self.WB[:, 0:4096], self.WB[:, 4096:8192]]
            dtv_gate = ("tc", 0)
            wgate = [("tc", 0), ("tc", 0)]
            eow = self.Bar[:, 12288:13312]
            eow_gate = ("yld", 0)
            b1gate = {0: ("sc", p.now("sc")), 1: ("sc", p.now("sc"))}
            b2gate = {0: ("vc", p.now("vc")), 1: ("vc", p.now("vc"))}
            for e in range(NE):
                b1s = ST_B1 if e % 2 == 0 else 128
                b2s = 0 if e % 2 == 0 else 1024
                lb = p.do("g", lambda g, e=e, b1s=b1s: g.dma_start(
                    self.ST[:, b1s:b1s + 32], self.b1c[e, :, :]),
                    wait=(b1gate[e % 2],), inc="gld", inc_amt=16)
                lb2 = p.do("g", lambda g, e=e, b2s=b2s: g.dma_start(
                    self.SL[0:1, b2s:b2s + 1024], self.b2r[e, :, :]),
                    wait=(b2gate[e % 2],), inc="gld", inc_amt=16)
                for ms in range(4):
                    mt = 4 * e + ms
                    lv = p.do("y", lambda y, b=b, mt=mt: y.dma_start(
                        self.FSA[:, :], self.Lsp[b, mt, :, :]),
                        wait=(st_gate, ("sc", p.now("sc"))), inc="yld", inc_amt=16)
                    dx = p.do("s", lambda s, ms=ms, mt=mt: s.activation(
                        dsl[ms % 2], self.FSA[:, :], AF.Exp,
                        bias=self.ST[:, ST_NAM + mt:ST_NAM + mt + 1]),
                        wait=(("yld", lv), ("vc", nam_done), dtv_gate), inc="sc")
                    st_gate = ("sc", dx)
                    for grp in range(2):
                        tv = None
                        for j in range(8):
                            k = grp * 8 + j
                            tv = p.do("t", lambda t, ms=ms, k=k, j=j: t.transpose(
                                pbf[:, j * 128:(j + 1) * 128],
                                dsl[ms % 2][:, k * 128:(k + 1) * 128], self.ident[:, :]),
                                wait=[("sc", dx), ("vc", p.now("vc"))] if j == 0 else (),
                                inc="tc" if j == 7 else None)
                        dtv_gate = ("tc", tv)
                        src3 = pbf[:, 0:1024].rearrange("p (k c) -> p k c", c=128)
                        dd = self.Bar[:, :].rearrange(
                            "p (k c) -> p k c", c=512)[:, grp * 8:(grp + 1) * 8,
                                                       ms * 128:(ms + 1) * 128]
                        p.do("v", lambda v, dd=dd, src3=src3: v.tensor_copy(dd, src3),
                             wait=(("tc", tv),), inc="vc")
                dtr_done = p.now("vc")
                sgate = {0: ("vc", 0), 1: ("vc", 0)}
                for ds in range(NCH):
                    ps = [self.PA, self.PB][ds % 2]
                    tv = None
                    for k in range(16):
                        tv = p.do("t", lambda t, ds=ds, k=k, ps=ps: t.matmul(
                            ps[:, :], self.A[:, k * 1024 + ds * 128:k * 1024 + (ds + 1) * 128],
                            self.Bar[:, :].rearrange("p (k c) -> p k c", c=512)[:, k, :],
                            start=(k == 0), stop=(k == 15)),
                            wait=[("vc", dtr_done), sgate[ds % 2]] if k == 0 else (),
                            inc="tc" if k == 15 else None)
                    sv = p.do("v", lambda v, ds=ds, ps=ps: v.tensor_copy(
                        self.Bar[:, 8192 + ds * 512:8192 + (ds + 1) * 512], ps[:, :]),
                        wait=(("tc", tv),), inc="vc")
                    sgate[ds % 2] = ("vc", sv)
                slots_done = p.now("vc")
                hgate = {0: ("sc", 0), 1: ("sc", 0)}
                for q8 in range(8):
                    wl = p.do("g", lambda g, e=e, q8=q8: g.dma_start(
                        wslot[q8 % 2].rearrange("q (k c) -> q k c", c=512),
                        self.w1[e, :, q8 * 512:(q8 + 1) * 512].rearrange(
                            "(k q) c -> q k c", q=128)),
                        wait=(wgate[q8 % 2],), inc="gld", inc_amt=16)
                    for ht in range(4):
                        ps = [self.PA, self.PB][ht % 2]
                        tv = None
                        for k in range(NCH):
                            tv = p.do("t", lambda t, q8=q8, ht=ht, k=k, ps=ps: t.matmul(
                                ps[:, :],
                                wslot[q8 % 2][:, k * 512 + ht * 128:k * 512 + (ht + 1) * 128],
                                self.Bar[:, 8192 + k * 512:8192 + (k + 1) * 512],
                                start=(k == 0), stop=(k == NCH - 1)),
                                wait=[("gld", wl), ("vc", slots_done),
                                      hgate[ht % 2]] if k == 0 else (),
                                inc="tc" if k == NCH - 1 else None)
                        hv = p.do("s", lambda s, q8=q8, ht=ht, ps=ps, b1s=b1s: s.activation(
                            self.A[:, 16384 + (q8 * 4 + ht) * 512:
                                   16384 + (q8 * 4 + ht + 1) * 512],
                            ps[:, :], AF.Gelu,
                            bias=self.ST[:, b1s + q8 * 4 + ht:b1s + q8 * 4 + ht + 1]),
                            wait=(("tc", tv), ("gld", lb)), inc="sc")
                        hgate[ht % 2] = ("sc", hv)
                    wgate[q8 % 2] = ("tc", p.now("tc"))
                h_done = p.now("sc")
                eops = [self.PL[:, 0:512], self.PL[:, 512:1024],
                        self.PL[:, 1024:1536], self.PL[:, 1536:2048],
                        self.PA[:, :], self.PB[:, :], self.PC[:, :], self.PD[:, :]]
                for hq in range(8):
                    wl = p.do("g", lambda g, e=e, hq=hq: g.dma_start(
                        wslot[hq % 2].rearrange("q (k c) -> q k c", c=1024),
                        self.w2[e, hq * 512:(hq + 1) * 512, :].rearrange(
                            "(k q) c -> q k c", q=128)),
                        wait=(wgate[hq % 2],), inc="gld", inc_amt=16)
                    for hc in range(4):
                        kk = hq * 4 + hc
                        for gi in range(8):
                            msub, dh = gi // 2, gi % 2
                            last = (hc == 3 and gi == 7)
                            w_ = ()
                            if hc == 0 and gi == 0:
                                w_ = [("gld", wl)] + (
                                    [("sc", h_done), ("vc", p.now("vc"))] if hq == 0 else [])
                            tq = p.do("t", lambda t, kk=kk, hc=hc, hq=hq, msub=msub, dh=dh, gi=gi: t.matmul(
                                eops[gi],
                                self.A[:, 16384 + kk * 512 + msub * 128:
                                       16384 + kk * 512 + (msub + 1) * 128],
                                wslot[hq % 2][:, hc * 1024 + dh * 512:hc * 1024 + (dh + 1) * 512],
                                start=(kk == 0), stop=False),
                                wait=w_, inc="tc" if last else None)
                    wgate[hq % 2] = ("tc", tq)
                tvb = None
                for gi in range(8):
                    msub, dh = gi // 2, gi % 2
                    tvb = p.do("t", lambda t, msub=msub, dh=dh, gi=gi, b2s=b2s: t.matmul(
                        eops[gi], self.ones[0:1, 0:128],
                        self.SL[0:1, b2s + dh * 512:b2s + (dh + 1) * 512],
                        start=False, stop=True),
                        wait=(("gld", lb2),) if gi == 0 else (),
                        inc="tc" if gi == 7 else None)

                b1gate[e % 2] = ("sc", p.now("sc"))
                b2gate[e % 2] = ("tc", tvb)
                for msub in range(4):
                    mt = 4 * e + msub
                    ev = None
                    for dh in range(2):
                        ev = p.do("v", lambda v, msub=msub, dh=dh: v.tensor_copy(
                            eow[:, dh * 512:(dh + 1) * 512], eops[msub * 2 + dh]),
                            wait=[("tc", tvb), eow_gate] if dh == 0 else (),
                            inc="vc")
                    yv = p.do("y", lambda y, b=b, mt=mt: y.dma_start(
                        self.eod[b, mt * 128:(mt + 1) * 128, :], eow[:, :]),
                        wait=(("vc", ev),), inc="yld", inc_amt=16)
                    eow_gate = ("yld", yv)
            craw = self.Bar
            rrb = self.SL[:, 0:512]
            el = None
            for mt in range(32):
                el = p.do("g", lambda g, b=b, mt=mt: g.dma_start(
                    self.A[:, mt * 1024:(mt + 1) * 1024],
                    self.eod[b, mt * 128:(mt + 1) * 128, :]),
                    wait=[("tc", p.now("tc")), ("sc", p.now("sc")), eow_gate,
                          ("vc", p.now("vc"))] if mt == 0 else (),
                    inc="gld", inc_amt=16)
            eo_done = el
            lgate = ("vc", 0)
            hgate = {0: ("sc", 0), 1: ("sc", 0)}
            for nb4 in range(4):
                lgate = ("vc", p.now("vc")) if nb4 else lgate
                tgate = ("tc", p.now("tc")) if nb4 else ("tc", 0)
                for g8 in range(16):
                    mt0 = g8 * 2
                    half = g8 % 2
                    fsl = self.FSA[:, half * 1024:half * 1024 + 1024]
                    dst3 = fsl.rearrange("p (a c) -> p a c", c=512)
                    src3 = self.Lsp[b, mt0:mt0 + 2, :, nb4 * 512:(nb4 + 1) * 512] \
                        .rearrange("a p c -> p a c")
                    lv = p.do("y", lambda y, dst3=dst3, src3=src3: y.dma_start(dst3, src3),
                        wait=(lgate, hgate[half], eow_gate) if g8 <= 1
                             else (hgate[half],),
                        inc="yld", inc_amt=16)
                    sv = None
                    for a2 in range(2):
                        sv = p.do("v", lambda v, fsl=fsl, a2=a2, nb4=nb4: v.tensor_sub(
                            fsl[:, a2 * 512:(a2 + 1) * 512], fsl[:, a2 * 512:(a2 + 1) * 512],
                            self.PMRN[:, nb4 * 512:(nb4 + 1) * 512]),
                            wait=[("yld", lv), ("gld", rn_b_done)] if a2 == 0 else (),
                            inc="vc")
                    cx = p.do("s", lambda s, mt0=mt0, fsl=fsl: s.activation(
                        craw[:, mt0 * 512:(mt0 + 2) * 512], fsl, AF.Exp),
                        wait=(("vc", sv), tgate) if g8 == 0 else (("vc", sv),),
                        inc="sc")
                    hgate[half] = ("sc", cx)
                cr_done = p.now("sc")
                tv = None
                for mt in range(32):
                    tv = p.do("t", lambda t, mt=mt: t.matmul(
                        self.PA[0:1, :], self.ones[:, 0:1],
                        craw[:, mt * 512:(mt + 1) * 512],
                        start=(mt == 0), stop=(mt == 31)),
                        wait=[("sc", cr_done), ("vc", p.now("vc"))] if mt == 0 else (),
                        inc="tc" if mt == 31 else None)
                rv = p.do("v", lambda v: v.reciprocal(self.FSA[0:1, 1024:1536],
                                                      self.PA[0:1, :]),
                          wait=(("tc", tv),), inc="vc")
                rv = p.do("v", lambda v: v.tensor_copy(self.SL[0:1, 512:1024],
                                                       self.FSA[0:1, 1024:1536]),
                          wait=(("vc", rv),), inc="vc")
                tv = p.do("t", lambda t: t.matmul(
                    self.PB[:, :], self.ones[0:1, 0:128], self.SL[0:1, 512:1024],
                    start=True, stop=True), wait=(("vc", rv),), inc="tc")
                rv = p.do("v", lambda v: v.tensor_copy(rrb, self.PB[:, :]),
                          wait=(("tc", tv),), inc="vc")
                rrb_done = rv
                dgate = {0: ("vc", 0), 1: ("vc", 0)}
                for dc in range(NCH):
                    pi = dc % 2
                    ps = [self.PC, self.PD][pi]
                    tv = None
                    for mt in range(32):
                        tv = p.do("t", lambda t, dc=dc, mt=mt, ps=ps: t.matmul(
                            ps[:, :], self.A[:, mt * 1024 + dc * 128:mt * 1024 + (dc + 1) * 128],
                            craw[:, mt * 512:(mt + 1) * 512],
                            start=(mt == 0), stop=(mt == 31)),
                            wait=[("gld", eo_done), ("sc", cr_done),
                                  dgate[pi]] if mt == 0 else (),
                            inc="tc" if mt == 31 else None)
                    fs2 = self.FSA[:, (dc % 2) * 512:(dc % 2) * 512 + 512]
                    ev = p.do("v", lambda v, fs2=fs2, ps=ps: v.tensor_mul(fs2, ps[:, :], rrb),
                              wait=(("tc", tv), ("vc", rrb_done)), inc="vc")
                    xs = self.xfm(b, dc)[:, nb4 * 512:(nb4 + 1) * 512]
                    ev = p.do("v", lambda v, xs=xs, fs2=fs2: v.scalar_tensor_tensor(
                        xs, fs2, 0.5, xs, ALU.mult, ALU.add), inc="vc")
                    dgate[pi] = ("vc", ev)
            self.xfm_ready = ("vc", p.now("vc"))


def build_program(parts):
    nc = bass.Bass("TRN2", target_bir_lowering=False)
    ctx = ExitStack()
    with ctx:
        k = Kern(nc, ctx, parts)
        k.declare_io()
        k.alloc()
        block = ctx.enter_context(nc.Block())
        k.emit_init()
        k.emit_load_x()
        for part in parts:
            if part == "l0ffn":
                k.emit_l0ffn()
            elif part == "l0attn":
                k.emit_l0attn()
            elif part == "l1attn":
                k.emit_l1attn()
            elif part == "l1ffn_a":
                k.emit_l1ffn(first=True)
            elif part == "l1ffn_b":
                k.emit_l1ffn(first=False)
        dbg = os.environ.get("KDBG", "")
        if dbg:
            p = k.p
            srcs = {"A": k.A, "Bar": k.Bar, "PMRN": k.PMRN, "DT": k.DT,
                    "SL": k.SL, "WB": k.WB, "XFM": k.XFM, "ST": k.ST}[dbg.split(":")[0]]
            off = int(dbg.split(":")[1]) if ":" in dbg else 0
            nmax = srcs.shape[1]
            w = min(16384, nmax - off)
            cw = min(w, 1024)
            dst0 = k.y_out[0, :, :].rearrange("(a p) c -> p a c", p=128)[:, 0:max(w // cw, 1), 0:cw]
            src3 = srcs[:, off:off + w].rearrange("p (a c) -> p a c", c=cw)
            p.do("g", lambda g: g.dma_start(dst0, src3),
                 wait=(("vc", p.now("vc")), ("sc", p.now("sc")),
                       ("tc", p.now("tc"))), inc="gld", inc_amt=16)
            p.do("g", lambda g: None, wait=(("gld", p.now("gld")),))
        if not dbg:
            k.emit_store_y()
        k.p.build(block)
    return nc


LAUNCH_PLAN = [
    ["l0ffn", "l0attn", "l0ffn", "l1ffn_a", "l1attn", "l1ffn_b"],
]


def _norm_parts(parts):
    return ["l1ffn" if p.startswith("l1ffn") else p for p in parts]


def prep_inputs(inp):
    f = np.ascontiguousarray
    d = {}
    d["w1"] = f(inp["L0_ffn_w1"])
    d["w2"] = f(inp["L0_ffn_w2"])
    d["se"] = f(inp["L0_ffn_slot_embeds"].reshape(4096, 1024))
    d["gnc"] = f(inp["L0_ffn_norm_gamma"].reshape(8, 128).T)
    d["gsc"] = f(inp["L0_ffn_slot_norm_gamma"].reshape(8, 128).T)
    d["b1c"] = f(inp["L0_ffn_b1"].reshape(8, 32, 128).transpose(0, 2, 1))
    d["b2r"] = f(inp["L0_ffn_b2"].reshape(8, 1, 1024))
    d["u_n"] = f(inp["L0_attn_ternary_factor_u"])
    d["uT"] = f(inp["L0_attn_ternary_factor_u"].T)
    d["v_n"] = f(inp["L0_attn_ternary_factor_v"])
    d["vT"] = f(inp["L0_attn_ternary_factor_v"].T)
    rows, cols = np.tril_indices(64)
    tril = np.zeros((16, 64, 64), dtype=np.float32)
    tril[:, rows, cols] = inp["L1_attn_params"]
    d["trilT"] = f(tril.transpose(0, 2, 1))
    d["WvT"] = f(inp["L1_attn_Wv"].T)
    d["bvr"] = f(inp["L1_attn_bv"].reshape(1, 1024))
    d["WoT"] = f(inp["L1_attn_Wo"].T)
    d["bor"] = f(inp["L1_attn_bo"].reshape(1, 1024))
    g1r = inp["L1_ffn_G1_R"][0].transpose(1, 0, 2).reshape(4, 1024)
    g1i = inp["L1_ffn_G1_I"][0].transpose(1, 0, 2).reshape(4, 1024)
    g2r = inp["L1_ffn_G2_R"][:, :, 0, :].reshape(4, 256)
    g2i = inp["L1_ffn_G2_I"][:, :, 0, :].reshape(4, 256)
    d["G1r"] = f(np.concatenate([g1r, g1i, g2r, g2i], axis=1))
    d["brr"] = f(inp["L1_ffn_bias_real"].reshape(1, 512))
    d["bir"] = f(inp["L1_ffn_bias_imag"].reshape(1, 512))
    return d


def kernel(**inputs):
    x = np.ascontiguousarray(np.asarray(inputs["L0_attn_qz"], dtype=np.float32))
    wd = prep_inputs({k: np.asarray(v, dtype=np.float32) for k, v in inputs.items()})
    shards = [x[2 * i:2 * i + 2] for i in range(8)]
    progs = {}
    for parts in LAUNCH_PLAN:
        key = tuple(parts)
        if key not in progs:
            progs[key] = build_program(parts)
    cur = shards
    for parts in LAUNCH_PLAN:
        nc = progs[tuple(parts)]
        needed = {"l0ffn": ["w1", "w2", "se", "gnc", "gsc", "b1c", "b2r"],
                  "l0attn": ["u_n", "uT", "v_n", "vT"],
                  "l1attn": ["trilT", "WvT", "bvr", "WoT", "bor"],
                  "l1ffn": ["G1r", "brr", "bir"]}
        names = set()
        for pp_ in _norm_parts(parts):
            names.update(needed[pp_])
        in_maps = []
        for ci in range(8):
            m = {"x": cur[ci]}
            for nm in names:
                m[nm] = wd[nm]
            in_maps.append(m)
        res = run_bass_kernel_spmd(nc, in_maps, core_ids=list(range(8)))
        cur = [np.asarray(r["y"]) for r in res.results]
    return np.concatenate(cur, axis=0)


# version 08 (final)
# speedup vs baseline: 1.2964x; 1.0318x over previous
import os
import numpy as np
from contextlib import ExitStack
import concourse.bass as bass
import concourse.mybir as mybir
from concourse.bass_utils import run_bass_kernel_spmd

F32 = mybir.dt.float32
BF16 = mybir.dt.bfloat16
AF = mybir.ActivationFunctionType
ALU = mybir.AluOpType
AX = mybir.AxisListType

NB = 2
NT = 2048
DM = 1024
NCH = 8
NE = 8
HID = 4096

ST_RM = 0
ST_CS = 32
ST_NAM = 64
ST_RN = 96
ST_MS1 = 112
ST_NMX = 128
ST_RS = 144
ST_RR = 160
ST_B1 = 176
ST_GN = 208
ST_GS = 216
ST_MS2 = 224


class P:
    ENGS = ("g", "t", "v", "s", "y")

    def __init__(self, nc, ctx):
        self.nc = nc
        self.ops = {e: [] for e in self.ENGS}
        self.cnt = {}
        self.sems = {}
        for name in ("gld", "yld", "gc", "tc", "vc", "sc"):
            self.sems[name] = ctx.enter_context(nc.semaphore(name))
            self.cnt[name] = 0

    def do(self, eng, fn, wait=(), inc=None, inc_amt=1):
        wait = tuple((s, v) for (s, v) in wait if v > 0)
        for s, v in wait:
            assert v <= self.cnt[s], f"wait {s}>={v} but planned only {self.cnt[s]}"
        if inc is not None:
            self.cnt[inc] += inc_amt
        sems = self.sems

        def emit(e, fn=fn, wait=wait, inc=inc, inc_amt=inc_amt):
            for s, v in wait:
                e.wait_ge(sems[s], v)
            inst = fn(e)
            if inc is not None:
                assert inst is not None
                inst.then_inc(sems[inc], inc_amt)
        self.ops[eng].append(emit)
        return self.cnt[inc] if inc is not None else None

    def now(self, name):
        return self.cnt[name]

    def snap(self):
        return dict(self.cnt)

    def build(self, block):
        ops = self.ops

        @block.gpsimd
        def _(g):
            for f in ops["g"]:
                f(g)

        @block.tensor
        def _(t):
            for f in ops["t"]:
                f(t)

        @block.vector
        def _(v):
            for f in ops["v"]:
                f(v)

        @block.scalar
        def _(s):
            for f in ops["s"]:
                f(s)

        @block.sync
        def _(y):
            for f in ops["y"]:
                f(y)


class Kern:
    def __init__(self, nc, ctx, parts):
        self.nc = nc
        self.ctx = ctx
        self.parts = list(parts)
        self.p = P(nc, ctx)
        self.xfm_ready = ("vc", 0)

    def declare_io(self):
        nc = self.nc
        def ap(name, shape, dt, kind):
            return nc.dram_tensor(name, shape, dt, kind=kind).ap()
        I, O, N = "ExternalInput", "ExternalOutput", "Internal"
        self.x_in = ap("x", [NB, NT, DM], F32, I)
        self.y_out = ap("y", [NB, NT, DM], F32, O)
        pset = {"l1ffn" if q.startswith("l1ffn") else q for q in self.parts}
        if "l0ffn" in pset:
            self.w1 = ap("w1", [NE, DM, HID], F32, I)
            self.w2 = ap("w2", [NE, HID, DM], F32, I)
            self.se = ap("se", [NE * 512, DM], F32, I)
            self.gnc = ap("gnc", [128, NCH], F32, I)
            self.gsc = ap("gsc", [128, NCH], F32, I)
            self.b1c = ap("b1c", [NE, 128, 32], F32, I)
            self.b2r = ap("b2r", [NE, 1, DM], F32, I)
            self.Lsp = ap("Lsp", [NB, 32, 128, NT], F32, N)
            self.eod = ap("eod", [NB, NE * 512, DM], BF16, N)
        if "l0attn" in pset:
            self.u_n = ap("u_n", [DM, DM], F32, I)
            self.uT = ap("uT", [DM, DM], F32, I)
            self.v_n = ap("v_n", [DM, DM], F32, I)
            self.vT = ap("vT", [DM, DM], F32, I)
        if "l1attn" in pset:
            self.trilT = ap("trilT", [16, 64, 64], F32, I)
            self.WvT = ap("WvT", [DM, DM], F32, I)
            self.bvr = ap("bvr", [1, DM], F32, I)
            self.WoT = ap("WoT", [DM, DM], F32, I)
            self.bor = ap("bor", [1, DM], F32, I)
        if "l1ffn" in pset:
            self.G1r = ap("G1r", [4, 2560], F32, I)
            self.brr = ap("brr", [1, 512], F32, I)
            self.bir = ap("bir", [1, 512], F32, I)
            self.pdr = ap("pdr", [1024, 256], F32, N)
            self.wts = ap("wts", [128, 6144], BF16, N)
        self.bncb = ap("bncb", [1, 2048], BF16, N)

    def alloc(self):
        nc, E = self.nc, self.ctx.enter_context
        self.XFM = E(nc.sbuf_tensor("XFM", [128, NB * NCH * NT], BF16))
        self.A = E(nc.sbuf_tensor("Abuf", [128, 32768], BF16))
        self.Bar = E(nc.sbuf_tensor("Bar", [128, 16384], BF16))
        self.DT = E(nc.sbuf_tensor("DT", [128, 6144], BF16))
        self.WB = E(nc.sbuf_tensor("WB", [128, 8192], BF16))
        self.SL = E(nc.sbuf_tensor("SL", [128, 2048], BF16))
        self.FSA = E(nc.sbuf_tensor("FSA", [128, 2048], F32))
        self.PMRN = E(nc.sbuf_tensor("PMRN", [128, 2048], BF16))
        self.ST = E(nc.sbuf_tensor("STT", [128, 256], F32))
        self.ident = E(nc.sbuf_tensor("ident", [128, 128], BF16))
        self.ones = E(nc.sbuf_tensor("onesb", [128, 512], BF16))
        self.PL = E(nc.psum_tensor("PL", [128, 2048], F32))
        self.PM = E(nc.psum_tensor("PM", [128, 2048], F32))
        self.PA = self.PM[:, 0:512]
        self.PB = self.PM[:, 512:1024]
        self.PC = self.PM[:, 1024:1536]
        self.PD = self.PM[:, 1536:2048]

    def xfm(self, b, c):
        return self.XFM[:, (b * NCH + c) * NT:(b * NCH + c) * NT + NT]

    def emit_init(self):
        p = self.p
        p.do("g", lambda g: g.memset(self.ones[:, :], 1.0), inc="gc")
        p.do("g", lambda g: g.memset(self.ident[:, :], 0.0), inc="gc")
        v = p.now("gc")
        p.do("g", lambda g: g.affine_select(
            out=self.ident[:, :], in_=self.ident[:, :],
            compare_op=ALU.not_equal, fill=1.0, base=0,
            pattern=[[-1, 128]], channel_multiplier=1), wait=(("gc", v),), inc="gc")
        p.do("g", lambda g: g.memset(self.ST[:, ST_MS2 + 4:ST_MS2 + 5], 1e-5),
             inc="gc")
        if "l0ffn" in self.parts:
            p.do("y", lambda y: y.dma_start(self.ST[:, ST_GN:ST_GN + 8], self.gnc[:, :]),
                 inc="yld", inc_amt=16)
            p.do("y", lambda y: y.dma_start(self.ST[:, ST_GS:ST_GS + 8], self.gsc[:, :]),
                 inc="yld", inc_amt=16)
        self.init_gc = p.now("gc")
        self.init_yld = p.now("yld")

    def emit_load_x(self):
        p = self.p
        xin = self.SL[:, 0:1024]
        pbf = self.PL[:, 0:512].bitcast(BF16)
        xfm3 = self.XFM[:, :].rearrange("p (k n) -> p k n", n=NT)
        for b in range(NB):
            for k in range(16):
                ld = p.do("g", lambda g, b=b, k=k: g.dma_start(
                    xin, self.x_in[b, k * 128:(k + 1) * 128, :]),
                    wait=(("tc", p.now("tc")),), inc="gld", inc_amt=16)
                tw = [("gld", ld), ("gc", self.init_gc), ("vc", p.now("vc"))]
                tcv = None
                for c in range(NCH):
                    tcv = p.do("t", lambda t, c=c: t.transpose(
                        pbf[:, c * 128:(c + 1) * 128], xin[:, c * 128:(c + 1) * 128],
                        self.ident[:, :]),
                        wait=tw if c == 0 else (), inc="tc" if c == NCH - 1 else None)
                dst = xfm3[:, b * NCH:(b + 1) * NCH, k * 128:k * 128 + 128]
                src = pbf[:, :].rearrange("p (c j) -> p c j", j=128)
                p.do("v", lambda v, dst=dst, src=src: v.tensor_copy(dst, src),
                     wait=(("tc", tcv),), inc="vc")
        self.xfm_ready = ("vc", p.now("vc"))

    def emit_store_y(self):
        p = self.p
        xout = self.FSA[:, 0:1024]
        pbf = self.PL[:, 0:512].bitcast(BF16)
        xfm3 = self.XFM[:, :].rearrange("p (k n) -> p k n", n=NT)
        rs, rv = self.xfm_ready
        for b in range(NB):
            for k in range(16):
                src3 = xfm3[:, b * NCH:(b + 1) * NCH, k * 128:k * 128 + 128]
                tcv = None
                for c in range(NCH):
                    tw = [(rs, rv), ("vc", p.now("vc")), ("yld", p.now("yld"))] if c == 0 else []
                    tcv = p.do("t", lambda t, c=c, src3=src3: t.transpose(
                        pbf[:, c * 128:(c + 1) * 128], src3[:, c, :], self.ident[:, :]),
                        wait=tw, inc="tc" if c == NCH - 1 else None)
                p.do("v", lambda v: v.tensor_copy(xout, pbf[:, :]),
                     wait=(("tc", tcv),), inc="vc")
                p.do("y", lambda y, b=b, k=k: y.dma_start(
                    self.y_out[b, k * 128:(k + 1) * 128, :], xout),
                    wait=(("vc", p.now("vc")),), inc="yld", inc_amt=16)
        p.do("y", lambda y: None, wait=(("yld", p.now("yld")),))

    def emit_ln(self, b, dst, rms=False):
        p = self.p
        sq = [self.SL[:, 0:512], self.SL[:, 512:1024]]
        rs, rv = self.xfm_ready
        r0 = lambda q: self.FSA[0:1, 0:512]
        r1 = lambda q: self.FSA[0:1, 512:1024]
        r2 = lambda q: self.FSA[0:1, 1024:1536]
        r3 = lambda q: self.FSA[0:1, 1536:2048]
        TMPB = self.A[:, 0:2048]
        SRNB = self.A[:, 2048:4096]
        MSRB = self.A[:, 4096:6144]

        for q in range(4):
            n0 = q * 512
            wA = [(rs, rv), ("vc", p.now("vc")), ("sc", p.now("sc")),
                  ("yld", p.now("yld"))]
            sqv = []
            for c in range(2):
                sqv.append(p.do("s", lambda s, c=c, n0=n0: s.activation(
                    sq[c % 2], self.xfm(b, c)[:, n0:n0 + 512], AF.Square),
                    wait=[(rs, rv), ("tc", p.now("tc")), ("yld", p.now("yld")),
                          ("gld", p.now("gld"))] if c == 0 else (), inc="sc"))
            tsum = None
            for c in range(NCH):
                tsum = p.do("t", lambda t, c=c, n0=n0: t.matmul(
                    self.PA[0:1, :], self.ones[:, 0:1], self.xfm(b, c)[:, n0:n0 + 512],
                    start=(c == 0), stop=(c == NCH - 1)),
                    wait=wA if c == 0 else (), inc="tc" if c == NCH - 1 else None)
            tssq = None
            for c in range(NCH):
                tssq = p.do("t", lambda t, c=c: t.matmul(
                    self.PB[0:1, :], self.ones[:, 0:1], sq[c % 2],
                    start=(c == 0), stop=(c == NCH - 1)),
                    wait=[("sc", sqv[c]), ("vc", p.now("vc"))] if c == 0 else
                         [("sc", sqv[c])], inc="tc")
                if c + 2 < NCH:
                    sqv.append(p.do("s", lambda s, c=c, n0=n0: s.activation(
                        sq[c % 2], self.xfm(b, c + 2)[:, n0:n0 + 512], AF.Square),
                        wait=(("tc", tssq),), inc="sc"))
            p.do("v", lambda v, q=q: v.tensor_copy(r0(q), self.PA[0:1, :]),
                 wait=(("tc", p.now("tc")), ("sc", p.now("sc"))), inc="vc")
            p.do("v", lambda v, q=q: v.tensor_copy(r1(q), self.PB[0:1, :]), inc="vc")
            p.do("v", lambda v, q=q: v.tensor_scalar_mul(r2(q), r0(q), 1.0 / DM),
                 inc="vc")
            p.do("v", lambda v, q=q: v.tensor_scalar_mul(r1(q), r1(q), 1.0 / DM),
                 inc="vc")
            p.do("v", lambda v, q=q: v.tensor_mul(r3(q), r2(q), r2(q)), inc="vc")
            p.do("v", lambda v, q=q: v.tensor_sub(r1(q), r1(q), r3(q)), inc="vc")
            vv = p.now("vc")
            sv = p.do("s", lambda s, q=q: s.activation(
                r3(q), r1(q), AF.Sqrt, bias=self.ST[0:1, ST_MS2 + 4:ST_MS2 + 5]),
                wait=(("vc", vv), ("gc", self.init_gc)), inc="sc")
            p.do("v", lambda v, q=q: v.reciprocal(r0(q), r3(q)),
                 wait=(("sc", sv),), inc="vc")
            if rms:
                sv = p.do("s", lambda s, q=q: s.activation(r3(q), r1(q), AF.Sqrt),
                          wait=(("vc", p.now("vc")),), inc="sc")
                p.do("v", lambda v, q=q: v.tensor_mul(r3(q), r3(q), r0(q)),
                     wait=(("sc", sv),), inc="vc")
                p.do("v", lambda v, q=q: v.tensor_scalar(r3(q), r3(q), 32.0, 1e-12,
                                                         ALU.mult, ALU.max), inc="vc")
                p.do("v", lambda v, q=q: v.reciprocal(r1(q), r3(q)), inc="vc")
                p.do("v", lambda v, q=q: v.tensor_mul(r0(q), r0(q), r1(q)), inc="vc")
                p.do("v", lambda v, q=q: v.tensor_scalar_mul(r0(q), r0(q), 32.0),
                     inc="vc")
            p.do("v", lambda v, q=q: v.tensor_mul(r2(q), r2(q), r0(q)), inc="vc")
            p.do("v", lambda v, q=q: v.tensor_copy(self.SL[0:1, 1024:1536], r0(q)),
                 inc="vc")
            p.do("v", lambda v, q=q: v.tensor_copy(self.SL[0:1, 1536:2048], r2(q)),
                 inc="vc")
            vrow = p.now("vc")
            p.do("t", lambda t, q=q: t.matmul(
                self.PA[:, :], self.ones[0:1, 0:128], self.SL[0:1, 1024:1536],
                start=True, stop=True),
                wait=[("vc", vrow)], inc="tc")
            tb = p.do("t", lambda t, q=q: t.matmul(
                self.PB[:, :], self.ones[0:1, 0:128], self.SL[0:1, 1536:2048],
                start=True, stop=True),
                inc="tc")
            p.do("v", lambda v, q=q: v.tensor_copy(SRNB[:, q * 512:(q + 1) * 512],
                                                   self.PA[:, :]),
                 wait=[("tc", tb)] + ([("tc", p.now("tc")), ("sc", p.now("sc"))]
                                      if q == 0 else []), inc="vc")
            p.do("v", lambda v, q=q: v.tensor_copy(MSRB[:, q * 512:(q + 1) * 512],
                                                   self.PB[:, :]), inc="vc")
        vbc = p.now("vc")
        for c in range(NCH):
            p.do("v", lambda v, c=c: v.tensor_mul(TMPB, self.xfm(b, c), SRNB),
                 wait=[("vc", vbc), ("yld", self.init_yld),
                       ("tc", p.now("tc")), ("sc", p.now("sc"))] if c == 0 else (),
                 inc="vc")
            p.do("v", lambda v, c=c: v.tensor_sub(dst(c), TMPB, MSRB), inc="vc")
            if rms:
                p.do("v", lambda v, c=c: v.tensor_scalar_mul(
                    dst(c), dst(c), self.ST[:, ST_GN + c:ST_GN + c + 1]), inc="vc")
        return p.now("vc")

    def emit_l1ffn(self, first):
        p = self.p
        WRT = self.DT[:, 0:2048]
        WIT = self.DT[:, 2048:4096]
        NIT = self.DT[:, 4096:6144]
        if first:
            g1 = self.WB[0:4, 0:2560]
            l1 = p.do("g", lambda g: g.dma_start(g1, self.G1r[:, :]),
                      wait=(("vc", p.now("vc")), ("sc", p.now("sc"))),
                      inc="gld", inc_amt=16)
            fs = self.FSA[:, 0:512]
            wn = self.WB[:, 2560:4608]
            pbf = self.PC[:, :].bitcast(BF16)
            gq_prev = p.now("gld")
            for ri in range(2):
                for ms in range(8):
                    tv = p.do("t", lambda t, ri=ri, ms=ms: t.matmul(
                        self.PA[:, 0:256],
                        g1[:, ri * 1024 + ms * 128:ri * 1024 + (ms + 1) * 128],
                        g1[:, 2048 + ri * 256:2048 + ri * 256 + 256],
                        start=True, stop=True),
                        wait=[("gld", max(l1, gq_prev)), ("vc", p.now("vc")),
                              ("yld", p.now("yld"))], inc="tc")
                    vv = p.do("v", lambda v: v.tensor_copy(fs[:, 0:256], self.PA[:, 0:256]),
                              wait=(("tc", tv), ("yld", p.now("yld"))), inc="vc")
                    p.do("y", lambda y, ms=ms: y.dma_start(
                        self.pdr[ms * 128:(ms + 1) * 128, :], fs[:, 0:256]),
                        wait=(("vc", vv),), inc="yld", inc_amt=16)
                yv = p.now("yld")
                lw = None
                for b32 in range(32):
                    rc = b32 // 8
                    dst = wn[(b32 % 8) * 16:(b32 % 8) * 16 + 16,
                             rc * 512:(rc + 1) * 512].rearrange("p (d g) -> p d g", g=16)
                    src2 = self.pdr[b32 * 32:(b32 + 1) * 32, :].rearrange(
                        "d (e g) -> e d g", g=16)
                    lw = p.do("g", lambda g, dst=dst, src2=src2: g.dma_start(dst, src2),
                              wait=[("yld", yv), ("tc", p.now("tc"))] if b32 == 0 else (),
                              inc="gld", inc_amt=16)
                gq_prev = lw
                dstT = WRT if ri == 0 else WIT
                for grp in range(2):
                    tv = None
                    for j in range(8):
                        kk = grp * 8 + j
                        rc, jc = kk // 4, kk % 4
                        tv = p.do("t", lambda t, rc=rc, jc=jc, j=j: t.transpose(
                            pbf[:, j * 128:(j + 1) * 128],
                            wn[:, rc * 512 + jc * 128:rc * 512 + (jc + 1) * 128],
                            self.ident[:, :]),
                            wait=[("gld", lw), ("vc", p.now("vc"))] if j == 0 else (),
                            inc="tc" if j == 7 else None)
                    for j in range(8):
                        kk = grp * 8 + j
                        rc, jc = kk // 4, kk % 4
                        p.do("v", lambda v, rc=rc, jc=jc, j=j, dstT=dstT: v.tensor_copy(
                            dstT[:, jc * 512 + rc * 128:jc * 512 + (rc + 1) * 128],
                            pbf[:, j * 128:(j + 1) * 128]),
                            wait=[("tc", tv)] if j == 0 else (), inc="vc")
            p.do("v", lambda v: v.tensor_scalar_mul(NIT, WIT, -1.0), inc="vc")
            wv = p.now("vc")
            p.do("y", lambda y: y.dma_start(self.wts[:, :], self.DT[:, :]),
                 wait=(("vc", wv),), inc="yld", inc_amt=16)
            self.l1ffn_w_done = p.now("yld")
            wgate = ("vc", wv)
        else:
            lw = p.do("g", lambda g: g.dma_start(self.DT[:, :], self.wts[:, :]),
                      wait=(("vc", p.now("vc")), ("sc", p.now("sc")),
                            ("yld", self.l1ffn_w_done)),
                      inc="gld", inc_amt=16)
            wgate = ("gld", lw)

        lbr = p.do("g", lambda g: g.dma_start(self.PMRN[0:1, 0:512], self.brr[:, :]),
                   wait=(("tc", p.now("tc")), ("sc", p.now("sc")),
                         ("vc", p.now("vc"))), inc="gld", inc_amt=16)
        lbr = p.do("g", lambda g: g.dma_start(self.PMRN[0:1, 512:1024], self.bir[:, :]),
                   inc="gld", inc_amt=16)
        for b in range(NB):
            lnv = self.emit_ln(b, lambda c: self.Bar[:, c * 2048:(c + 1) * 2048])
            pp = [self.PA, self.PB]
            gate = {0: ("vc", 0), 1: ("vc", 0)}
            for oc in range(NCH):
                real = oc < 4
                om = oc % 4
                for nb4 in range(4):
                    pi = (oc * 4 + nb4) % 2
                    ps = pp[pi]
                    bias_row = (self.PMRN[0:1, 0:512] if real
                                else self.PMRN[0:1, 512:1024])
                    wa, wb_ = (WRT, NIT) if real else (WIT, WRT)
                    for k in range(4):
                        p.do("t", lambda t, k=k, om=om, nb4=nb4, wa=wa, ps=ps: t.matmul(
                            ps[:, :], wa[:, k * 512 + om * 128:k * 512 + (om + 1) * 128],
                            self.Bar[:, k * 2048 + nb4 * 512:k * 2048 + nb4 * 512 + 512],
                            start=(k == 0), stop=False),
                            wait=[("vc", lnv), wgate, gate[pi]] if k == 0 else ())
                    for k in range(4):
                        p.do("t", lambda t, k=k, om=om, nb4=nb4, wb_=wb_, ps=ps: t.matmul(
                            ps[:, :], wb_[:, k * 512 + om * 128:k * 512 + (om + 1) * 128],
                            self.Bar[:, (4 + k) * 2048 + nb4 * 512:(4 + k) * 2048 + nb4 * 512 + 512],
                            start=False, stop=False))
                    tv = p.do("t", lambda t, om=om, ps=ps, bias_row=bias_row: t.matmul(
                        ps[:, :], bias_row[:, om * 128:(om + 1) * 128],
                        self.ones[0:1, 0:512], start=False, stop=True),
                        wait=(("gld", lbr),), inc="tc")
                    xs = self.xfm(b, oc)[:, nb4 * 512:(nb4 + 1) * 512]
                    ev = p.do("v", lambda v, xs=xs, ps=ps: v.scalar_tensor_tensor(
                        xs, ps[:, :], 0.5, xs, ALU.mult, ALU.add),
                        wait=(("tc", tv),), inc="vc")
                    gate[pi] = ("vc", ev)
            self.xfm_ready = ("vc", p.now("vc"))

    def emit_l0attn(self):
        p = self.p
        for b in range(NB):
            lnv = self.emit_ln(b, lambda c: self.Bar[:, c * 2048:(c + 1) * 2048])
            pp = [self.PA, self.PB]
            gate = {0: ("vc", 0), 1: ("sc", 0)}
            slot_gate = [("tc", 0)] * 4
            pcount = 0
            for m in range(8):
                lds = []
                for ti, wsrc in enumerate((self.uT, self.vT)):
                    si = ti * 2 + (m % 2)
                    slot = self.DT[:, si * 1024:(si + 1) * 1024]
                    src = wsrc[:, m * 128:(m + 1) * 128].rearrange(
                        "(k q) c -> q k c", q=128)
                    dst3 = slot.rearrange("q (k c) -> q k c", c=128)
                    ld = p.do("g", lambda g, dst3=dst3, src=src: g.dma_start(dst3, src),
                              wait=[slot_gate[si], ("vc", lnv)], inc="gld", inc_amt=16)
                    lds.append(ld)
                for ti in range(2):
                    si = ti * 2 + (m % 2)
                    slot = self.DT[:, si * 1024:(si + 1) * 1024]
                    for nb4 in range(4):
                        pi = pcount % 2
                        pcount += 1
                        ps = pp[pi]
                        tv = None
                        for k in range(NCH):
                            tv = p.do("t", lambda t, k=k, m=m, nb4=nb4, slot=slot, ps=ps: t.matmul(
                                ps[:, :], slot[:, k * 128:(k + 1) * 128],
                                self.Bar[:, k * 2048 + nb4 * 512:k * 2048 + nb4 * 512 + 512],
                                start=(k == 0), stop=(k == NCH - 1)),
                                wait=[("gld", lds[ti]), ("vc", lnv), gate[pi]] if k == 0 else (),
                                inc="tc" if k == NCH - 1 else None)
                        dst = self.A[:, ti * 16384 + m * 2048 + nb4 * 512:
                                     ti * 16384 + m * 2048 + nb4 * 512 + 512]
                        ev = p.do("s", lambda s, dst=dst, ps=ps: s.mul(dst, ps[:, :], 0.03125),
                                  wait=(("tc", tv),), inc="sc")
                        gate[pi] = ("sc", ev)
                    slot_gate[si] = ("tc", p.now("tc"))
            SKIP = os.environ.get("L0SKIP", "")
            pbf = [self.PC[:, :].bitcast(BF16), self.PD[:, :].bitcast(BF16)]
            vbank = [self.PA, self.PB, self.PC, self.PD]
            qh_row = self.Bar[:, 4096:6144]
            qhT_row = self.Bar[:, 6144:8192]
            bounce = self.Bar[:, 8192:10240]
            zrow = self.SL[0:1, 0:128]
            zv = p.do("g", lambda g: g.memset(zrow, 0.0),
                      wait=(("sc", p.now("sc")), ("tc", p.now("tc"))), inc="gc")
            prev_pair_done = ("vc", p.now("vc"))
            prev_bounce = ("gld", p.now("gld"))
            for pr in range(0 if "heads" not in SKIP else 8, 8):
                tmv = []
                for ti in range(2 if "notm" not in SKIP else 0):
                    for grp in range(2):
                        tv = None
                        for j in range(8):
                            srcap = self.A[:, ti * 16384 + pr * 2048 + (grp * 8 + j) * 128:
                                           ti * 16384 + pr * 2048 + (grp * 8 + j) * 128 + 128]
                            tv = p.do("t", lambda t, j=j, ti=ti, srcap=srcap: t.transpose(
                                pbf[ti][:, j * 128:(j + 1) * 128], srcap, self.ident[:, :]),
                                wait=([("sc", p.now("sc")), ("vc", prev_pair_done[1]),
                                       ("gld", p.now("gld"))] if grp == 0 and ti == 0 else
                                      [("vc", p.now("vc"))]) if j == 0 else (),
                                inc="tc" if j == 7 else None)
                        dst = self.Bar[:, ti * 2048 + grp * 1024:ti * 2048 + (grp + 1) * 1024]
                        ev = p.do("v", lambda v, dst=dst, ti=ti: v.tensor_copy(
                            dst, pbf[ti][:, 0:1024]), wait=(("tc", tv),), inc="vc")
                        tmv.append(ev)
                tm_done = p.now("vc")
                ev_gate = ("vc", tm_done)
                for h in range(2):
                    base = 64 * h
                    for tb in range(0 if "noz" not in SKIP else 4, 4):
                        p.do("t", lambda t, tb=tb: t.matmul(
                            vbank[tb][:, :], zrow, self.ones[0:1, 0:512],
                            start=True, stop=False),
                            wait=[ev_gate, ("gc", zv)] if tb == 0 else ())
                    plg = {0: ("sc", 0), 1: ("sc", 0)}
                    for i in range(0 if "norows" not in SKIP else 16, 16):
                        sg = {}
                        sgT = {}
                        tvf = {}
                        tvft = {}
                        for th in range(2):
                            for q in range(2):
                                tb = th * 2 + q
                                tvf[th] = p.do("t", lambda t, i=i, tb=tb, pr=pr, base=base: t.matmul(
                                    self.PL[:, tb * 512:(tb + 1) * 512],
                                    self.A[base:base + 64, pr * 2048 + i * 128:pr * 2048 + (i + 1) * 128],
                                    self.A[base:base + 64,
                                           16384 + pr * 2048 + tb * 512:16384 + pr * 2048 + tb * 512 + 512],
                                    start=True, stop=True),
                                    wait=([("sc", p.now("sc")), ("vc", tm_done), ev_gate]
                                          if i == 0 and th == 0 and q == 0 else
                                          ([plg[th]] if q == 0 else ())),
                                    inc="tc" if q == 1 else None)
                        for th in range(2):
                            sg[th] = p.do("s", lambda s, th=th: s.activation(
                                qh_row[:, th * 1024:(th + 1) * 1024],
                                self.PL[:, th * 1024:(th + 1) * 1024], AF.Sigmoid),
                                wait=(("tc", tvf[th]), ("gld", p.now("gld"))), inc="sc")
                        for th in range(2):
                            for q in range(2):
                                tb = th * 2 + q
                                p.do("t", lambda t, i=i, tb=tb, base=base: t.matmul(
                                    vbank[tb][64:128, :],
                                    self.Bar[:, i * 128 + base:i * 128 + base + 64],
                                    qh_row[:, tb * 512:(tb + 1) * 512],
                                    start=False, stop=False, tile_position=(0, 64)),
                                    wait=[("sc", sg[th])] if q == 0 else ())
                            for q in range(2):
                                tb = th * 2 + q
                                tvft[th] = p.do("t", lambda t, i=i, tb=tb, pr=pr, base=base: t.matmul(
                                    self.PL[:, tb * 512:(tb + 1) * 512],
                                    self.A[base:base + 64,
                                           16384 + pr * 2048 + i * 128:16384 + pr * 2048 + (i + 1) * 128],
                                    self.A[base:base + 64, pr * 2048 + tb * 512:pr * 2048 + tb * 512 + 512],
                                    start=True, stop=True),
                                    inc="tc" if q == 1 else None)
                        for th in range(2):
                            sgT[th] = p.do("s", lambda s, th=th: s.activation(
                                qhT_row[:, th * 1024:(th + 1) * 1024],
                                self.PL[:, th * 1024:(th + 1) * 1024], AF.Sigmoid),
                                wait=(("tc", tvft[th]),), inc="sc")
                            plg[th] = ("sc", sgT[th])
                        for th in range(2):
                            for q in range(2):
                                sb = th * 2 + q
                                p.do("t", lambda t, i=i, sb=sb, base=base: t.matmul(
                                    vbank[sb][0:64, :],
                                    self.Bar[:, 2048 + i * 128 + base:2048 + i * 128 + base + 64],
                                    qhT_row[:, sb * 512:(sb + 1) * 512],
                                    start=False, stop=False, tile_position=(0, 0)),
                                    wait=[("sc", sgT[th])] if q == 0 else ())

                    vdone = None
                    for tb in range(0 if "noz" not in SKIP else 4, 4):
                        vdone = p.do("t", lambda t, tb=tb: t.matmul(
                            vbank[tb][:, :], zrow, self.ones[0:1, 0:512],
                            start=False, stop=True),
                            inc="tc" if tb == 3 else None)
                    evs = []
                    for ti, rows0 in (((0, 0), (1, 64)) if "noevac" not in SKIP else ()):
                        direct = (rows0 == base)
                        for tb in range(4):
                            if direct:
                                dst = self.A[base:base + 64,
                                             ti * 16384 + pr * 2048 + tb * 512:
                                             ti * 16384 + pr * 2048 + tb * 512 + 512]
                            else:
                                dst = bounce[rows0:rows0 + 64, tb * 512:(tb + 1) * 512]
                            ev = p.do("v", lambda v, dst=dst, tb=tb, rows0=rows0: v.tensor_copy(
                                dst, vbank[tb][rows0:rows0 + 64, :]),
                                wait=[("tc", vdone), prev_bounce] if (tb == 0 and ti == 0 and vdone) else [prev_bounce] if tb == 0 and ti == 0 else (),
                                inc="vc")
                        if (not direct) and ("nobounce" not in SKIP):
                            bv = p.do("g", lambda g, ti=ti, pr=pr, rows0=rows0, base=base: g.dma_start(
                                self.A[base:base + 64,
                                       ti * 16384 + pr * 2048:ti * 16384 + (pr + 1) * 2048],
                                bounce[rows0:rows0 + 64, :]),
                                wait=(("vc", p.now("vc")),), inc="gld", inc_amt=16)
                            prev_bounce = ("gld", bv)
                    ev_gate = ("vc", p.now("vc"))
                prev_pair_done = ("vc", p.now("vc"))
            gate = {0: ("vc", 0), 1: ("vc", 0)}
            slot_gate = [("tc", 0)] * 4
            pcount = 0
            for dc in range(8):
                lds = []
                for ti, wsrc in enumerate((self.u_n, self.v_n)):
                    si = ti * 2 + (dc % 2)
                    slot = self.DT[:, si * 1024:(si + 1) * 1024]
                    src = wsrc[:, dc * 128:(dc + 1) * 128].rearrange(
                        "(k q) c -> q k c", q=128)
                    dst3 = slot.rearrange("q (k c) -> q k c", c=128)
                    ld = p.do("g", lambda g, dst3=dst3, src=src: g.dma_start(dst3, src),
                              wait=[slot_gate[si], prev_bounce, ("vc", prev_pair_done[1])],
                              inc="gld", inc_amt=16)
                    lds.append(ld)
                for nb4 in range(4):
                    pi = pcount % 2
                    pcount += 1
                    ps = pp[pi]
                    tv = None
                    for ti in range(2):
                        for k in range(NCH):
                            tv = p.do("t", lambda t, k=k, ti=ti, dc=dc, nb4=nb4, ps=ps: t.matmul(
                                ps[:, :],
                                self.DT[:, (ti * 2 + dc % 2) * 1024 + k * 128:
                                        (ti * 2 + dc % 2) * 1024 + (k + 1) * 128],
                                self.A[:, ti * 16384 + k * 2048 + nb4 * 512:
                                       ti * 16384 + k * 2048 + nb4 * 512 + 512],
                                start=(ti == 0 and k == 0), stop=(ti == 1 and k == NCH - 1)),
                                wait=[("gld", lds[1]), gate[pi],
                                      ("vc", prev_pair_done[1])] if ti == 0 and k == 0 else (),
                                inc="tc" if (ti == 1 and k == NCH - 1) else None)
                    xs = self.xfm(b, dc)[:, nb4 * 512:(nb4 + 1) * 512]
                    ev = p.do("v", lambda v, xs=xs, ps=ps: v.scalar_tensor_tensor(
                        xs, ps[:, :], 0.03125, xs, ALU.mult, ALU.add),
                        wait=(("tc", tv),), inc="vc")
                    gate[pi] = ("vc", ev)
                slot_gate[2 * 0 + dc % 2] = ("tc", p.now("tc"))
                slot_gate[2 * 1 + dc % 2] = ("tc", p.now("tc"))
            self.xfm_ready = ("vc", p.now("vc"))

    def emit_l1attn(self):
        p = self.p
        WQK = self.DT[:, 0:1024]
        att = [self.DT[:, 1024:3072], self.DT[:, 3072:5120]]
        nrow = self.SL[0:1, 0:2048]
        stg1 = self.DT[0:16, 5120:5248]
        stg2 = self.DT[0:16, 5248:5376]
        nmx_bf = self.DT[:, 5376:5392]
        rr_bf = self.DT[:, 5392:5408]
        rrb = self.PMRN
        zrow = self.DT[0:1, 5408:5536]
        SKIP1 = os.environ.get("L1SKIP", "")
        for b in range(NB if "bb0" not in SKIP1 else 1):
            zv = p.do("g", lambda g: g.memset(zrow, 0.0),
                      wait=(("sc", p.now("sc")), ("tc", p.now("tc")),
                            ("vc", p.now("vc"))), inc="gc")
            lbv = p.do("g", lambda g: g.dma_start(self.PMRN[0:1, 0:1024],
                                                  self.bvr[:, :]),
                       inc="gld", inc_amt=16)
            lnv = self.emit_ln(b, lambda c: self.Bar[:, c * 2048:(c + 1) * 2048])
            lv = p.do("g", lambda g: g.dma_start(
                self.WB[:, :].rearrange("q (k c) -> q k c", c=1024),
                self.WvT[:, :].rearrange("(k q) c -> q k c", q=128)),
                wait=(("tc", p.now("tc")), ("sc", p.now("sc"))), inc="gld", inc_amt=16)
            pp = [self.PA, self.PB]
            gate = {0: ("vc", 0), 1: ("vc", 0)}
            pcount = 0
            for nt in range(16):
                for oh in range(2):
                    pi = pcount % 2
                    pcount += 1
                    ps = pp[pi]
                    for k in range(NCH):
                        p.do("t", lambda t, k=k, nt=nt, oh=oh, ps=ps: t.matmul(
                            ps[:, :],
                            self.Bar[:, k * 2048 + nt * 128:k * 2048 + (nt + 1) * 128],
                            self.WB[:, k * 1024 + oh * 512:k * 1024 + oh * 512 + 512],
                            start=(k == 0), stop=False),
                            wait=[("vc", lnv), ("gld", lv), gate[pi]] if k == 0 else ())
                    tv = p.do("t", lambda t, oh=oh, ps=ps: t.matmul(
                        ps[:, :], self.ones[0:1, 0:128],
                        self.PMRN[0:1, oh * 512:oh * 512 + 512],
                        start=False, stop=True), wait=(("gld", lbv),), inc="tc")
                    dst = self.A[:, 16384 + nt * 1024 + oh * 512:16384 + nt * 1024 + oh * 512 + 512]
                    ev = p.do("v", lambda v, dst=dst, ps=ps: v.tensor_copy(dst, ps[:, :]),
                              wait=(("tc", tv),), inc="vc")
                    gate[pi] = ("vc", ev)
            vh_done = p.now("vc")
            wq_gate = ("vc", 0)
            for h in range(16):
                base = 64 * (h % 2)
                ld = p.do("g", lambda g, h=h, base=base: g.dma_start(
                    self.SL[base:base + 64, 0:64], self.trilT[h, :, :]),
                    wait=[wq_gate, ("tc", p.now("tc"))], inc="gld", inc_amt=16)
                tv = p.do("t", lambda t, base=base: t.matmul(
                    self.PC[base:base + 64, 0:64], self.SL[base:base + 64, 0:64],
                    self.SL[base:base + 64, 0:64], start=True, stop=True,
                    tile_position=(base, base)),
                    wait=(("gld", ld), ("vc", p.now("vc"))), inc="tc")
                ev = p.do("v", lambda v, h=h, base=base: v.tensor_copy(
                    WQK[base:base + 64, h * 64:(h + 1) * 64],
                    self.PC[base:base + 64, 0:64]),
                    wait=(("tc", tv),), inc="vc")
                wq_gate = ("vc", ev)
            gate = {0: ("sc", 0), 1: ("sc", 0)}
            pcount = 0
            for pr in range(8):
                for nb4 in range(4):
                    pi = pcount % 2
                    pcount += 1
                    ps0, ps1 = (self.PA, self.PB) if pi == 0 else (self.PC, self.PD)
                    tv = None
                    for par in range(2):
                        h = 2 * pr + par
                        base = 64 * par
                        psx = ps0 if par == 0 else ps1
                        tv = p.do("t", lambda t, h=h, base=base, pr=pr, nb4=nb4, psx=psx: t.matmul(
                            psx[base:base + 64, :], WQK[base:base + 64, h * 64:(h + 1) * 64],
                            self.Bar[base:base + 64, pr * 2048 + nb4 * 512:pr * 2048 + nb4 * 512 + 512],
                            start=True, stop=True, tile_position=(base, base)),
                            wait=[("vc", wq_gate[1]), gate[pi]] if par == 0 else (),
                            inc="tc" if par == 1 else None)
                    for par in range(2):
                        base = 64 * par
                        psx = ps0 if par == 0 else ps1
                        dst = self.A[base:base + 64,
                                     pr * 2048 + nb4 * 512:pr * 2048 + nb4 * 512 + 512]
                        ev = p.do("s", lambda s, dst=dst, psx=psx, base=base: s.mul(
                            dst, psx[base:base + 64, :], 0.125),
                            wait=(("tc", tv),) if par == 0 else (), inc="sc")
                    gate[pi] = ("sc", ev)
            lo = p.do("g", lambda g: g.dma_start(
                self.WB[:, :].rearrange("q (k c) -> q k c", c=1024),
                self.WoT[:, :].rearrange("(k q) c -> q k c", q=128)),
                wait=(("vc", vh_done), ("tc", p.now("tc"))), inc="gld", inc_amt=16)
            pvbank = [self.PA, self.PB, self.PC, self.PD]
            prev_evac = ("vc", 0)
            plbf = self.PL[:, :].bitcast(BF16)
            for h in range(0 if "heads" not in SKIP1 else 16, 1 if "h1" in SKIP1 else 16):
                pr, par = h // 2, h % 2
                base = 64 * par
                sgp = 0
                sgd = {0: ("sc", 0), 1: ("sc", 0)}
                for i in range(16):
                    PX = [self.PL, self.PM][i % 2] if par == 0 else self.PL
                    pxk = (i % 2) if par == 0 else 0
                    tv = None
                    for tb in range(4):
                        tv = p.do("t", lambda t, i=i, tb=tb, pr=pr, base=base, PX=PX: t.matmul(
                            PX[:, tb * 512:(tb + 1) * 512],
                            self.A[base:base + 64, pr * 2048 + i * 128:pr * 2048 + (i + 1) * 128],
                            self.Bar[base:base + 64, pr * 2048 + tb * 512:pr * 2048 + tb * 512 + 512],
                            start=True, stop=True),
                            wait=[("sc", p.now("sc")), ("vc", p.now("vc"))] if (i <= 1 and tb == 0)
                                 else ([sgd[pxk]] if tb == 0 else ()),
                            inc="tc" if tb == 3 else None)
                    rv = p.do("v", lambda v, PX=PX: v.reduce_max(
                        self.ST[:, ST_MS1:ST_MS1 + 1], PX[:, :], axis=AX.X,
                        negate=True), wait=(("tc", tv),), inc="vc")
                    rv = p.do("v", lambda v, i=i: v.tensor_copy(
                        nmx_bf[:, i:i + 1], self.ST[:, ST_MS1:ST_MS1 + 1]),
                        wait=(("vc", rv),), inc="vc")
                    rv = p.do("v", lambda v, i=i: v.tensor_copy(
                        self.ST[:, ST_NMX + i:ST_NMX + i + 1], nmx_bf[:, i:i + 1]),
                        wait=(("vc", rv),), inc="vc")
                    sgp = p.do("s", lambda s, i=i, PX=PX: s.activation(
                        att[0], PX[:, :], AF.Exp,
                        bias=self.ST[:, ST_NMX + i:ST_NMX + i + 1],
                        accum_out=self.ST[:, ST_RS + i:ST_RS + i + 1]),
                        wait=(("vc", rv),), inc="sc")
                    sgd[pxk] = ("sc", sgp)
                tv = p.do("t", lambda t: t.transpose(
                    plbf[0:16, 0:128], nmx_bf[:, 0:16], self.ident[:, :]),
                    wait=(("sc", sgp),), inc="tc")
                fv = p.do("v", lambda v: v.tensor_copy(stg1, plbf[0:16, 0:128]),
                          wait=(("tc", tv),), inc="vc")
                yv = p.do("y", lambda y: y.dma_start(
                    self.bncb[0:1, :].rearrange("x (a c) -> (x a) c", c=128), stg1),
                    wait=(("vc", fv), ("yld", p.now("yld"))), inc="yld", inc_amt=16)
                yv = p.do("y", lambda y: y.dma_start(
                    nrow, self.bncb[0:1, :]), wait=(("yld", yv),), inc="yld", inc_amt=16)
                rv = p.do("v", lambda v: v.reciprocal(
                    self.ST[:, ST_RR:ST_RR + 16], self.ST[:, ST_RS:ST_RS + 16]),
                    wait=(("sc", sgp),), inc="vc")
                rv = p.do("v", lambda v: v.tensor_copy(
                    rr_bf[:, 0:16], self.ST[:, ST_RR:ST_RR + 16]),
                    wait=(("vc", rv),), inc="vc")
                tv = p.do("t", lambda t: t.transpose(
                    plbf[0:16, 128:256], rr_bf[:, 0:16], self.ident[:, :]),
                    wait=(("vc", rv),), inc="tc")
                fv = p.do("v", lambda v: v.tensor_copy(stg2, plbf[0:16, 128:256]),
                          wait=(("tc", tv),), inc="vc")
                yv2 = p.do("y", lambda y: y.dma_start(
                    self.bncb[0:1, :].rearrange("x (a c) -> (x a) c", c=128), stg2),
                    wait=(("vc", fv), ("yld", yv)), inc="yld", inc_amt=16)
                yv2 = p.do("g", lambda g, base=base: g.dma_start(
                    rrb[base:base + 64, :],
                    bass.AP(self.bncb.tensor, 0, [[0, 64], [1, 2048]])),
                    wait=(("yld", yv2),
                          ("vc", prev_evac[1] if h else p.now("vc"))),
                    inc="gld", inc_amt=16)
                rrb_done = yv2
                if par == 0:
                    for sb in range(4):
                        p.do("t", lambda t, sb=sb: t.matmul(
                            pvbank[sb][:, :], zrow, self.ones[0:1, 0:512],
                            start=True, stop=False),
                            wait=[prev_evac, ("gc", zv)] if sb == 0 else ())
                sgd2 = {0: ("sc", 0), 1: ("sc", 0)}
                pvlast = None
                for j in range(16):
                    tvh = {}
                    for sh in range(2):
                        for q in range(2):
                            sb = sh * 2 + q
                            p.do("t", lambda t, j=j, sb=sb, pr=pr, base=base: t.matmul(
                                self.PL[:, sb * 512:(sb + 1) * 512],
                                self.Bar[base:base + 64, pr * 2048 + j * 128:pr * 2048 + (j + 1) * 128],
                                self.A[base:base + 64, pr * 2048 + sb * 512:pr * 2048 + sb * 512 + 512],
                                start=True, stop=False),
                                wait=[("yld", p.now("yld")), ("sc", p.now("sc"))]
                                     if (sh == 0 and q == 0 and j == 0) else
                                     ([sgd2[sh]] if q == 0 else ()))
                            tvh[sh] = p.do("t", lambda t, sb=sb: t.matmul(
                                self.PL[:, sb * 512:(sb + 1) * 512], self.ones[0:1, 0:128],
                                nrow[0:1, sb * 512:sb * 512 + 512],
                                start=False, stop=True), inc="tc" if q == 1 else None)
                    sgx = {}
                    for sh in range(2):
                        sgx[sh] = p.do("s", lambda s, j=j, sh=sh: s.activation(
                            att[j % 2][:, sh * 1024:(sh + 1) * 1024],
                            self.PL[:, sh * 1024:(sh + 1) * 1024], AF.Exp),
                            wait=(("tc", tvh[sh]),), inc="sc")
                        sgd2[sh] = ("sc", sgx[sh])
                    for sh in range(2):
                        for q in range(2):
                            sb = sh * 2 + q
                            pvlast = p.do("t", lambda t, j=j, sb=sb, par=par, h=h: t.matmul(
                                pvbank[sb][64 * par:64 * par + 64, :],
                                self.A[:, 16384 + j * 1024 + h * 64:16384 + j * 1024 + (h + 1) * 64],
                                att[j % 2][:, sb * 512:(sb + 1) * 512],
                                start=False, stop=False, tile_position=(0, 64 * par)),
                                wait=[("sc", sgx[sh])] + (
                                    [("vc", p.now("vc"))] if (q == 0 and j == 0 and
                                                              sh == 0 and par == 1) else [])
                                    if q == 0 else ())
                if par == 1:
                    pvlast = None
                    for sb in range(4):
                        pvlast = p.do("t", lambda t, sb=sb: t.matmul(
                            pvbank[sb][:, :], zrow, self.ones[0:1, 0:512],
                            start=False, stop=True),
                            inc="tc" if sb == 3 else None)
                    evs = None
                    for sb in range(4):
                        dst = self.A[:, pr * 2048 + sb * 512:pr * 2048 + sb * 512 + 512]
                        evs = p.do("v", lambda v, dst=dst, sb=sb: v.tensor_tensor(
                            dst, pvbank[sb][:, :], rrb[:, sb * 512:(sb + 1) * 512],
                            ALU.mult),
                            wait=[("tc", pvlast), ("gld", rrb_done)] if sb == 0 else (),
                            inc="vc")
                    prev_evac = ("vc", evs)
            lbo = p.do("g", lambda g: g.dma_start(self.PMRN[0:1, 0:1024],
                                                  self.bor[:, :]),
                       wait=(("vc", p.now("vc")),), inc="gld", inc_amt=16)
            gate = {0: ("vc", 0), 1: ("vc", 0)}
            pcount = 0
            for dc in range(8):
                for nb4 in range(4):
                    pi = pcount % 2
                    pcount += 1
                    ps = pp[pi]
                    for k in range(NCH):
                        p.do("t", lambda t, k=k, dc=dc, nb4=nb4, ps=ps: t.matmul(
                            ps[:, :], self.WB[:, k * 1024 + dc * 128:k * 1024 + (dc + 1) * 128],
                            self.A[:, k * 2048 + nb4 * 512:k * 2048 + nb4 * 512 + 512],
                            start=(k == 0), stop=False),
                            wait=[("gld", lo), ("vc", p.now("vc")), gate[pi]] if k == 0 else ())
                    tv = p.do("t", lambda t, dc=dc, ps=ps: t.matmul(
                        ps[:, :], self.PMRN[0:1, dc * 128:(dc + 1) * 128],
                        self.ones[0:1, 0:512], start=False, stop=True),
                        wait=(("gld", lbo),), inc="tc")
                    xs = self.xfm(b, dc)[:, nb4 * 512:(nb4 + 1) * 512]
                    ev = p.do("v", lambda v, xs=xs, ps=ps: v.tensor_add(xs, ps[:, :], xs),
                              wait=(("tc", tv),), inc="vc")
                    gate[pi] = ("vc", ev)
            self.xfm_ready = ("vc", p.now("vc"))

    def emit_l0ffn(self):
        p = self.p
        for b in range(NB):
            lnv = self.emit_ln(b, lambda c: self.Bar[:, c * 2048:(c + 1) * 2048], rms=True)
            p.do("g", lambda g: g.memset(self.PMRN[:, :], -1e30),
                 wait=(("vc", p.now("vc")),), inc="gc")
            pmg = p.now("gc")
            se_b = self.SL[:, 0:1024]
            sn_b = self.SL[:, 1024:2048]
            snf = self.DT[:, 2048:3072]
            pbf = self.PC[:, :].bitcast(BF16)
            ev_gate = {0: ("vc", 0), 1: ("vc", 0)}
            st_gate = {0: ("sc", 0), 1: ("sc", 0)}
            sp_gate = ("yld", 0)
            se_free = ("vc", 0)
            for mt in range(32):
                PP = [self.PL, self.PM][mt % 2]
                ppbf = PP[:, 1024:1536].bitcast(BF16)
                ld = p.do("g", lambda g, mt=mt: g.dma_start(
                    se_b, self.se[mt * 128:(mt + 1) * 128, :]),
                    wait=(se_free,), inc="gld", inc_amt=16)
                sv = p.do("v", lambda v: v.scalar_tensor_tensor(
                    sn_b, se_b, 1.0, se_b, ALU.mult, ALU.mult,
                    accum_out=self.ST[:, ST_MS1:ST_MS1 + 1]),
                    wait=(("gld", ld), ("yld", self.init_yld)), inc="vc")
                sq = p.do("s", lambda s: s.activation(
                    self.ST[:, ST_MS1 + 1:ST_MS1 + 2], self.ST[:, ST_MS1:ST_MS1 + 1],
                    AF.Sqrt), wait=(("vc", sv),), inc="sc")
                sv = p.do("v", lambda v: v.tensor_scalar_max(
                    self.ST[:, ST_MS1 + 1:ST_MS1 + 2], self.ST[:, ST_MS1 + 1:ST_MS1 + 2],
                    1e-12), wait=(("sc", sq),), inc="vc")
                sv = p.do("v", lambda v: v.reciprocal(
                    self.ST[:, ST_MS1 + 2:ST_MS1 + 3], self.ST[:, ST_MS1 + 1:ST_MS1 + 2]),
                    wait=(("vc", sv),), inc="vc")
                sv = p.do("v", lambda v: v.tensor_scalar(
                    sn_b, se_b, self.ST[:, ST_MS1 + 2:ST_MS1 + 3], 32.0,
                    ALU.mult, ALU.mult), wait=(("vc", sv),), inc="vc")
                se_free = ("vc", sv)
                tv = None
                for c in range(NCH):
                    tv = p.do("t", lambda t, c=c, ppbf=ppbf: t.transpose(
                        ppbf[:, c * 128:(c + 1) * 128], sn_b[:, c * 128:(c + 1) * 128],
                        self.ident[:, :]),
                        wait=[("vc", p.now("vc")), ("gc", self.init_gc),
                              st_gate[mt % 2], ev_gate[mt % 2]] if c == 0 else (),
                        inc="tc" if c == NCH - 1 else None)
                snv = None
                for c in range(NCH):
                    snv = p.do("v", lambda v, c=c, ppbf=ppbf: v.tensor_scalar_mul(
                        snf[:, c * 128:(c + 1) * 128], ppbf[:, c * 128:(c + 1) * 128],
                        self.ST[:, ST_GS + c:ST_GS + c + 1]),
                        wait=[("tc", tv)] if c == 0 else (), inc="vc")
                tv = None
                for nb4 in range(4):
                    for k in range(NCH):
                        tv = p.do("t", lambda t, k=k, nb4=nb4, PP=PP: t.matmul(
                            PP[:, nb4 * 512:(nb4 + 1) * 512],
                            snf[:, k * 128:(k + 1) * 128],
                            self.Bar[:, k * 2048 + nb4 * 512:k * 2048 + nb4 * 512 + 512],
                            start=(k == 0), stop=(k == NCH - 1)),
                            wait=[("vc", max(snv, lnv))] if k == 0 and nb4 == 0 else (),
                            inc="tc" if (k == NCH - 1 and nb4 == 3) else None)
                rv = p.do("v", lambda v, PP=PP: v.reduce_max(
                    self.ST[:, ST_MS1 + 3:ST_MS1 + 4], PP[:, :], axis=AX.X,
                    negate=True), wait=(("tc", tv), ("gc", pmg)), inc="vc")
                p.do("v", lambda v, mt=mt: v.tensor_scalar_mul(
                    self.ST[:, ST_RM + mt:ST_RM + mt + 1],
                    self.ST[:, ST_MS1 + 3:ST_MS1 + 4], -1.0),
                    wait=(("vc", rv),), inc="vc")
                pmv = p.do("v", lambda v, PP=PP: v.tensor_max(
                    self.PMRN[:, :], self.PMRN[:, :], PP[:, :]), inc="vc")
                ev_gate[mt % 2] = ("vc", pmv)
                sx = p.do("s", lambda s, mt=mt, PP=PP: s.activation(
                    self.DT[:, 0:2048], PP[:, :], AF.Exp,
                    bias=self.ST[:, ST_MS1 + 3:ST_MS1 + 4],
                    accum_out=self.ST[:, ST_CS + mt:ST_CS + mt + 1]),
                    wait=(("vc", rv),), inc="sc")
                sx = p.do("s", lambda s, PP=PP: s.activation(self.FSA[:, :], PP[:, :],
                                                             AF.Copy),
                          wait=(sp_gate,), inc="sc")
                st_gate[mt % 2] = ("sc", sx)
                yv = p.do("y", lambda y, b=b, mt=mt: y.dma_start(
                    self.Lsp[b, mt, :, :], self.FSA[:, :]),
                    wait=(("sc", sx),), inc="yld", inc_amt=16)
                sp_gate = ("yld", yv)
            st_gate = st_gate[31 % 2]
            sv = p.do("s", lambda s: s.activation(
                self.ST[:, ST_NAM:ST_NAM + 32], self.ST[:, ST_CS:ST_CS + 32], AF.Ln),
                wait=(("sc", p.now("sc")),), inc="sc")
            vv = p.do("v", lambda v: v.tensor_add(
                self.ST[:, ST_NAM:ST_NAM + 32], self.ST[:, ST_NAM:ST_NAM + 32],
                self.ST[:, ST_RM:ST_RM + 32]), wait=(("sc", sv),), inc="vc")
            vv = p.do("v", lambda v: v.tensor_scalar_mul(
                self.ST[:, ST_NAM:ST_NAM + 32], self.ST[:, ST_NAM:ST_NAM + 32], -1.0),
                wait=(("vc", vv),), inc="vc")
            nam_done = vv
            rnv = None
            for grp in range(4):
                tv = None
                for j in range(4):
                    k = grp * 4 + j
                    tv = p.do("t", lambda t, k=k, j=j: t.transpose(
                        pbf[:, j * 128:(j + 1) * 128],
                        self.PMRN[:, k * 128:(k + 1) * 128], self.ident[:, :]),
                        wait=[("vc", p.now("vc"))] if j == 0 else (),
                        inc="tc" if j == 3 else None)
                rnv = p.do("v", lambda v, grp=grp: v.reduce_max(
                    self.ST[:, ST_RN + grp * 4:ST_RN + (grp + 1) * 4],
                    pbf[:, 0:512].rearrange("p (a c) -> p a c", a=4), axis=AX.X),
                    wait=(("tc", tv),), inc="vc")
            cv = p.do("v", lambda v: v.tensor_copy(
                self.SL[:, 0:16], self.ST[:, ST_RN:ST_RN + 16]),
                wait=(("vc", rnv),), inc="vc")
            tv = p.do("t", lambda t: t.transpose(
                self.PL[:, :].bitcast(BF16)[0:16, 0:128], self.SL[:, 0:16],
                self.ident[:, :]), wait=(("vc", cv), st_gate), inc="tc")
            cv = p.do("v", lambda v: v.tensor_copy(
                self.SL[0:16, 128:256], self.PL[:, :].bitcast(BF16)[0:16, 0:128]),
                wait=(("tc", tv),), inc="vc")
            yv = p.do("y", lambda y: y.dma_start(
                self.bncb[0:1, :].rearrange("x (a c) -> (x a) c", c=128),
                self.SL[0:16, 128:256]), wait=(("vc", cv), sp_gate), inc="yld", inc_amt=16)
            bv = p.do("g", lambda g: g.dma_start(
                self.PMRN[:, :], bass.AP(self.bncb.tensor, 0, [[0, 128], [1, 2048]])),
                wait=(("yld", yv), ("vc", p.now("vc"))), inc="gld", inc_amt=16)
            rn_b_done = bv
            for k in range(16):
                tv = None
                for c in range(NCH):
                    tv = p.do("t", lambda t, k=k, c=c: t.transpose(
                        pbf[:, c * 128:(c + 1) * 128],
                        self.Bar[:, c * 2048 + k * 128:c * 2048 + (k + 1) * 128],
                        self.ident[:, :]),
                        wait=[("vc", p.now("vc")), ("tc", p.now("tc"))] if c == 0 else (),
                        inc="tc" if c == NCH - 1 else None)
                p.do("v", lambda v, k=k: v.tensor_copy(
                    self.A[:, k * 1024:(k + 1) * 1024], pbf[:, 0:1024]),
                    wait=(("tc", tv),), inc="vc")
            xntm_done = p.now("vc")
            dsl = [self.DT[:, 0:2048], self.DT[:, 2048:4096]]
            wslot = [self.WB[:, 0:4096], self.WB[:, 4096:8192]]
            dtv_gate = ("tc", 0)
            wgate = [("tc", 0), ("tc", 0)]
            eow = self.Bar[:, 12288:13312]
            eow_gate = ("yld", 0)
            b1gate = {0: ("sc", p.now("sc")), 1: ("sc", p.now("sc"))}
            b2gate = {0: ("vc", p.now("vc")), 1: ("vc", p.now("vc"))}
            for e in range(NE):
                b1s = ST_B1 if e % 2 == 0 else 128
                b2s = 0 if e % 2 == 0 else 1024
                lb = p.do("g", lambda g, e=e, b1s=b1s: g.dma_start(
                    self.ST[:, b1s:b1s + 32], self.b1c[e, :, :]),
                    wait=(b1gate[e % 2],), inc="gld", inc_amt=16)
                lb2 = p.do("g", lambda g, e=e, b2s=b2s: g.dma_start(
                    self.SL[0:1, b2s:b2s + 1024], self.b2r[e, :, :]),
                    wait=(b2gate[e % 2],), inc="gld", inc_amt=16)
                for ms in range(4):
                    mt = 4 * e + ms
                    lv = p.do("y", lambda y, b=b, mt=mt: y.dma_start(
                        self.FSA[:, :], self.Lsp[b, mt, :, :]),
                        wait=(st_gate, ("sc", p.now("sc"))), inc="yld", inc_amt=16)
                    dx = p.do("s", lambda s, ms=ms, mt=mt: s.activation(
                        dsl[ms % 2], self.FSA[:, :], AF.Exp,
                        bias=self.ST[:, ST_NAM + mt:ST_NAM + mt + 1]),
                        wait=(("yld", lv), ("vc", nam_done), dtv_gate), inc="sc")
                    st_gate = ("sc", dx)
                    for grp in range(2):
                        tv = None
                        for j in range(8):
                            k = grp * 8 + j
                            tv = p.do("t", lambda t, ms=ms, k=k, j=j: t.transpose(
                                pbf[:, j * 128:(j + 1) * 128],
                                dsl[ms % 2][:, k * 128:(k + 1) * 128], self.ident[:, :]),
                                wait=[("sc", dx), ("vc", p.now("vc"))] if j == 0 else (),
                                inc="tc" if j == 7 else None)
                        dtv_gate = ("tc", tv)
                        src3 = pbf[:, 0:1024].rearrange("p (k c) -> p k c", c=128)
                        dd = self.Bar[:, :].rearrange(
                            "p (k c) -> p k c", c=512)[:, grp * 8:(grp + 1) * 8,
                                                       ms * 128:(ms + 1) * 128]
                        p.do("v", lambda v, dd=dd, src3=src3: v.tensor_copy(dd, src3),
                             wait=(("tc", tv),), inc="vc")
                dtr_done = p.now("vc")
                sgate = {0: ("vc", 0), 1: ("vc", 0)}
                for ds in range(NCH):
                    ps = [self.PA, self.PB][ds % 2]
                    tv = None
                    for k in range(16):
                        tv = p.do("t", lambda t, ds=ds, k=k, ps=ps: t.matmul(
                            ps[:, :], self.A[:, k * 1024 + ds * 128:k * 1024 + (ds + 1) * 128],
                            self.Bar[:, :].rearrange("p (k c) -> p k c", c=512)[:, k, :],
                            start=(k == 0), stop=(k == 15)),
                            wait=[("vc", dtr_done), sgate[ds % 2]] if k == 0 else (),
                            inc="tc" if k == 15 else None)
                    sv = p.do("v", lambda v, ds=ds, ps=ps: v.tensor_copy(
                        self.Bar[:, 8192 + ds * 512:8192 + (ds + 1) * 512], ps[:, :]),
                        wait=(("tc", tv),), inc="vc")
                    sgate[ds % 2] = ("vc", sv)
                slots_done = p.now("vc")
                hgate = {0: ("sc", 0), 1: ("sc", 0)}
                for q8 in range(8):
                    wl = p.do("g", lambda g, e=e, q8=q8: g.dma_start(
                        wslot[q8 % 2].rearrange("q (k c) -> q k c", c=512),
                        self.w1[e, :, q8 * 512:(q8 + 1) * 512].rearrange(
                            "(k q) c -> q k c", q=128)),
                        wait=(wgate[q8 % 2],), inc="gld", inc_amt=16)
                    for ht in range(4):
                        ps = [self.PA, self.PB][ht % 2]
                        tv = None
                        for k in range(NCH):
                            tv = p.do("t", lambda t, q8=q8, ht=ht, k=k, ps=ps: t.matmul(
                                ps[:, :],
                                wslot[q8 % 2][:, k * 512 + ht * 128:k * 512 + (ht + 1) * 128],
                                self.Bar[:, 8192 + k * 512:8192 + (k + 1) * 512],
                                start=(k == 0), stop=(k == NCH - 1)),
                                wait=[("gld", wl), ("vc", slots_done),
                                      hgate[ht % 2]] if k == 0 else (),
                                inc="tc" if k == NCH - 1 else None)
                        hv = p.do("s", lambda s, q8=q8, ht=ht, ps=ps, b1s=b1s: s.activation(
                            self.A[:, 16384 + (q8 * 4 + ht) * 512:
                                   16384 + (q8 * 4 + ht + 1) * 512],
                            ps[:, :], AF.Gelu,
                            bias=self.ST[:, b1s + q8 * 4 + ht:b1s + q8 * 4 + ht + 1]),
                            wait=(("tc", tv), ("gld", lb)), inc="sc")
                        hgate[ht % 2] = ("sc", hv)
                    wgate[q8 % 2] = ("tc", p.now("tc"))
                h_done = p.now("sc")
                eops = [self.PL[:, 0:512], self.PL[:, 512:1024],
                        self.PL[:, 1024:1536], self.PL[:, 1536:2048],
                        self.PA[:, :], self.PB[:, :], self.PC[:, :], self.PD[:, :]]
                for hq in range(8):
                    wl = p.do("g", lambda g, e=e, hq=hq: g.dma_start(
                        wslot[hq % 2].rearrange("q (k c) -> q k c", c=1024),
                        self.w2[e, hq * 512:(hq + 1) * 512, :].rearrange(
                            "(k q) c -> q k c", q=128)),
                        wait=(wgate[hq % 2],), inc="gld", inc_amt=16)
                    for hc in range(4):
                        kk = hq * 4 + hc
                        for gi in range(8):
                            msub, dh = gi // 2, gi % 2
                            last = (hc == 3 and gi == 7)
                            w_ = ()
                            if hc == 0 and gi == 0:
                                w_ = [("gld", wl)] + (
                                    [("sc", h_done), ("vc", p.now("vc"))] if hq == 0 else [])
                            tq = p.do("t", lambda t, kk=kk, hc=hc, hq=hq, msub=msub, dh=dh, gi=gi: t.matmul(
                                eops[gi],
                                self.A[:, 16384 + kk * 512 + msub * 128:
                                       16384 + kk * 512 + (msub + 1) * 128],
                                wslot[hq % 2][:, hc * 1024 + dh * 512:hc * 1024 + (dh + 1) * 512],
                                start=(kk == 0), stop=False),
                                wait=w_, inc="tc" if last else None)
                    wgate[hq % 2] = ("tc", tq)
                tvb = None
                for gi in range(8):
                    msub, dh = gi // 2, gi % 2
                    tvb = p.do("t", lambda t, msub=msub, dh=dh, gi=gi, b2s=b2s: t.matmul(
                        eops[gi], self.ones[0:1, 0:128],
                        self.SL[0:1, b2s + dh * 512:b2s + (dh + 1) * 512],
                        start=False, stop=True),
                        wait=(("gld", lb2),) if gi == 0 else (),
                        inc="tc" if gi == 7 else None)

                b1gate[e % 2] = ("sc", p.now("sc"))
                b2gate[e % 2] = ("tc", tvb)
                for msub in range(4):
                    mt = 4 * e + msub
                    ev = None
                    for dh in range(2):
                        ev = p.do("v", lambda v, msub=msub, dh=dh: v.tensor_copy(
                            eow[:, dh * 512:(dh + 1) * 512], eops[msub * 2 + dh]),
                            wait=[("tc", tvb), eow_gate] if dh == 0 else (),
                            inc="vc")
                    yv = p.do("y", lambda y, b=b, mt=mt: y.dma_start(
                        self.eod[b, mt * 128:(mt + 1) * 128, :], eow[:, :]),
                        wait=(("vc", ev),), inc="yld", inc_amt=16)
                    eow_gate = ("yld", yv)
            craw = self.Bar
            rrb = self.SL[:, 0:512]
            el = None
            for mt in range(32):
                el = p.do("g", lambda g, b=b, mt=mt: g.dma_start(
                    self.A[:, mt * 1024:(mt + 1) * 1024],
                    self.eod[b, mt * 128:(mt + 1) * 128, :]),
                    wait=[("tc", p.now("tc")), ("sc", p.now("sc")), eow_gate,
                          ("vc", p.now("vc"))] if mt == 0 else (),
                    inc="gld", inc_amt=16)
            eo_done = el
            lgate = ("vc", 0)
            hgate = {0: ("sc", 0), 1: ("sc", 0)}
            for nb4 in range(4):
                lgate = ("vc", p.now("vc")) if nb4 else lgate
                tgate = ("tc", p.now("tc")) if nb4 else ("tc", 0)
                for g8 in range(16):
                    mt0 = g8 * 2
                    half = g8 % 2
                    fsl = self.FSA[:, half * 1024:half * 1024 + 1024]
                    dst3 = fsl.rearrange("p (a c) -> p a c", c=512)
                    src3 = self.Lsp[b, mt0:mt0 + 2, :, nb4 * 512:(nb4 + 1) * 512] \
                        .rearrange("a p c -> p a c")
                    lv = p.do("y", lambda y, dst3=dst3, src3=src3: y.dma_start(dst3, src3),
                        wait=(lgate, hgate[half], eow_gate) if g8 <= 1
                             else (hgate[half],),
                        inc="yld", inc_amt=16)
                    sv = None
                    for a2 in range(2):
                        sv = p.do("v", lambda v, fsl=fsl, a2=a2, nb4=nb4: v.tensor_sub(
                            fsl[:, a2 * 512:(a2 + 1) * 512], fsl[:, a2 * 512:(a2 + 1) * 512],
                            self.PMRN[:, nb4 * 512:(nb4 + 1) * 512]),
                            wait=[("yld", lv), ("gld", rn_b_done)] if a2 == 0 else (),
                            inc="vc")
                    cx = p.do("s", lambda s, mt0=mt0, fsl=fsl: s.activation(
                        craw[:, mt0 * 512:(mt0 + 2) * 512], fsl, AF.Exp),
                        wait=(("vc", sv), tgate) if g8 == 0 else (("vc", sv),),
                        inc="sc")
                    hgate[half] = ("sc", cx)
                    for a2 in range(2):
                        mt = mt0 + a2
                        tv = p.do("t", lambda t, mt=mt: t.matmul(
                            self.PA[0:1, :], self.ones[:, 0:1],
                            craw[:, mt * 512:(mt + 1) * 512],
                            start=(mt == 0), stop=(mt == 31)),
                            wait=[("sc", cx)] + (
                                [("vc", p.now("vc"))] if mt == 0 else [])
                                if a2 == 0 else [("sc", cx)],
                            inc="tc" if mt == 31 else None)
                cr_done = p.now("sc")
                rv = p.do("v", lambda v: v.reciprocal(self.FSA[0:1, 1024:1536],
                                                      self.PA[0:1, :]),
                          wait=(("tc", tv),), inc="vc")
                rv = p.do("v", lambda v: v.tensor_copy(self.SL[0:1, 512:1024],
                                                       self.FSA[0:1, 1024:1536]),
                          wait=(("vc", rv),), inc="vc")
                tv = p.do("t", lambda t: t.matmul(
                    self.PB[:, :], self.ones[0:1, 0:128], self.SL[0:1, 512:1024],
                    start=True, stop=True), wait=(("vc", rv),), inc="tc")
                rv = p.do("v", lambda v: v.tensor_copy(rrb, self.PB[:, :]),
                          wait=(("tc", tv),), inc="vc")
                rrb_done = rv
                dps = [self.PL[:, 0:512], self.PL[:, 512:1024],
                       self.PL[:, 1024:1536], self.PL[:, 1536:2048],
                       self.PC, self.PD]
                dgate = {i: ("vc", 0) for i in range(6)}
                for dc in range(NCH):
                    pi = dc % 6
                    ps = dps[pi]
                    tv = None
                    for mt in range(32):
                        tv = p.do("t", lambda t, dc=dc, mt=mt, ps=ps: t.matmul(
                            ps[:, :], self.A[:, mt * 1024 + dc * 128:mt * 1024 + (dc + 1) * 128],
                            craw[:, mt * 512:(mt + 1) * 512],
                            start=(mt == 0), stop=(mt == 31)),
                            wait=[("gld", eo_done), ("sc", cr_done),
                                  dgate[pi], ("vc", rrb_done)] if mt == 0 else (),
                            inc="tc" if mt == 31 else None)
                    fs2 = self.FSA[:, (dc % 2) * 512:(dc % 2) * 512 + 512]
                    ev = p.do("v", lambda v, fs2=fs2, ps=ps: v.tensor_mul(fs2, ps[:, :], rrb),
                              wait=(("tc", tv), ("vc", rrb_done)), inc="vc")
                    xs = self.xfm(b, dc)[:, nb4 * 512:(nb4 + 1) * 512]
                    ev = p.do("v", lambda v, xs=xs, fs2=fs2: v.scalar_tensor_tensor(
                        xs, fs2, 0.5, xs, ALU.mult, ALU.add), inc="vc")
                    dgate[pi] = ("vc", ev)
            self.xfm_ready = ("vc", p.now("vc"))


def build_program(parts):
    nc = bass.Bass("TRN2", target_bir_lowering=False)
    ctx = ExitStack()
    with ctx:
        k = Kern(nc, ctx, parts)
        k.declare_io()
        k.alloc()
        block = ctx.enter_context(nc.Block())
        k.emit_init()
        k.emit_load_x()
        for part in parts:
            if part == "l0ffn":
                k.emit_l0ffn()
            elif part == "l0attn":
                k.emit_l0attn()
            elif part == "l1attn":
                k.emit_l1attn()
            elif part == "l1ffn_a":
                k.emit_l1ffn(first=True)
            elif part == "l1ffn_b":
                k.emit_l1ffn(first=False)
        dbg = os.environ.get("KDBG", "")
        if dbg:
            p = k.p
            srcs = {"A": k.A, "Bar": k.Bar, "PMRN": k.PMRN, "DT": k.DT,
                    "SL": k.SL, "WB": k.WB, "XFM": k.XFM, "ST": k.ST}[dbg.split(":")[0]]
            off = int(dbg.split(":")[1]) if ":" in dbg else 0
            nmax = srcs.shape[1]
            w = min(16384, nmax - off)
            cw = min(w, 1024)
            dst0 = k.y_out[0, :, :].rearrange("(a p) c -> p a c", p=128)[:, 0:max(w // cw, 1), 0:cw]
            src3 = srcs[:, off:off + w].rearrange("p (a c) -> p a c", c=cw)
            p.do("g", lambda g: g.dma_start(dst0, src3),
                 wait=(("vc", p.now("vc")), ("sc", p.now("sc")),
                       ("tc", p.now("tc"))), inc="gld", inc_amt=16)
            p.do("g", lambda g: None, wait=(("gld", p.now("gld")),))
        if not dbg:
            k.emit_store_y()
        k.p.build(block)
    return nc


LAUNCH_PLAN = [
    ["l0ffn", "l0attn", "l0ffn", "l1ffn_a", "l1attn", "l1ffn_b"],
]


def _norm_parts(parts):
    return ["l1ffn" if p.startswith("l1ffn") else p for p in parts]


def prep_inputs(inp):
    f = np.ascontiguousarray
    d = {}
    d["w1"] = f(inp["L0_ffn_w1"])
    d["w2"] = f(inp["L0_ffn_w2"])
    d["se"] = f(inp["L0_ffn_slot_embeds"].reshape(4096, 1024))
    d["gnc"] = f(inp["L0_ffn_norm_gamma"].reshape(8, 128).T)
    d["gsc"] = f(inp["L0_ffn_slot_norm_gamma"].reshape(8, 128).T)
    d["b1c"] = f(inp["L0_ffn_b1"].reshape(8, 32, 128).transpose(0, 2, 1))
    d["b2r"] = f(inp["L0_ffn_b2"].reshape(8, 1, 1024))
    d["u_n"] = f(inp["L0_attn_ternary_factor_u"])
    d["uT"] = f(inp["L0_attn_ternary_factor_u"].T)
    d["v_n"] = f(inp["L0_attn_ternary_factor_v"])
    d["vT"] = f(inp["L0_attn_ternary_factor_v"].T)
    rows, cols = np.tril_indices(64)
    tril = np.zeros((16, 64, 64), dtype=np.float32)
    tril[:, rows, cols] = inp["L1_attn_params"]
    d["trilT"] = f(tril.transpose(0, 2, 1))
    d["WvT"] = f(inp["L1_attn_Wv"].T)
    d["bvr"] = f(inp["L1_attn_bv"].reshape(1, 1024))
    d["WoT"] = f(inp["L1_attn_Wo"].T)
    d["bor"] = f(inp["L1_attn_bo"].reshape(1, 1024))
    g1r = inp["L1_ffn_G1_R"][0].transpose(1, 0, 2).reshape(4, 1024)
    g1i = inp["L1_ffn_G1_I"][0].transpose(1, 0, 2).reshape(4, 1024)
    g2r = inp["L1_ffn_G2_R"][:, :, 0, :].reshape(4, 256)
    g2i = inp["L1_ffn_G2_I"][:, :, 0, :].reshape(4, 256)
    d["G1r"] = f(np.concatenate([g1r, g1i, g2r, g2i], axis=1))
    d["brr"] = f(inp["L1_ffn_bias_real"].reshape(1, 512))
    d["bir"] = f(inp["L1_ffn_bias_imag"].reshape(1, 512))
    return d


def kernel(**inputs):
    x = np.ascontiguousarray(np.asarray(inputs["L0_attn_qz"], dtype=np.float32))
    wd = prep_inputs({k: np.asarray(v, dtype=np.float32) for k, v in inputs.items()})
    shards = [x[2 * i:2 * i + 2] for i in range(8)]
    progs = {}
    for parts in LAUNCH_PLAN:
        key = tuple(parts)
        if key not in progs:
            progs[key] = build_program(parts)
    cur = shards
    for parts in LAUNCH_PLAN:
        nc = progs[tuple(parts)]
        needed = {"l0ffn": ["w1", "w2", "se", "gnc", "gsc", "b1c", "b2r"],
                  "l0attn": ["u_n", "uT", "v_n", "vT"],
                  "l1attn": ["trilT", "WvT", "bvr", "WoT", "bor"],
                  "l1ffn": ["G1r", "brr", "bir"]}
        names = set()
        for pp_ in _norm_parts(parts):
            names.update(needed[pp_])
        in_maps = []
        for ci in range(8):
            m = {"x": cur[ci]}
            for nm in names:
                m[nm] = wd[nm]
            in_maps.append(m)
        res = run_bass_kernel_spmd(nc, in_maps, core_ids=list(range(8)))
        cur = [np.asarray(r["y"]) for r in res.results]
    return np.concatenate(cur, axis=0)
